# version 01
# baseline (speedup 1.0000x reference)
import contextlib
import math

import numpy as np
import concourse.bass as bass
import concourse.mybir as mybir
from concourse.bass_utils import run_bass_kernel_spmd

F32 = mybir.dt.float32
BF16 = mybir.dt.bfloat16
I32 = mybir.dt.int32
AF = mybir.ActivationFunctionType
ALU = mybir.AluOpType
AX = mybir.AxisListType.X

D = 4096
S = 4096
B = 4
NH = 32
NKV = 8
HD = 128
QCOLS = 4096
KCOLS = 1024
VCOLS = 1024
TT = 512
NT_FULL = S // TT
NT_OWN = NT_FULL // 2
NKC = D // 128
SLAB = 512
SLAB6 = 256
G_LISTS = [[0, 3, 4, 7], [1, 2, 5, 6]]
INV_SQRT_HD = 1.0 / math.sqrt(HD)
LN_EPS = 1e-5

_DEBUG_TAPS = []
_STOP_AFTER = None
_FORCE_PARITY = None
_ATTN_LIMIT = None


class Sem:
    def __init__(self, h):
        self.h = h
        self.n = 0

    def inc(self, inst, k=1):
        inst.then_inc(self.h, k)
        self.n += k
        return self.n


def build_program():
    nc = bass.Bass("TRN2", target_bir_lowering=False)
    ALLE = list(mybir.ALL_ENGINES)

    def din(name, shape):
        return nc.dram_tensor(name, shape, F32, kind="ExternalInput")

    xT = din("xT", [D, S])
    xTo = din("xTo", [D, S // 2])
    wqkvT = din("wqkvT", [D, QCOLS + KCOLS + VCOLS])
    bqk_d = din("bqk", [128, 40])
    bqks_d = din("bqks", [128, 40])
    bvrow_d = din("bvrow", [1, VCOLS])
    woT = din("woT", [D, D])
    bo_d = din("bo", [128, 32])
    bwT = din("bwT", [D, D])
    bb_d = din("bb", [128, 32])
    art_d = din("art", [D, 80])
    bfm_d = din("bfm", [1024, D])
    cro_d = din("cro", [128, S // 2])
    sgo_d = din("sgo", [128, S // 2])
    crf_d = din("crf", [128, S])
    sgf_d = din("sgf", [128, S])
    msk_d = din("msk", [128, 32 * TT])

    out_d = nc.dram_tensor("out", [D, S // 2], F32, kind="ExternalOutput")

    def scratch(name, shape, dtype):
        kind = "ExternalOutput" if name in _DEBUG_TAPS else "Internal"
        return nc.dram_tensor(name, shape, dtype, kind=kind)

    xbf = scratch("xbf", [D, S], BF16)
    xbfo = scratch("xbfo", [D, S // 2], BF16)
    qs = scratch("qs", [QCOLS, S // 2], BF16)
    ks = scratch("ks", [KCOLS, S], BF16)
    vs = scratch("vs", [S, VCOLS], BF16)
    aos = scratch("aos", [D, S // 2], BF16)
    x2s = scratch("x2s", [D, S // 2], F32)
    x2bf = scratch("x2bf", [D, S // 2], BF16)

    ctx = contextlib.ExitStack()
    with ctx:
        E = ctx.enter_context

        s_sy = Sem(E(nc.semaphore("s_sy")))
        s_a = Sem(E(nc.semaphore("s_a")))
        s_g = Sem(E(nc.semaphore("s_g")))
        s_pe = Sem(E(nc.semaphore("s_pe")))
        s_ve = Sem(E(nc.semaphore("s_ve")))
        s_ac = Sem(E(nc.semaphore("s_ac")))
        s_gp = Sem(E(nc.semaphore("s_gp")))
        s_bar = Sem(E(nc.semaphore("s_bar")))
        s_st = Sem(E(nc.semaphore("s_st")))
        ALL_SEMS = [s_sy, s_a, s_g, s_pe, s_ve, s_ac, s_gp, s_bar, s_st]

        T, V, A, SY, G = nc.tensor, nc.vector, nc.scalar, nc.sync, nc.gpsimd
        DMA_SEM = {id(SY): s_sy, id(A): s_a, id(G): s_g}
        ENG_SEM = {id(T): s_pe, id(V): s_ve, id(A): s_ac, id(G): s_gp}

        def dmai(q, out_ap, in_ap, **kw):
            sem = DMA_SEM[id(q)]
            inst = q.dma_start(out_ap, in_ap, **kw)
            sem.inc(inst, 16)
            return sem.n

        def fence(eng, inst):
            sem = ENG_SEM[id(eng)]
            sem.inc(inst)
            eng.wait_ge(sem.h, sem.n)

        def barrier():
            for eng in (T, V, A, SY, G):
                dsem = DMA_SEM.get(id(eng))
                if dsem is not None:
                    eng.wait_ge(dsem.h, dsem.n)
                s_bar.inc(eng.nop())
            tgt = s_bar.n
            for eng in (T, V, A, SY, G):
                eng.wait_ge(s_bar.h, tgt)

        pb = [E(nc.psum_tensor(f"pb{i}", [128, 512], F32)) for i in range(8)]

        ones_col_bf = E(nc.sbuf_tensor("ones_col_bf", [128, 1], BF16))
        ones_row_bf = E(nc.sbuf_tensor("ones_row_bf", [1, 128], BF16))
        ones_row_f = E(nc.sbuf_tensor("ones_row_f", [1, 128], F32))
        sel_lo = E(nc.sbuf_tensor("sel_lo", [1, 128], BF16))
        sel_hi = E(nc.sbuf_tensor("sel_hi", [1, 128], BF16))
        bqk_sb = E(nc.sbuf_tensor("bqk_sb", [128, 40], F32))
        bqks_sb = E(nc.sbuf_tensor("bqks_sb", [128, 40], F32))
        bv_sb = E(nc.sbuf_tensor("bv_sb", [1, VCOLS], BF16))
        bo_sb = E(nc.sbuf_tensor("bo_sb", [128, 32], F32))
        bb_sb = E(nc.sbuf_tensor("bb_sb", [128, 32], F32))
        eye_sb = E(nc.sbuf_tensor("eye_sb", [128, 128], F32))
        eps_sb = E(nc.sbuf_tensor("eps_sb", [1, 1], F32))

        with nc.Block() as block0:
            @block0.tensor
            def _(e):
                e.nop()

            @block0.vector
            def _(e):
                e.nop()

            @block0.scalar
            def _(e):
                e.nop()

            @block0.sync
            def _(e):
                e.nop()

            @block0.gpsimd
            def _(e):
                e.nop()

        c0s = contextlib.ExitStack()
        eye_tmp = c0s.enter_context(nc.sbuf_tensor("eye_tmp", [128, 256], F32))
        with nc.body("p0", valid_engines=ALLE):
            G.memset(ones_col_bf[:, :], 1.0)
            G.memset(eps_sb[:, :], LN_EPS)
            G.memset(ones_row_bf[:, :], 1.0)
            G.memset(ones_row_f[:, :], 1.0)
            G.memset(sel_lo[0:1, 0:64], 1.0)
            G.memset(sel_lo[0:1, 64:128], 0.0)
            G.memset(sel_hi[0:1, 0:64], 0.0)
            G.memset(sel_hi[0:1, 64:128], 1.0)
            idn_i = c0s.enter_context(nc.sbuf_tensor("idn_i", [128, 128], I32))
            inst = G.iota(idn_i[:, :], pattern=[[1, 128]], base=0,
                          channel_multiplier=-1)
            s_gp.inc(inst)
            V.wait_ge(s_gp.h, s_gp.n)
            inst = V.tensor_copy(eye_tmp[:, 0:128], idn_i[:, :])
            fence(V, inst)
            inst = V.tensor_scalar(eye_tmp[:, 128:256], eye_tmp[:, 0:128],
                                   0.5, None, ALU.is_lt)
            fence(V, inst)
            inst = V.tensor_scalar(eye_tmp[:, 0:128], eye_tmp[:, 0:128],
                                   -0.5, None, ALU.is_ge)
            fence(V, inst)
            inst = V.tensor_tensor(eye_sb[:, :], eye_tmp[:, 0:128],
                                   eye_tmp[:, 128:256], ALU.mult)
            s_ve.inc(inst)
            dmai(A, bqk_sb[:, :], bqk_d[:, :])
            dmai(A, bqks_sb[:, :], bqks_d[:, :])
            dmai(G, bv_sb[:, :], bvrow_d[:, :])
            dmai(A, bo_sb[:, :], bo_d[:, :])
            dmai(A, bb_sb[:, :], bb_d[:, :])
            for i in range(8):
                dmai(G, xbf[:, i * TT:(i + 1) * TT], xT[:, i * TT:(i + 1) * TT])
            for i in range(4):
                dmai(G, xbfo[:, i * TT:(i + 1) * TT],
                     xTo[:, i * TT:(i + 1) * TT])
            xcast_done = s_g.n
        c0s.close()

        with contextlib.ExitStack() as c12:
            E12 = c12.enter_context
            negm_bf = E12(nc.sbuf_tensor("negm_bf", [1, S], BF16))
            negmo_bf = E12(nc.sbuf_tensor("negmo_bf", [1, S // 2], BF16))
            rstd_bc = E12(nc.sbuf_tensor("rstd_bc", [128, S], F32))
            rstdo_bc = E12(nc.sbuf_tensor("rstdo_bc", [128, S // 2], F32))
            rstd_col = E12(nc.sbuf_tensor("rstd_col", [128, NKC], F32))
            colsum_bf = E12(nc.sbuf_tensor("colsum_bf", [1, SLAB], BF16))
            colsum_v_bf = E12(nc.sbuf_tensor("colsum_v_bf", [1, SLAB], BF16))
            crs = E12(nc.sbuf_tensor("crs", [128, S // 2], F32))
            sgs = E12(nc.sbuf_tensor("sgs", [128, S // 2], F32))

            with nc.body("p1", valid_engines=ALLE):
                with contextlib.ExitStack() as c1:
                    E1 = c1.enter_context
                    xtile = [E1(nc.sbuf_tensor(f"st_x{i}", [128, NKC * TT],
                                               BF16)) for i in range(2)]
                    xsq = E1(nc.sbuf_tensor("st_xsq", [128, NKC * TT], BF16))
                    rowt = E1(nc.sbuf_tensor("rowt", [1, 4 * TT], F32))

                    prev = {"rows_read": None, "xfree": {},
                            "sq_read": None, "bc_read": None, "n": 0}

                    def stats_pass(src, n_tiles, negm_row, rstd_bcast,
                                   do_cols):
                        for t in range(n_tiles):
                            bi = prev["n"] % 2
                            prev["n"] += 1
                            SY.wait_ge(s_g.h, xcast_done)
                            xf = prev["xfree"].get(bi)
                            if xf is not None:
                                SY.wait_ge(xf[0].h, xf[1])
                            ld = dmai(SY, xtile[bi][:, :],
                                      src[:, t * TT:(t + 1) * TT]
                                      .rearrange("(c p) t -> p c t", p=128))
                            V.wait_ge(s_sy.h, ld)
                            if prev["sq_read"] is not None:
                                V.wait_ge(prev["sq_read"][0].h,
                                          prev["sq_read"][1])
                            inst = V.tensor_tensor(xsq[:, :], xtile[bi][:, :],
                                                   xtile[bi][:, :], ALU.mult)
                            sq_done = s_ve.inc(inst)
                            T.wait_ge(s_sy.h, ld)
                            if prev["rows_read"] is not None:
                                T.wait_ge(prev["rows_read"][0].h,
                                          prev["rows_read"][1])
                            for k in range(NKC):
                                inst = T.matmul(
                                    pb[4][0:1, :], ones_col_bf[:, :],
                                    xtile[bi][:, k * TT:(k + 1) * TT],
                                    start=(k == 0), stop=(k == NKC - 1))
                            T.wait_ge(s_ve.h, sq_done)
                            for k in range(NKC):
                                inst = T.matmul(
                                    pb[5][0:1, :], ones_col_bf[:, :],
                                    xsq[:, k * TT:(k + 1) * TT],
                                    start=(k == 0), stop=(k == NKC - 1))
                            sums_done = s_pe.inc(inst)
                            prev["sq_read"] = (s_pe, sums_done)
                            prev["xfree"][bi] = (s_pe, sums_done)
                            V.wait_ge(s_pe.h, sums_done)
                            inst = V.tensor_scalar(
                                negm_row[0:1, t * TT:(t + 1) * TT],
                                pb[4][0:1, :], -1.0 / D, None, ALU.mult)
                            fence(V, inst)
                            inst = V.tensor_scalar(rowt[0:1, 0:TT],
                                                   pb[5][0:1, :], 1.0 / D,
                                                   None, ALU.mult)
                            fence(V, inst)
                            prev["rows_read"] = (s_ve, s_ve.n)
                            inst = V.tensor_tensor(
                                rowt[0:1, TT:2 * TT],
                                negm_row[0:1, t * TT:(t + 1) * TT],
                                negm_row[0:1, t * TT:(t + 1) * TT], ALU.mult)
                            fence(V, inst)
                            inst = V.tensor_tensor(rowt[0:1, 2 * TT:3 * TT],
                                                   rowt[0:1, 0:TT],
                                                   rowt[0:1, TT:2 * TT],
                                                   ALU.subtract)
                            var_done = s_ve.inc(inst)
                            A.wait_ge(s_ve.h, var_done)
                            inst = A.activation(rowt[0:1, 3 * TT:4 * TT],
                                                rowt[0:1, 2 * TT:3 * TT],
                                                AF.Ln, bias=eps_sb[0:1, 0:1],
                                                scale=1.0)
                            fence(A, inst)
                            V.wait_ge(s_ac.h, s_ac.n)
                            inst = V.nop()
                            fence(V, inst)
                            inst = A.activation(rowt[0:1, 0:TT],
                                                rowt[0:1, 3 * TT:4 * TT],
                                                AF.Exp, scale=-0.5)
                            rstd_done = s_ac.inc(inst)
                            T.wait_ge(s_ac.h, rstd_done)
                            if prev["bc_read"] is not None:
                                T.wait_ge(prev["bc_read"][0].h,
                                          prev["bc_read"][1])
                            inst = T.matmul(pb[6][:, :], ones_row_f[0:1, :],
                                            rowt[0:1, 0:TT], start=True,
                                            stop=True)
                            bc_done = s_pe.inc(inst)
                            A.wait_ge(s_pe.h, bc_done)
                            inst = A.copy(rstd_bcast[:, t * TT:(t + 1) * TT],
                                          pb[6][:, :])
                            bc_ev = s_ac.inc(inst)
                            prev["bc_read"] = (s_ac, bc_ev)
                            if do_cols:
                                A.wait_ge(s_ac.h, bc_ev)
                                for c in range(4):
                                    cc = 4 * t + c
                                    dmai(A, rstd_col[:, cc:cc + 1],
                                         rowt[0:1, c * 128:(c + 1) * 128])
                            V.wait_ge(s_a.h, s_a.n)
                            V.wait_ge(s_ac.h, bc_ev)

                    stats_pass(xbf, NT_FULL, negm_bf, rstd_bc, True)
                    stats_pass(xbfo, NT_OWN, negmo_bf, rstdo_bc, False)
                    barrier()

            with nc.body("p2", valid_engines=ALLE):
                with contextlib.ExitStack() as c2:
                    E2 = c2.enter_context
                    wbuf = [E2(nc.sbuf_tensor(f"qk_w{i}", [128, NKC * SLAB],
                                              BF16)) for i in range(2)]
                    xtile = [E2(nc.sbuf_tensor(f"qk_x{i}", [128, NKC * TT],
                                               BF16)) for i in range(2)]
                    t1b = [E2(nc.sbuf_tensor(f"qk_t1{i}", [128, TT], F32))
                           for i in range(2)]
                    t1sb = [E2(nc.sbuf_tensor(f"qk_t1s{i}", [128, TT], F32))
                            for i in range(2)]
                    r1b = E2(nc.sbuf_tensor("qk_r1", [128, TT], F32))
                    outb = [E2(nc.sbuf_tensor(f"qk_o{i}", [128, TT], BF16))
                            for i in range(2)]

                    st2 = {"wfree": {}, "xfree": {}, "t1free": {},
                           "t1sfree": {}, "outfree": {}, "bankfree": {},
                           "cs_read": None, "rope": None, "rope_ld": 0,
                           "slab": 0, "tile": 0, "mm": 0, "ep": 0}

                    def rope_ensure(key):
                        if st2["rope"] == key:
                            return st2["rope_ld"]
                        inst = V.nop()
                        cons = s_ve.inc(inst)
                        A.wait_ge(s_ve.h, cons)
                        if key[0] == "own":
                            dmai(A, crs[:, :], cro_d[:, :])
                            ld = dmai(A, sgs[:, :], sgo_d[:, :])
                        else:
                            b0 = key[1]
                            dmai(A, crs[:, :], crf_d[:, b0:b0 + S // 2])
                            ld = dmai(A, sgs[:, :], sgf_d[:, b0:b0 + S // 2])
                        st2["rope"] = key
                        st2["rope_ld"] = ld
                        return ld

                    def load_slab(c0):
                        si = st2["slab"]
                        st2["slab"] += 1
                        bi = si % 2
                        wf = st2["wfree"].get(bi)
                        if wf is not None:
                            G.wait_ge(wf[0].h, wf[1])
                        ld = dmai(G, wbuf[bi][:, :]
                                  .rearrange("p (c n) -> p c n", c=NKC),
                                  wqkvT[:, c0:c0 + SLAB]
                                  .rearrange("(c p) n -> p c n", p=128))
                        return ld, bi

                    def colsums(wi, w_ld, cs_bf):
                        T.wait_ge(s_g.h, w_ld)
                        if st2["cs_read"] is not None:
                            T.wait_ge(st2["cs_read"][0].h, st2["cs_read"][1])
                        for k in range(NKC):
                            inst = T.matmul(
                                pb[4][0:1, :], ones_col_bf[:, :],
                                wbuf[wi][:, k * SLAB:(k + 1) * SLAB],
                                start=(k == 0), stop=(k == NKC - 1))
                        done = s_pe.inc(inst)
                        A.wait_ge(s_pe.h, done)
                        inst = A.copy(cs_bf[0:1, :], pb[4][0:1, :])
                        v = s_ac.inc(inst)
                        st2["cs_read"] = (s_ac, v)
                        return v

                    def load_xtile(src, t):
                        n = st2["tile"]
                        st2["tile"] += 1
                        bi = n % 2
                        xf = st2["xfree"].get(bi)
                        if xf is not None:
                            SY.wait_ge(xf[0].h, xf[1])
                        ld = dmai(SY, xtile[bi][:, :],
                                  src[:, t * TT:(t + 1) * TT]
                                  .rearrange("(c p) t -> p c t", p=128))
                        return ld, bi

                    def qk_slabs(col_base, n_slabs, src, n_tiles, negm_row,
                                 rstd_bcast, out_scr, bias0, rope_of_t):
                        for si in range(n_slabs):
                            c0 = col_base + si * SLAB
                            w_ld, wi = load_slab(c0)
                            cs_done = colsums(wi, w_ld, colsum_bf)
                            for t in range(n_tiles):
                                rope_ld = rope_ensure(rope_of_t(t))
                                x_ld, xi = load_xtile(src, t)
                                for m in range(4):
                                    bank = st2["mm"] % 4
                                    st2["mm"] += 1
                                    T.wait_ge(s_sy.h, x_ld)
                                    T.wait_ge(s_g.h, w_ld)
                                    T.wait_ge(s_ac.h, cs_done)
                                    bf_ = st2["bankfree"].get(bank)
                                    if bf_ is not None:
                                        T.wait_ge(bf_[0].h, bf_[1])
                                    for k in range(NKC):
                                        T.matmul(
                                            pb[bank][:, :],
                                            wbuf[wi][:, k * SLAB + m * 128:
                                                     k * SLAB + (m + 1) * 128],
                                            xtile[xi][:, k * TT:(k + 1) * TT],
                                            start=(k == 0), stop=False)
                                    inst = T.matmul(
                                        pb[bank][:, :],
                                        colsum_bf[0:1, m * 128:(m + 1) * 128],
                                        negm_row[0:1, t * TT:(t + 1) * TT],
                                        start=False, stop=True)
                                    mm_done = s_pe.inc(inst)
                                    ti = st2["ep"] % 2
                                    st2["ep"] += 1
                                    V.wait_ge(s_pe.h, mm_done)
                                    for key in ("t1free", "outfree",
                                                "t1sfree"):
                                        f = st2[key].get(ti)
                                        if f is not None:
                                            V.wait_ge(f[0].h, f[1])
                                    inst = V.tensor_tensor(
                                        t1b[ti][:, :], pb[bank][:, :],
                                        rstd_bcast[:, t * TT:(t + 1) * TT],
                                        ALU.mult)
                                    t1_done = s_ve.inc(inst)
                                    st2["bankfree"][bank] = (s_ve, t1_done)
                                    A.wait_ge(s_ve.h, t1_done)
                                    dmai(A, t1sb[ti][0:64, :],
                                         t1b[ti][64:128, :])
                                    sw = dmai(A, t1sb[ti][64:128, :],
                                              t1b[ti][0:64, :])
                                    cchunk = bias0 + 4 * si + m
                                    gc = (t * TT) % (S // 2)
                                    V.wait_ge(s_a.h, sw)
                                    V.wait_ge(s_a.h, rope_ld)
                                    inst = V.scalar_tensor_tensor(
                                        r1b[:, :], t1b[ti][:, :],
                                        bqk_sb[:, cchunk:cchunk + 1],
                                        crs[:, gc:gc + TT],
                                        ALU.add, ALU.mult)
                                    fence(V, inst)
                                    st2["t1free"][ti] = (s_ve, s_ve.n)
                                    inst = V.scalar_tensor_tensor(
                                        t1sb[ti][:, :], t1sb[ti][:, :],
                                        bqks_sb[:, cchunk:cchunk + 1],
                                        sgs[:, gc:gc + TT],
                                        ALU.add, ALU.mult)
                                    fence(V, inst)
                                    inst = V.tensor_tensor(outb[ti][:, :],
                                                           r1b[:, :],
                                                           t1sb[ti][:, :],
                                                           ALU.add)
                                    rp_done = s_ve.inc(inst)
                                    st2["t1sfree"][ti] = (s_ve, rp_done)
                                    G.wait_ge(s_ve.h, rp_done)
                                    r0 = si * SLAB + m * 128
                                    stv = dmai(G, out_scr[r0:r0 + 128,
                                                          t * TT:(t + 1) * TT],
                                               outb[ti][:, :])
                                    st2["outfree"][ti] = (s_g, stv)
                                st2["xfree"][xi] = (s_pe, s_pe.n)
                            st2["wfree"][wi] = (s_pe, s_pe.n)

                    qk_slabs(0, QCOLS // SLAB, xbfo, NT_OWN, negmo_bf,
                             rstdo_bc, qs, 0, lambda t: ("own", 0))
                    qk_slabs(QCOLS, KCOLS // SLAB, xbf, NT_FULL, negm_bf,
                             rstd_bc, ks, 32,
                             lambda t: ("full", (t // 4) * (S // 2)))

                    for si in range(VCOLS // SLAB):
                        c0 = QCOLS + KCOLS + si * SLAB
                        w_ld, wi = load_slab(c0)
                        cs_done = colsums(wi, w_ld, colsum_v_bf)
                        for t in range(NT_FULL):
                            x_ld, xi = load_xtile(xbf, t)
                            for m in range(4):
                                bank = st2["mm"] % 4
                                st2["mm"] += 1
                                T.wait_ge(s_sy.h, x_ld)
                                T.wait_ge(s_g.h, w_ld)
                                T.wait_ge(s_ac.h, cs_done)
                                bf_ = st2["bankfree"].get(bank)
                                if bf_ is not None:
                                    T.wait_ge(bf_[0].h, bf_[1])
                                for k in range(NKC):
                                    T.matmul(
                                        pb[bank][:, :],
                                        xtile[xi][:, k * TT + m * 128:
                                                  k * TT + (m + 1) * 128],
                                        wbuf[wi][:, k * SLAB:(k + 1) * SLAB],
                                        start=(k == 0), stop=False)
                                tok0 = t * TT + m * 128
                                T.matmul(pb[bank][:, :],
                                         negm_bf[0:1, tok0:tok0 + 128],
                                         colsum_v_bf[0:1, :],
                                         start=False, stop=False)
                                inst = T.matmul(
                                    pb[bank][:, :], ones_row_bf[0:1, :],
                                    bv_sb[0:1, si * SLAB:(si + 1) * SLAB],
                                    start=False, stop=True)
                                mm_done = s_pe.inc(inst)
                                ti = st2["ep"] % 2
                                st2["ep"] += 1
                                V.wait_ge(s_pe.h, mm_done)
                                of = st2["outfree"].get(ti)
                                if of is not None:
                                    V.wait_ge(of[0].h, of[1])
                                cc = 4 * t + m
                                inst = V.tensor_scalar(outb[ti][:, :],
                                                       pb[bank][:, :],
                                                       rstd_col[:, cc:cc + 1],
                                                       None, ALU.mult)
                                ep_done = s_ve.inc(inst)
                                st2["bankfree"][bank] = (s_ve, ep_done)
                                G.wait_ge(s_ve.h, ep_done)
                                stv = dmai(G, vs[tok0:tok0 + 128,
                                                 si * SLAB:(si + 1) * SLAB],
                                           outb[ti][:, :])
                                st2["outfree"][ti] = (s_g, stv)
                            st2["xfree"][xi] = (s_pe, s_pe.n)
                        st2["wfree"][wi] = (s_pe, s_pe.n)
                    barrier()

        if _STOP_AFTER == "qkv":
            _finalize(nc, ALLE, SY, s_st)
            return nc

        with contextlib.ExitStack() as c3:
            E3 = c3.enter_context
            k_sb = E3(nc.sbuf_tensor("at_k", [128, S], BF16))
            v_sb = E3(nc.sbuf_tensor("at_v", [128, 32 * 128], BF16))
            q_sb = [E3(nc.sbuf_tensor(f"at_q{i}", [128, S // 2], BF16))
                    for i in range(2)]
            sq_sb = E3(nc.sbuf_tensor("at_sq", [128, S], BF16))
            probs = [E3(nc.sbuf_tensor(f"at_p{i}", [128, TT], BF16))
                     for i in range(2)]
            probm = [E3(nc.sbuf_tensor(f"at_pm{i}", [128, TT], BF16))
                     for i in range(2)]
            mask_bf = E3(nc.sbuf_tensor("at_mb", [128, 32 * TT], BF16))
            rowb = E3(nc.sbuf_tensor("at_rowb", [1, TT], F32))
            scrow = E3(nc.sbuf_tensor("at_sc", [1, 32], F32))
            bias_sb = [E3(nc.sbuf_tensor(f"at_b{i}", [128, 1], F32))
                       for i in range(2)]
            rden = E3(nc.sbuf_tensor("at_rd", [1, TT], F32))
            rden2 = E3(nc.sbuf_tensor("at_rd2", [1, TT], F32))
            rden_bf = E3(nc.sbuf_tensor("at_rdb", [1, TT], BF16))
            rden_bc = E3(nc.sbuf_tensor("at_rdbc", [128, TT], BF16))
            ao_sb = [E3(nc.sbuf_tensor(f"at_ao{i}", [128, TT], BF16))
                     for i in range(2)]

            with nc.body("p3pre", valid_engines=ALLE):
                for j in range(4):
                    dmai(G, mask_bf[:, j * 8 * TT:(j + 1) * 8 * TT],
                         msk_d[:, j * 8 * TT:(j + 1) * 8 * TT])
                barrier()

            def attn_body():
                lim = _ATTN_LIMIT or (NKV, 4, NT_OWN)
                st3 = {"pvfree": None, "aofree": {}, "qfree": {},
                       "sbank": {}, "probfree": {}}
                for kvh in range(lim[0]):
                    SY.wait_ge(s_pe.h, s_pe.n)
                    k_ld = dmai(SY, k_sb[:, :],
                                ks[kvh * 128:(kvh + 1) * 128, :])
                    v_ld = dmai(SY, v_sb[:, :],
                                vs[:, kvh * 128:(kvh + 1) * 128]
                                .rearrange("(c p) h -> p c h", p=128))
                    V.wait_ge(s_sy.h, k_ld)
                    V.wait_ge(s_pe.h, s_pe.n)
                    inst = V.tensor_tensor(sq_sb[:, :], k_sb[:, :],
                                           k_sb[:, :], ALU.mult)
                    k2_done = s_ve.inc(inst)
                    for sub in range(NT_FULL):
                        T.wait_ge(s_ve.h, k2_done)
                        if sub > 0:
                            T.wait_ge(s_ac.h, s_ac.n)
                        inst = T.matmul(pb[5][0:1, :], ones_col_bf[:, :],
                                        sq_sb[:, sub * TT:(sub + 1) * TT],
                                        start=True, stop=True)
                        done = s_pe.inc(inst)
                        A.wait_ge(s_pe.h, done)
                        inst = A.copy(rowb[0:1, :], pb[5][0:1, :])
                        fence(A, inst)
                        V.wait_ge(s_ac.h, s_ac.n)
                        inst = V.reduce_max(scrow[0:1, 8 + sub:9 + sub],
                                            rowb[0:1, :], axis=AX)
                        fence(V, inst)
                    inst = V.reduce_max(scrow[0:1, 0:1],
                                        scrow[0:1, 8:8 + NT_FULL], axis=AX)
                    fence(V, inst)

                    for qi in range(lim[1]):
                        qh = kvh * 4 + qi
                        qbi = qh % 2
                        qf = st3["qfree"].get(qbi)
                        if qf is not None:
                            SY.wait_ge(qf[0].h, qf[1])
                        q_ld = dmai(SY, q_sb[qbi][:, :],
                                    qs[qh * 128:(qh + 1) * 128, :])
                        V.wait_ge(s_sy.h, q_ld)
                        V.wait_ge(s_pe.h, s_pe.n)
                        inst = V.tensor_tensor(sq_sb[:, 0:S // 2],
                                               q_sb[qbi][:, :],
                                               q_sb[qbi][:, :], ALU.mult)
                        q2_done = s_ve.inc(inst)
                        for sub in range(NT_OWN):
                            T.wait_ge(s_ve.h, q2_done)
                            T.wait_ge(s_ac.h, s_ac.n)
                            inst = T.matmul(pb[5][0:1, :], ones_col_bf[:, :],
                                            sq_sb[:, sub * TT:(sub + 1) * TT],
                                            start=True, stop=True)
                            done = s_pe.inc(inst)
                            A.wait_ge(s_pe.h, done)
                            inst = A.copy(rowb[0:1, :], pb[5][0:1, :])
                            fence(A, inst)
                            V.wait_ge(s_ac.h, s_ac.n)
                            inst = V.reduce_max(scrow[0:1, 8 + sub:9 + sub],
                                                rowb[0:1, :], axis=AX)
                            fence(V, inst)
                        inst = V.reduce_max(scrow[0:1, 1:2],
                                            scrow[0:1, 8:8 + NT_OWN], axis=AX)
                        fence(V, inst)
                        inst = V.tensor_tensor(scrow[0:1, 2:3],
                                               scrow[0:1, 0:1],
                                               scrow[0:1, 1:2], ALU.mult)
                        prod_done = s_ve.inc(inst)
                        A.wait_ge(s_ve.h, prod_done)
                        inst = A.activation(scrow[0:1, 3:4], scrow[0:1, 2:3],
                                            AF.Ln)
                        fence(A, inst)
                        inst = A.activation(scrow[0:1, 4:5], scrow[0:1, 3:4],
                                            AF.Exp, scale=0.5)
                        m_done = s_ac.inc(inst)
                        V.wait_ge(s_ac.h, m_done)
                        inst = V.tensor_scalar(scrow[0:1, 5:6],
                                               scrow[0:1, 4:5], 2.0,
                                               -INV_SQRT_HD, ALU.add, ALU.mult)
                        bias_done = s_ve.inc(inst)
                        T.wait_ge(s_ve.h, bias_done)
                        T.wait_ge(s_ac.h, s_ac.n)
                        inst = T.matmul(pb[6][:, 0:1], ones_row_f[0:1, :],
                                        scrow[0:1, 5:6], start=True, stop=True)
                        bb_done = s_pe.inc(inst)
                        A.wait_ge(s_pe.h, bb_done)
                        inst = A.copy(bias_sb[qbi][:, :], pb[6][:, 0:1])
                        biassb_done = s_ac.inc(inst)

                        for t in range(lim[2]):
                            nk = 4 * (2 * t + 2)

                            def scores(c):
                                sb = c % 2
                                bf_ = st3["sbank"].get(sb)
                                if bf_ is not None:
                                    T.wait_ge(bf_[0].h, bf_[1])
                                T.wait_ge(s_sy.h, max(k_ld, q_ld))
                                inst = T.matmul(
                                    pb[sb][:, :],
                                    k_sb[:, c * 128:(c + 1) * 128],
                                    q_sb[qbi][:, t * TT:(t + 1) * TT],
                                    start=True, stop=True)
                                return s_pe.inc(inst)

                            sc_done = {0: scores(0)}
                            pv_done = None
                            for c in range(nk):
                                if c + 1 < nk:
                                    sc_done[c + 1] = scores(c + 1)
                                sb = c % 2
                                A.wait_ge(s_pe.h, sc_done[c])
                                A.wait_ge(s_ac.h, biassb_done)
                                pf = st3["probfree"].get(sb)
                                if pf is not None:
                                    A.wait_ge(pf[0].h, pf[1])
                                inst = A.activation(probs[sb][:, :],
                                                    pb[sb][:, :], AF.Exp,
                                                    bias=bias_sb[qbi][:, 0:1],
                                                    scale=INV_SQRT_HD)
                                exp_done = s_ac.inc(inst)
                                st3["sbank"][sb] = (s_ac, exp_done)
                                d_ = c - (nk - 8)
                                if d_ >= 0:
                                    mcol = (t * 8 + d_) * TT
                                    V.wait_ge(s_ac.h, exp_done)
                                    if pf is not None:
                                        V.wait_ge(pf[0].h, pf[1])
                                    inst = V.tensor_tensor(
                                        probm[sb][:, :], probs[sb][:, :],
                                        mask_bf[:, mcol:mcol + TT],
                                        ALU.mult)
                                    use = (s_ve, s_ve.inc(inst))
                                    pmat = probm[sb]
                                else:
                                    use = (s_ac, exp_done)
                                    pmat = probs[sb]
                                T.wait_ge(use[0].h, use[1])
                                if c == 0 and st3["pvfree"] is not None:
                                    T.wait_ge(st3["pvfree"][0].h,
                                              st3["pvfree"][1])
                                T.wait_ge(s_sy.h, v_ld)
                                T.matmul(pb[2][:, :],
                                         v_sb[:, c * 128:(c + 1) * 128],
                                         pmat[:, :], start=(c == 0),
                                         stop=(c == nk - 1),
                                         skip_group_check=True)
                                inst = T.matmul(pb[4][0:1, :],
                                                ones_col_bf[:, :], pmat[:, :],
                                                start=(c == 0),
                                                stop=(c == nk - 1),
                                                skip_group_check=True)
                                pv_done = s_pe.inc(inst)
                                st3["probfree"][sb] = (s_pe, pv_done)
                            A.wait_ge(s_pe.h, pv_done)
                            inst = A.copy(rden[0:1, :], pb[4][0:1, :])
                            fence(A, inst)
                            den_ev = s_ac.n
                            V.wait_ge(s_ac.h, den_ev)
                            inst = V.reciprocal(rden2[0:1, :], rden[0:1, :])
                            fence(V, inst)
                            inst = V.tensor_copy(rden_bf[0:1, :],
                                                 rden2[0:1, :])
                            rdb_done = s_ve.inc(inst)
                            T.wait_ge(s_ve.h, rdb_done)
                            inst = T.matmul(pb[3][:, :], ones_row_bf[0:1, :],
                                            rden_bf[0:1, :], start=True,
                                            stop=True)
                            bc_done = s_pe.inc(inst)
                            A.wait_ge(s_pe.h, bc_done)
                            inst = A.copy(rden_bc[:, :], pb[3][:, :])
                            rbc_done = s_ac.inc(inst)
                            ai = (qh * NT_OWN + t) % 2
                            af = st3["aofree"].get(ai)
                            V.wait_ge(s_ac.h, rbc_done)
                            if af is not None:
                                V.wait_ge(af[0].h, af[1])
                            inst = V.tensor_tensor(ao_sb[ai][:, :],
                                                   pb[2][:, :],
                                                   rden_bc[:, :], ALU.mult)
                            ao_done = s_ve.inc(inst)
                            st3["pvfree"] = (s_ve, ao_done)
                            G.wait_ge(s_ve.h, ao_done)
                            stv = dmai(G, aos[qh * 128:(qh + 1) * 128,
                                              t * TT:(t + 1) * TT],
                                       ao_sb[ai][:, :])
                            st3["aofree"][ai] = (s_g, stv)
                        st3["qfree"][qbi] = (s_pe, s_pe.n)

            with nc.body("p3main", valid_engines=ALLE):
                attn_body()
                barrier()

        if _STOP_AFTER == "attn":
            _finalize(nc, ALLE, SY, s_st)
            return nc

        with nc.body("p4", valid_engines=ALLE):
            with contextlib.ExitStack() as c4:
                E4 = c4.enter_context
                wbuf = [E4(nc.sbuf_tensor(f"op_w{i}", [128, NKC * SLAB],
                                          BF16)) for i in range(2)]
                atile = [E4(nc.sbuf_tensor(f"op_a{i}", [128, NKC * TT],
                                           BF16)) for i in range(2)]
                xres = [E4(nc.sbuf_tensor(f"op_x{i}", [128, 4 * TT], F32))
                        for i in range(2)]
                t1 = E4(nc.sbuf_tensor("op_t1", [128, TT], F32))
                x2o = [E4(nc.sbuf_tensor(f"op_o{i}", [128, TT], F32))
                       for i in range(2)]
                st4 = {"wfree": {}, "afree": {}, "xfree": {}, "bankfree": {},
                       "ofree": {}}
                for si in range(D // SLAB):
                    wi = si % 2
                    wf = st4["wfree"].get(wi)
                    if wf is not None:
                        G.wait_ge(wf[0].h, wf[1])
                    w_ld = dmai(G, wbuf[wi][:, :]
                                .rearrange("p (c n) -> p c n", c=NKC),
                                woT[:, si * SLAB:(si + 1) * SLAB]
                                .rearrange("(c p) n -> p c n", p=128))
                    for t in range(NT_OWN):
                        it = si * NT_OWN + t
                        ai = it % 2
                        af = st4["afree"].get(ai)
                        if af is not None:
                            SY.wait_ge(af[0].h, af[1])
                        a_ld = dmai(SY, atile[ai][:, :],
                                    aos[:, t * TT:(t + 1) * TT]
                                    .rearrange("(c p) t -> p c t", p=128))
                        xf = st4["xfree"].get(ai)
                        if xf is not None:
                            A.wait_ge(xf[0].h, xf[1])
                        x_ld = dmai(A, xres[ai][:, :]
                                    .rearrange("p (c t) -> p c t", c=4),
                                    xTo[si * SLAB:(si + 1) * SLAB,
                                        t * TT:(t + 1) * TT]
                                    .rearrange("(c p) t -> p c t", p=128))
                        for m in range(4):
                            bank = (it * 4 + m) % 4
                            bf_ = st4["bankfree"].get(bank)
                            T.wait_ge(s_g.h, w_ld)
                            T.wait_ge(s_sy.h, a_ld)
                            if bf_ is not None:
                                T.wait_ge(bf_[0].h, bf_[1])
                            for k in range(NKC):
                                inst = T.matmul(
                                    pb[bank][:, :],
                                    wbuf[wi][:, k * SLAB + m * 128:
                                             k * SLAB + (m + 1) * 128],
                                    atile[ai][:, k * TT:(k + 1) * TT],
                                    start=(k == 0), stop=(k == NKC - 1))
                            mm_done = s_pe.inc(inst)
                            V.wait_ge(s_pe.h, mm_done)
                            V.wait_ge(s_a.h, x_ld)
                            oc = si * 4 + m
                            inst = V.tensor_scalar(t1[:, :], pb[bank][:, :],
                                                   bo_sb[:, oc:oc + 1], None,
                                                   ALU.add)
                            fence(V, inst)
                            st4["bankfree"][bank] = (s_ve, s_ve.n)
                            oi = (it * 4 + m) % 2
                            of = st4["ofree"].get(oi)
                            if of is not None:
                                V.wait_ge(of[0].h, of[1])
                            inst = V.tensor_tensor(
                                x2o[oi][:, :], t1[:, :],
                                xres[ai][:, m * TT:(m + 1) * TT], ALU.add)
                            x2_done = s_ve.inc(inst)
                            G.wait_ge(s_ve.h, x2_done)
                            r0 = si * SLAB + m * 128
                            dmai(G, x2s[r0:r0 + 128, t * TT:(t + 1) * TT],
                                 x2o[oi][:, :])
                            stv = dmai(G, x2bf[r0:r0 + 128,
                                               t * TT:(t + 1) * TT],
                                       x2o[oi][:, :])
                            st4["ofree"][oi] = (s_g, stv)
                        st4["afree"][ai] = (s_pe, s_pe.n)
                        st4["xfree"][ai] = (s_ve, s_ve.n)
                    st4["wfree"][wi] = (s_pe, s_pe.n)
                barrier()

        if _STOP_AFTER == "oproj":
            _finalize(nc, ALLE, SY, s_st)
            return nc

        with contextlib.ExitStack() as c56:
            E56 = c56.enter_context
            rhe = [E56(nc.sbuf_tensor(f"f_rhe{i}", [128, 8 * TT], BF16))
                   for i in range(NT_OWN)]
            negm2_bf = E56(nc.sbuf_tensor("f_negm2", [1, S // 2], BF16))
            rstd2_bc = E56(nc.sbuf_tensor("f_rstd2bc", [128, S // 2], F32))

            with nc.body("p5", valid_engines=ALLE):
                with contextlib.ExitStack() as c5:
                    E5 = c5.enter_context
                    xtile = [E5(nc.sbuf_tensor(f"f5_x{i}", [128, NKC * TT],
                                               BF16)) for i in range(2)]
                    xsq5 = E5(nc.sbuf_tensor("f5_xsq", [128, NKC * TT], BF16))
                    rowt5 = E5(nc.sbuf_tensor("f5_rowt", [1, 4 * TT], F32))
                    rstd2_col = E5(nc.sbuf_tensor("f5_rcol", [128, 16], F32))
                    art_sb = E5(nc.sbuf_tensor("f5_art", [128, NKC * 80],
                                               BF16))
                    csar_bf = E5(nc.sbuf_tensor("f5_csar", [1, 80], BF16))
                    rh2 = [E5(nc.sbuf_tensor(f"f5_rh2{i}", [128, TT], F32))
                           for i in range(2)]
                    lgs = E5(nc.sbuf_tensor("f5_lgs", [16, TT], F32))
                    lt_tm = E5(nc.sbuf_tensor("f5_lt", [128, 64], F32))
                    ptm = E5(nc.sbuf_tensor("f5_ptm", [128, 64], F32))
                    rcol = E5(nc.sbuf_tensor("f5_rc", [128, 12], F32))
                    rtmp = E5(nc.sbuf_tensor("f5_rt", [128, 64], F32))
                    comb_tm = E5(nc.sbuf_tensor("f5_comb", [128, 64], F32))
                    comb_sb = E5(nc.sbuf_tensor("f5_csb", [16, TT], BF16))
                    comb_row = E5(nc.sbuf_tensor("f5_crow", [1, 16 * TT],
                                                 BF16))

                    art_ld = dmai(G, art_sb[:, :]
                                  .rearrange("p (c n) -> p c n", c=NKC),
                                  art_d[:, :]
                                  .rearrange("(c p) n -> p c n", p=128))
                    T.wait_ge(s_g.h, art_ld)
                    for k in range(NKC):
                        inst = T.matmul(pb[4][0:1, 0:80], ones_col_bf[:, :],
                                        art_sb[:, k * 80:(k + 1) * 80],
                                        start=(k == 0), stop=(k == NKC - 1))
                    csd = s_pe.inc(inst)
                    A.wait_ge(s_pe.h, csd)
                    inst = A.copy(csar_bf[0:1, :], pb[4][0:1, 0:80])
                    csar_done = s_ac.inc(inst)

                    prev5 = {"rows_read": (s_ac, csar_done), "xfree": {},
                             "sq_read": None, "bc_read": None}
                    for t in range(NT_OWN):
                        bi = t % 2
                        xf = prev5["xfree"].get(bi)
                        if xf is not None:
                            SY.wait_ge(xf[0].h, xf[1])
                        ld = dmai(SY, xtile[bi][:, :],
                                  x2bf[:, t * TT:(t + 1) * TT]
                                  .rearrange("(c p) t -> p c t", p=128))
                        V.wait_ge(s_sy.h, ld)
                        if prev5["sq_read"] is not None:
                            V.wait_ge(prev5["sq_read"][0].h,
                                      prev5["sq_read"][1])
                        inst = V.tensor_tensor(xsq5[:, :], xtile[bi][:, :],
                                               xtile[bi][:, :], ALU.mult)
                        sq_done = s_ve.inc(inst)
                        T.wait_ge(s_sy.h, ld)
                        T.wait_ge(prev5["rows_read"][0].h,
                                  prev5["rows_read"][1])
                        for k in range(NKC):
                            inst = T.matmul(pb[4][0:1, :], ones_col_bf[:, :],
                                            xtile[bi][:, k * TT:(k + 1) * TT],
                                            start=(k == 0),
                                            stop=(k == NKC - 1))
                        T.wait_ge(s_ve.h, sq_done)
                        for k in range(NKC):
                            inst = T.matmul(pb[5][0:1, :], ones_col_bf[:, :],
                                            xsq5[:, k * TT:(k + 1) * TT],
                                            start=(k == 0),
                                            stop=(k == NKC - 1))
                        sums_done = s_pe.inc(inst)
                        prev5["sq_read"] = (s_pe, sums_done)
                        prev5["xfree"][bi] = (s_pe, sums_done)
                        V.wait_ge(s_pe.h, sums_done)
                        inst = V.tensor_scalar(
                            negm2_bf[0:1, t * TT:(t + 1) * TT],
                            pb[4][0:1, :], -1.0 / D, None, ALU.mult)
                        fence(V, inst)
                        inst = V.tensor_scalar(rowt5[0:1, 0:TT],
                                               pb[5][0:1, :], 1.0 / D, None,
                                               ALU.mult)
                        fence(V, inst)
                        prev5["rows_read"] = (s_ve, s_ve.n)
                        inst = V.tensor_tensor(
                            rowt5[0:1, TT:2 * TT],
                            negm2_bf[0:1, t * TT:(t + 1) * TT],
                            negm2_bf[0:1, t * TT:(t + 1) * TT], ALU.mult)
                        fence(V, inst)
                        inst = V.tensor_tensor(rowt5[0:1, 2 * TT:3 * TT],
                                               rowt5[0:1, 0:TT],
                                               rowt5[0:1, TT:2 * TT],
                                               ALU.subtract)
                        var_done = s_ve.inc(inst)
                        A.wait_ge(s_ve.h, var_done)
                        inst = A.activation(rowt5[0:1, 3 * TT:4 * TT],
                                            rowt5[0:1, 2 * TT:3 * TT], AF.Ln,
                                            bias=eps_sb[0:1, 0:1], scale=1.0)
                        fence(A, inst)
                        inst = A.activation(rowt5[0:1, 0:TT],
                                            rowt5[0:1, 3 * TT:4 * TT], AF.Exp,
                                            scale=-0.5)
                        rstd_done = s_ac.inc(inst)
                        T.wait_ge(s_ac.h, rstd_done)
                        if prev5["bc_read"] is not None:
                            T.wait_ge(prev5["bc_read"][0].h,
                                      prev5["bc_read"][1])
                        inst = T.matmul(pb[6][:, :], ones_row_f[0:1, :],
                                        rowt5[0:1, 0:TT], start=True,
                                        stop=True)
                        bc_done = s_pe.inc(inst)
                        A.wait_ge(s_pe.h, bc_done)
                        inst = A.copy(rstd2_bc[:, t * TT:(t + 1) * TT],
                                      pb[6][:, :])
                        bc_ev = s_ac.inc(inst)
                        prev5["bc_read"] = (s_ac, bc_ev)
                        A.wait_ge(s_ac.h, bc_ev)
                        for c in range(4):
                            cc = 4 * t + c
                            dmai(A, rstd2_col[:, cc:cc + 1],
                                 rowt5[0:1, c * 128:(c + 1) * 128])
                        V.wait_ge(s_a.h, s_a.n)
                        V.wait_ge(s_ac.h, bc_ev)

                    for t in range(NT_OWN):
                        bi = t % 2
                        xf = prev5["xfree"].get(bi)
                        if xf is not None:
                            SY.wait_ge(xf[0].h, xf[1])
                        ld = dmai(SY, xtile[bi][:, :],
                                  x2bf[:, t * TT:(t + 1) * TT]
                                  .rearrange("(c p) t -> p c t", p=128))
                        ri = t % 2
                        T.wait_ge(s_sy.h, ld)
                        T.wait_ge(s_ve.h, s_ve.n)
                        for k in range(NKC):
                            T.matmul(pb[4][0:64, :],
                                     art_sb[:, k * 80:k * 80 + 64],
                                     xtile[bi][:, k * TT:(k + 1) * TT],
                                     start=(k == 0), stop=False)
                        inst = T.matmul(pb[4][0:64, :], csar_bf[0:1, 0:64],
                                        negm2_bf[0:1, t * TT:(t + 1) * TT],
                                        start=False, stop=True)
                        rh_done = s_pe.inc(inst)
                        for k in range(NKC):
                            T.matmul(pb[5][0:16, :],
                                     art_sb[:, k * 80 + 64:k * 80 + 80],
                                     xtile[bi][:, k * TT:(k + 1) * TT],
                                     start=(k == 0), stop=False)
                        inst = T.matmul(pb[5][0:16, :], csar_bf[0:1, 64:80],
                                        negm2_bf[0:1, t * TT:(t + 1) * TT],
                                        start=False, stop=True)
                        lg_done = s_pe.inc(inst)
                        prev5["xfree"][bi] = (s_pe, lg_done)
                        V.wait_ge(s_pe.h, rh_done)
                        inst = V.tensor_tensor(
                            rh2[ri][0:64, :], pb[4][0:64, :],
                            rstd2_bc[0:64, t * TT:(t + 1) * TT], ALU.mult)
                        rh_ev = s_ve.inc(inst)
                        A.wait_ge(s_ve.h, rh_ev)
                        rh_c2 = dmai(A, rh2[ri][64:128, :], rh2[ri][0:64, :])
                        V.wait_ge(s_pe.h, lg_done)
                        inst = V.tensor_tensor(
                            lgs[0:16, :], pb[5][0:16, :],
                            rstd2_bc[0:16, t * TT:(t + 1) * TT], ALU.mult)
                        lg_ev = s_ve.inc(inst)
                        T.wait_ge(s_ve.h, lg_ev)
                        for j in range(4):
                            inst = T.transpose(pb[6][:, j * 16:(j + 1) * 16],
                                               lgs[0:16,
                                                   j * 128:(j + 1) * 128],
                                               eye_sb[0:16, 0:16])
                        ltr_done = s_pe.inc(inst)
                        A.wait_ge(s_pe.h, ltr_done)
                        inst = A.copy(lt_tm[:, :], pb[6][:, 0:64])
                        lt_ev = s_ac.inc(inst)
                        T.wait_ge(s_ac.h, lt_ev)
                        V.wait_ge(s_ac.h, lt_ev)
                        for j in range(4):
                            lsl = lt_tm[:, j * 16:(j + 1) * 16]
                            psl = ptm[:, j * 16:(j + 1) * 16]
                            csl = comb_tm[:, j * 16:(j + 1) * 16]
                            inst = V.reduce_max(rcol[:, 0:1], lsl, axis=AX)
                            fence(V, inst)
                            inst = V.tensor_scalar(rcol[:, 1:2], rcol[:, 0:1],
                                                   -1.0, None, ALU.mult)
                            fence(V, inst)
                            A.wait_ge(s_ve.h, s_ve.n)
                            inst = A.activation(psl, lsl, AF.Exp,
                                                bias=rcol[:, 1:2], scale=1.0)
                            fence(A, inst)
                            V.wait_ge(s_ac.h, s_ac.n)
                            inst = V.reduce_sum(rcol[:, 2:3], psl, axis=AX)
                            fence(V, inst)
                            inst = V.reduce_max(rcol[:, 3:4], psl, axis=AX)
                            fence(V, inst)
                            inst = V.tensor_scalar(rtmp[:, 0:16], psl,
                                                   rcol[:, 3:4], None,
                                                   ALU.is_ge)
                            fence(V, inst)
                            inst = V.scalar_tensor_tensor(rtmp[:, 16:32],
                                                          rtmp[:, 0:16], 1e30,
                                                          psl, ALU.mult,
                                                          ALU.subtract)
                            fence(V, inst)
                            inst = V.tensor_scalar(rtmp[:, 32:48],
                                                   rtmp[:, 16:32], -1.0,
                                                   None, ALU.mult)
                            fence(V, inst)
                            inst = V.reduce_max(rcol[:, 4:5], rtmp[:, 32:48],
                                                axis=AX)
                            fence(V, inst)
                            inst = V.tensor_scalar(rcol[:, 5:6], rcol[:, 2:3],
                                                   1e-6, None, ALU.mult)
                            fence(V, inst)
                            inst = V.tensor_tensor(rcol[:, 6:7], rcol[:, 3:4],
                                                   rcol[:, 4:5], ALU.add)
                            fence(V, inst)
                            inst = V.tensor_tensor(rcol[:, 7:8], rcol[:, 5:6],
                                                   rcol[:, 6:7], ALU.add)
                            fence(V, inst)
                            inst = V.reciprocal(rcol[:, 8:9], rcol[:, 7:8])
                            fence(V, inst)
                            inst = V.reciprocal(rcol[:, 9:10],
                                                rstd2_col[:, 4 * t + j:
                                                          4 * t + j + 1])
                            fence(V, inst)
                            inst = V.tensor_scalar(rtmp[:, 0:16], psl,
                                                   rcol[:, 4:5], None,
                                                   ALU.is_ge)
                            fence(V, inst)
                            inst = V.tensor_scalar(rtmp[:, 16:32], psl,
                                                   rcol[:, 8:9], 0.5,
                                                   ALU.mult, ALU.mult)
                            fence(V, inst)
                            inst = V.tensor_scalar(rtmp[:, 48:64],
                                                   rtmp[:, 16:32],
                                                   rcol[:, 9:10], None,
                                                   ALU.mult)
                            fence(V, inst)
                            inst = V.tensor_tensor(csl, rtmp[:, 0:16],
                                                   rtmp[:, 48:64], ALU.mult)
                            fence(V, inst)
                        comb_done = s_ve.n
                        T.wait_ge(s_ve.h, comb_done)
                        T.wait_ge(s_ac.h, s_ac.n)
                        for j in range(4):
                            inst = T.transpose(pb[6][0:16,
                                                     j * 128:(j + 1) * 128],
                                               comb_tm[:, j * 16:(j + 1) * 16],
                                               eye_sb[:, :])
                        ctr_done = s_pe.inc(inst)
                        A.wait_ge(s_pe.h, ctr_done)
                        inst = A.copy(comb_sb[0:16, :], pb[6][0:16, 0:TT])
                        cb_ev = s_ac.inc(inst)
                        G.wait_ge(s_ac.h, cb_ev)
                        G.wait_ge(s_pe.h, s_pe.n)
                        for e_ in range(16):
                            crow_ld = dmai(G, comb_row[0:1, e_ * TT:
                                                       (e_ + 1) * TT],
                                           comb_sb[e_:e_ + 1, :])
                        for c in range(8):
                            T.wait_ge(s_g.h, crow_ld)
                            T.wait_ge(s_ve.h, s_ve.n)
                            T.matmul(pb[7][:, :], sel_lo[0:1, :],
                                     comb_row[0:1, (2 * c) * TT:
                                              (2 * c + 1) * TT],
                                     start=True, stop=False)
                            inst = T.matmul(pb[7][:, :], sel_hi[0:1, :],
                                            comb_row[0:1, (2 * c + 1) * TT:
                                                     (2 * c + 2) * TT],
                                            start=False, stop=True)
                            bcp_done = s_pe.inc(inst)
                            V.wait_ge(s_pe.h, bcp_done)
                            V.wait_ge(s_a.h, rh_c2)
                            inst = V.tensor_tensor(
                                rhe[t][:, c * TT:(c + 1) * TT], pb[7][:, :],
                                rh2[ri][:, :], ALU.mult)
                            s_ve.inc(inst)
                    barrier()

            if _STOP_AFTER == "router":
                _finalize(nc, ALLE, SY, s_st)
                return nc

            with nc.body("p6", valid_engines=ALLE):
                with contextlib.ExitStack() as c6:
                    E6 = c6.enter_context
                    NCH6 = SLAB6 // 128
                    wbuf = [E6(nc.sbuf_tensor(f"f6_w{i}", [128, NKC * SLAB6],
                                              BF16)) for i in range(2)]
                    bfbuf = [E6(nc.sbuf_tensor(f"f6_b{i}", [128, 8 * SLAB6],
                                               BF16)) for i in range(2)]
                    xtile6 = [E6(nc.sbuf_tensor(f"f6_x{i}", [128, NKC * TT],
                                                BF16)) for i in range(2)]
                    x2res = [E6(nc.sbuf_tensor(f"f6_r{i}",
                                               [128, NCH6 * TT], F32))
                             for i in range(2)]
                    t16 = E6(nc.sbuf_tensor("f6_t1", [128, TT], F32))
                    ob6 = [E6(nc.sbuf_tensor(f"f6_o{i}", [128, TT], F32))
                           for i in range(2)]
                    cs6 = E6(nc.sbuf_tensor("f6_cs", [1, SLAB6], BF16))
                    st6 = {"wfree": {}, "xfree": {}, "rfree": {},
                           "bankfree": {}, "ofree": {}, "cs_read": None}
                    for si in range(D // SLAB6):
                        wi = si % 2
                        wf = st6["wfree"].get(wi)
                        if wf is not None:
                            G.wait_ge(wf[0].h, wf[1])
                        w_ld = dmai(G, wbuf[wi][:, :]
                                    .rearrange("p (c n) -> p c n", c=NKC),
                                    bwT[:, si * SLAB6:(si + 1) * SLAB6]
                                    .rearrange("(c p) n -> p c n", p=128))
                        b_ld = dmai(G, bfbuf[wi][:, :]
                                    .rearrange("p (c n) -> p c n", c=8),
                                    bfm_d[:, si * SLAB6:(si + 1) * SLAB6]
                                    .rearrange("(c p) n -> p c n", p=128))
                        T.wait_ge(s_g.h, w_ld)
                        if st6["cs_read"] is not None:
                            T.wait_ge(st6["cs_read"][0].h,
                                      st6["cs_read"][1])
                        for k in range(NKC):
                            inst = T.matmul(
                                pb[4][0:1, 0:SLAB6], ones_col_bf[:, :],
                                wbuf[wi][:, k * SLAB6:(k + 1) * SLAB6],
                                start=(k == 0), stop=(k == NKC - 1))
                        csd = s_pe.inc(inst)
                        A.wait_ge(s_pe.h, csd)
                        inst = A.copy(cs6[0:1, :], pb[4][0:1, 0:SLAB6])
                        cs_done = s_ac.inc(inst)
                        st6["cs_read"] = (s_ac, cs_done)
                        for t in range(NT_OWN):
                            it = si * NT_OWN + t
                            xi = it % 2
                            xf = st6["xfree"].get(xi)
                            if xf is not None:
                                SY.wait_ge(xf[0].h, xf[1])
                            x_ld = dmai(SY, xtile6[xi][:, :],
                                        x2bf[:, t * TT:(t + 1) * TT]
                                        .rearrange("(c p) t -> p c t", p=128))
                            rf = st6["rfree"].get(xi)
                            if rf is not None:
                                A.wait_ge(rf[0].h, rf[1])
                            r_ld = dmai(A, x2res[xi][:, :]
                                        .rearrange("p (c t) -> p c t",
                                                   c=NCH6),
                                        x2s[si * SLAB6:(si + 1) * SLAB6,
                                            t * TT:(t + 1) * TT]
                                        .rearrange("(c p) t -> p c t", p=128))
                            for m in range(NCH6):
                                bank = (it * NCH6 + m) % 4
                                bf_ = st6["bankfree"].get(bank)
                                T.wait_ge(s_g.h, b_ld)
                                T.wait_ge(s_sy.h, x_ld)
                                T.wait_ge(s_ac.h, cs_done)
                                if bf_ is not None:
                                    T.wait_ge(bf_[0].h, bf_[1])
                                for k in range(NKC):
                                    T.matmul(
                                        pb[bank][:, :],
                                        wbuf[wi][:, k * SLAB6 + m * 128:
                                                 k * SLAB6 + (m + 1) * 128],
                                        xtile6[xi][:, k * TT:(k + 1) * TT],
                                        start=(k == 0), stop=False)
                                for c in range(8):
                                    T.matmul(
                                        pb[bank][:, :],
                                        bfbuf[wi][:, c * SLAB6 + m * 128:
                                                  c * SLAB6 + (m + 1) * 128],
                                        rhe[t][:, c * TT:(c + 1) * TT],
                                        start=False, stop=False)
                                inst = T.matmul(
                                    pb[bank][:, :],
                                    cs6[0:1, m * 128:(m + 1) * 128],
                                    negm2_bf[0:1, t * TT:(t + 1) * TT],
                                    start=False, stop=True)
                                mm_done = s_pe.inc(inst)
                                V.wait_ge(s_pe.h, mm_done)
                                V.wait_ge(s_a.h, r_ld)
                                inst = V.tensor_tensor(
                                    t16[:, :], pb[bank][:, :],
                                    rstd2_bc[:, t * TT:(t + 1) * TT],
                                    ALU.mult)
                                fence(V, inst)
                                st6["bankfree"][bank] = (s_ve, s_ve.n)
                                oi = (it * NCH6 + m) % 2
                                of = st6["ofree"].get(oi)
                                if of is not None:
                                    V.wait_ge(of[0].h, of[1])
                                oc = (si * SLAB6) // 128 + m
                                inst = V.scalar_tensor_tensor(
                                    ob6[oi][:, :], t16[:, :],
                                    bb_sb[:, oc:oc + 1],
                                    x2res[xi][:, m * TT:(m + 1) * TT],
                                    ALU.add, ALU.add)
                                o_done = s_ve.inc(inst)
                                SY.wait_ge(s_ve.h, o_done)
                                r0 = si * SLAB6 + m * 128
                                inst = SY.dma_start(
                                    out_d[r0:r0 + 128, t * TT:(t + 1) * TT],
                                    ob6[oi][:, :])
                                s_st.inc(inst, 16)
                                st6["ofree"][oi] = (s_st, s_st.n)
                            st6["xfree"][xi] = (s_pe, s_pe.n)
                            st6["rfree"][xi] = (s_ve, s_ve.n)
                        st6["wfree"][wi] = (s_pe, s_pe.n)

            _finalize(nc, ALLE, SY, s_st)
    return nc


def _finalize(nc, ALLE, SY, s_st):
    with nc.body("fin", valid_engines=ALLE):
        if s_st.n > 0:
            SY.wait_ge(s_st.h, s_st.n)
        else:
            SY.nop()



def _deinterleave_perm():
    per_head = np.concatenate([np.arange(0, HD, 2), np.arange(1, HD, 2)])
    perm = np.arange(QCOLS + KCOLS + VCOLS)
    for h in range(NH):
        perm[h * HD:(h + 1) * HD] = h * HD + per_head
    for h in range(NKV):
        base = QCOLS + h * HD
        perm[base:base + HD] = base + per_head
    return perm


def _rope_tables(cols):
    inv = 10000.0 ** (-(np.arange(64, dtype=np.float64)) / 64.0)
    ang = cols[None, :].astype(np.float64) * inv[:, None]
    c = np.cos(ang)
    s = np.sin(ang)
    cr = np.ascontiguousarray(np.concatenate([c, c], axis=0)
                              .astype(np.float32))
    sg = np.ascontiguousarray(np.concatenate([-s, s], axis=0)
                              .astype(np.float32))
    return cr, sg


def _own_cols(parity):
    return np.concatenate([np.arange(g * TT, (g + 1) * TT)
                           for g in G_LISTS[parity]])


def _fm_chunks(v, n):
    return np.ascontiguousarray(v.reshape(n, 128).T)


def _prepare_in_maps(L0_attn_x, L0_attn_w_qkv, L0_attn_b_qkv, L0_attn_w_o,
                     L0_attn_b_o, L0_ffn_base_w, L0_ffn_base_b, L0_ffn_A,
                     L0_ffn_B, L0_ffn_router_w):
    x = np.ascontiguousarray(np.asarray(L0_attn_x, dtype=np.float32))
    w_qkv = np.asarray(L0_attn_w_qkv, dtype=np.float32)
    b_qkv = np.asarray(L0_attn_b_qkv, dtype=np.float32)
    w_o = np.asarray(L0_attn_w_o, dtype=np.float32)
    b_o = np.asarray(L0_attn_b_o, dtype=np.float32)
    base_w = np.asarray(L0_ffn_base_w, dtype=np.float32)
    base_b = np.asarray(L0_ffn_base_b, dtype=np.float32)
    A_ = np.asarray(L0_ffn_A, dtype=np.float32)
    B_ = np.asarray(L0_ffn_B, dtype=np.float32)
    router_w = np.asarray(L0_ffn_router_w, dtype=np.float32)

    perm = _deinterleave_perm()
    wq_p = w_qkv[perm]
    b_p = b_qkv[perm]
    wqkvT = np.ascontiguousarray(wq_p.T)
    bqk = _fm_chunks(b_p[:QCOLS + KCOLS], 40)
    bqks = np.ascontiguousarray(
        np.concatenate([bqk[64:128], bqk[0:64]], axis=0))
    bvrow = np.ascontiguousarray(b_p[QCOLS + KCOLS:][None, :])
    woT = np.ascontiguousarray(w_o.T)
    bo = _fm_chunks(b_o, 32)
    bwT = np.ascontiguousarray(base_w.T)
    bb = _fm_chunks(base_b, 32)
    art = np.ascontiguousarray(np.concatenate([A_.T, router_w.T], axis=1))
    bfm = np.ascontiguousarray(B_.transpose(0, 2, 1).reshape(16 * 64, D))
    crf, sgf = _rope_tables(np.arange(S))

    pp = np.arange(128)[:, None]
    ff = np.arange(TT)[None, :]
    msks = []
    for p in range(2):
        m = np.zeros((128, 32 * TT), dtype=np.float32)
        for t in range(NT_OWN):
            g = G_LISTS[p][t]
            nk = 4 * (2 * t + 2)
            for j in range(8):
                c = nk - 8 + j
                m[:, (t * 8 + j) * TT:(t * 8 + j + 1) * TT] = (
                    (512 * g + ff) >= (128 * c + pp)).astype(np.float32)
        msks.append(m)

    in_maps = []
    for core in range(8):
        b = core // 2
        p = core % 2
        cols = _own_cols(p)
        xTa = np.ascontiguousarray(x[b].T)
        xToa = np.ascontiguousarray(xTa[:, cols])
        cro, sgo = _rope_tables(cols)
        in_maps.append({
            "xT": xTa, "xTo": xToa, "wqkvT": wqkvT, "bqk": bqk, "bqks": bqks,
            "bvrow": bvrow, "woT": woT, "bo": bo, "bwT": bwT, "bb": bb,
            "art": art, "bfm": bfm, "cro": cro, "sgo": sgo,
            "crf": crf, "sgf": sgf, "msk": msks[p],
        })

    return in_maps


def kernel(**inputs):
    in_maps = _prepare_in_maps(**inputs)
    nc = build_program()
    res = run_bass_kernel_spmd(nc, in_maps, core_ids=list(range(8)))

    out = np.empty((B, S, D), dtype=np.float32)
    for core in range(8):
        b = core // 2
        p = core % 2
        cols = _own_cols(p)
        out[b, cols, :] = res.results[core]["out"].T
    return out


if __name__ == "__main__":
    import reference
    inputs = reference.setup_inputs()
    got = kernel(**{k: np.asarray(v) for k, v in inputs.items()})
    want = np.asarray(reference.reference(**inputs))
    err = np.sqrt(((got - want) ** 2).mean() / (want ** 2).mean())
    print("rel l2:", err)


# version 03
# speedup vs baseline: 1.0781x; 1.0781x over previous
import contextlib
import math

import numpy as np
import concourse.bass as bass
import concourse.mybir as mybir
from concourse.bass_utils import run_bass_kernel_spmd

F32 = mybir.dt.float32
BF16 = mybir.dt.bfloat16
I32 = mybir.dt.int32
AF = mybir.ActivationFunctionType
ALU = mybir.AluOpType
AX = mybir.AxisListType.X

D = 4096
S = 4096
B = 4
NH = 32
NKV = 8
HD = 128
QCOLS = 4096
KCOLS = 1024
VCOLS = 1024
TT = 512
NT_FULL = S // TT
NT_OWN = NT_FULL // 2
NKC = D // 128
SLAB = 512
SLAB6 = 256
G_LISTS = [[0, 3, 4, 7], [1, 2, 5, 6]]
INV_SQRT_HD = 1.0 / math.sqrt(HD)
LN_EPS = 1e-5

_DEBUG_TAPS = []
_STOP_AFTER = None
_FORCE_PARITY = None
_ATTN_LIMIT = None


class Sem:
    def __init__(self, h):
        self.h = h
        self.n = 0

    def inc(self, inst, k=1):
        inst.then_inc(self.h, k)
        self.n += k
        return self.n


def build_program():
    nc = bass.Bass("TRN2", target_bir_lowering=False)
    ALLE = list(mybir.ALL_ENGINES)

    def din(name, shape):
        return nc.dram_tensor(name, shape, F32, kind="ExternalInput")

    xT = din("xT", [D, S])
    xTo = din("xTo", [D, S // 2])
    wqkvT = din("wqkvT", [D, QCOLS + KCOLS + VCOLS])
    bqk_d = din("bqk", [128, 40])
    bqks_d = din("bqks", [128, 40])
    bvrow_d = din("bvrow", [1, VCOLS])
    woT = din("woT", [D, D])
    bo_d = din("bo", [128, 32])
    bwT = din("bwT", [D, D])
    bb_d = din("bb", [128, 32])
    art_d = din("art", [D, 80])
    bfm_d = din("bfm", [1024, D])
    cro_d = din("cro", [128, S // 2])
    sgo_d = din("sgo", [128, S // 2])
    crf_d = din("crf", [128, S])
    sgf_d = din("sgf", [128, S])
    msk_d = din("msk", [128, 32 * TT])

    out_d = nc.dram_tensor("out", [D, S // 2], F32, kind="ExternalOutput")

    def scratch(name, shape, dtype):
        kind = "ExternalOutput" if name in _DEBUG_TAPS else "Internal"
        return nc.dram_tensor(name, shape, dtype, kind=kind)

    xbf = scratch("xbf", [D, S], BF16)
    xbfo = scratch("xbfo", [D, S // 2], BF16)
    qs = scratch("qs", [QCOLS, S // 2], BF16)
    ks = scratch("ks", [KCOLS, S], BF16)
    vs = scratch("vs", [S, VCOLS], BF16)
    aos = scratch("aos", [D, S // 2], BF16)
    x2s = scratch("x2s", [D, S // 2], F32)
    x2bf = scratch("x2bf", [D, S // 2], BF16)

    ctx = contextlib.ExitStack()
    with ctx:
        E = ctx.enter_context

        s_sy = Sem(E(nc.semaphore("s_sy")))
        s_a = Sem(E(nc.semaphore("s_a")))
        s_g = Sem(E(nc.semaphore("s_g")))
        s_pe = Sem(E(nc.semaphore("s_pe")))
        s_ve = Sem(E(nc.semaphore("s_ve")))
        s_ac = Sem(E(nc.semaphore("s_ac")))
        s_gp = Sem(E(nc.semaphore("s_gp")))
        s_bar = Sem(E(nc.semaphore("s_bar")))
        s_st = Sem(E(nc.semaphore("s_st")))
        ALL_SEMS = [s_sy, s_a, s_g, s_pe, s_ve, s_ac, s_gp, s_bar, s_st]

        T, V, A, SY, G = nc.tensor, nc.vector, nc.scalar, nc.sync, nc.gpsimd
        DMA_SEM = {id(SY): s_sy, id(A): s_a, id(G): s_g}
        ENG_SEM = {id(T): s_pe, id(V): s_ve, id(A): s_ac, id(G): s_gp}

        def dmai(q, out_ap, in_ap, **kw):
            sem = DMA_SEM[id(q)]
            inst = q.dma_start(out_ap, in_ap, **kw)
            sem.inc(inst, 16)
            return sem.n

        def fence(eng, inst):
            sem = ENG_SEM[id(eng)]
            sem.inc(inst)
            eng.wait_ge(sem.h, sem.n)

        def barrier():
            for eng in (T, V, A, SY, G):
                dsem = DMA_SEM.get(id(eng))
                if dsem is not None:
                    eng.wait_ge(dsem.h, dsem.n)
                s_bar.inc(eng.nop())
            tgt = s_bar.n
            for eng in (T, V, A, SY, G):
                eng.wait_ge(s_bar.h, tgt)

        pb = [E(nc.psum_tensor(f"pb{i}", [128, 512], F32)) for i in range(8)]

        ones_col_bf = E(nc.sbuf_tensor("ones_col_bf", [128, 1], BF16))
        ones_row_bf = E(nc.sbuf_tensor("ones_row_bf", [1, 128], BF16))
        ones_row_f = E(nc.sbuf_tensor("ones_row_f", [1, 128], F32))
        sel_lo = E(nc.sbuf_tensor("sel_lo", [1, 128], BF16))
        sel_hi = E(nc.sbuf_tensor("sel_hi", [1, 128], BF16))
        bqk_sb = E(nc.sbuf_tensor("bqk_sb", [128, 40], F32))
        bqks_sb = E(nc.sbuf_tensor("bqks_sb", [128, 40], F32))
        bv_sb = E(nc.sbuf_tensor("bv_sb", [1, VCOLS], BF16))
        bo_sb = E(nc.sbuf_tensor("bo_sb", [128, 32], F32))
        bb_sb = E(nc.sbuf_tensor("bb_sb", [128, 32], F32))
        eye_sb = E(nc.sbuf_tensor("eye_sb", [128, 128], F32))
        eps_sb = E(nc.sbuf_tensor("eps_sb", [1, 1], F32))
        nmax = E(nc.sbuf_tensor("nmax", [1, 4], F32))

        with nc.Block() as block0:
            @block0.tensor
            def _(e):
                e.nop()

            @block0.vector
            def _(e):
                e.nop()

            @block0.scalar
            def _(e):
                e.nop()

            @block0.sync
            def _(e):
                e.nop()

            @block0.gpsimd
            def _(e):
                e.nop()

        c0s = contextlib.ExitStack()
        eye_tmp = c0s.enter_context(nc.sbuf_tensor("eye_tmp", [128, 256], F32))
        with nc.body("p0", valid_engines=ALLE):
            G.memset(ones_col_bf[:, :], 1.0)
            G.memset(eps_sb[:, :], LN_EPS)
            G.memset(ones_row_bf[:, :], 1.0)
            G.memset(ones_row_f[:, :], 1.0)
            G.memset(sel_lo[0:1, 0:64], 1.0)
            G.memset(sel_lo[0:1, 64:128], 0.0)
            G.memset(sel_hi[0:1, 0:64], 0.0)
            G.memset(sel_hi[0:1, 64:128], 1.0)
            G.memset(nmax[:, :], 0.0)
            idn_i = c0s.enter_context(nc.sbuf_tensor("idn_i", [128, 128], I32))
            inst = G.iota(idn_i[:, :], pattern=[[1, 128]], base=0,
                          channel_multiplier=-1)
            s_gp.inc(inst)
            V.wait_ge(s_gp.h, s_gp.n)
            inst = V.tensor_copy(eye_tmp[:, 0:128], idn_i[:, :])
            fence(V, inst)
            inst = V.tensor_scalar(eye_tmp[:, 128:256], eye_tmp[:, 0:128],
                                   0.5, None, ALU.is_lt)
            fence(V, inst)
            inst = V.tensor_scalar(eye_tmp[:, 0:128], eye_tmp[:, 0:128],
                                   -0.5, None, ALU.is_ge)
            fence(V, inst)
            inst = V.tensor_tensor(eye_sb[:, :], eye_tmp[:, 0:128],
                                   eye_tmp[:, 128:256], ALU.mult)
            s_ve.inc(inst)
            dmai(A, bqk_sb[:, :], bqk_d[:, :])
            dmai(A, bqks_sb[:, :], bqks_d[:, :])
            dmai(G, bv_sb[:, :], bvrow_d[:, :])
            dmai(A, bo_sb[:, :], bo_d[:, :])
            dmai(A, bb_sb[:, :], bb_d[:, :])
            for i in range(8):
                dmai(G, xbf[:, i * TT:(i + 1) * TT], xT[:, i * TT:(i + 1) * TT])
            for i in range(4):
                dmai(G, xbfo[:, i * TT:(i + 1) * TT],
                     xTo[:, i * TT:(i + 1) * TT])
            xcast_done = s_g.n
        c0s.close()

        with contextlib.ExitStack() as c12:
            E12 = c12.enter_context
            negm_bf = E12(nc.sbuf_tensor("negm_bf", [1, S], BF16))
            negmo_bf = E12(nc.sbuf_tensor("negmo_bf", [1, S // 2], BF16))
            rstd_bc = E12(nc.sbuf_tensor("rstd_bc", [128, S], F32))
            rstdo_bc = E12(nc.sbuf_tensor("rstdo_bc", [128, S // 2], F32))
            rstd_col = E12(nc.sbuf_tensor("rstd_col", [128, NKC], F32))
            colsum_bf = E12(nc.sbuf_tensor("colsum_bf", [1, SLAB], BF16))
            colsum_v_bf = E12(nc.sbuf_tensor("colsum_v_bf", [1, SLAB], BF16))
            crs = E12(nc.sbuf_tensor("crs", [128, S // 2], F32))
            sgs = E12(nc.sbuf_tensor("sgs", [128, S // 2], F32))

            with nc.body("p1", valid_engines=ALLE):
                with contextlib.ExitStack() as c1:
                    E1 = c1.enter_context
                    xtile = [E1(nc.sbuf_tensor(f"st_x{i}", [128, NKC * TT],
                                               BF16)) for i in range(2)]
                    xsq = E1(nc.sbuf_tensor("st_xsq", [128, NKC * TT], BF16))
                    rowt = E1(nc.sbuf_tensor("rowt", [1, 4 * TT], F32))

                    prev = {"rows_read": None, "xfree": {},
                            "sq_read": None, "bc_read": None, "n": 0}

                    def stats_pass(src, n_tiles, negm_row, rstd_bcast,
                                   do_cols):
                        for t in range(n_tiles):
                            bi = prev["n"] % 2
                            prev["n"] += 1
                            SY.wait_ge(s_g.h, xcast_done)
                            xf = prev["xfree"].get(bi)
                            if xf is not None:
                                SY.wait_ge(xf[0].h, xf[1])
                            ld = dmai(SY, xtile[bi][:, :],
                                      src[:, t * TT:(t + 1) * TT]
                                      .rearrange("(c p) t -> p c t", p=128))
                            V.wait_ge(s_sy.h, ld)
                            if prev["sq_read"] is not None:
                                V.wait_ge(prev["sq_read"][0].h,
                                          prev["sq_read"][1])
                            inst = V.tensor_tensor(xsq[:, :], xtile[bi][:, :],
                                                   xtile[bi][:, :], ALU.mult)
                            sq_done = s_ve.inc(inst)
                            T.wait_ge(s_sy.h, ld)
                            if prev["rows_read"] is not None:
                                T.wait_ge(prev["rows_read"][0].h,
                                          prev["rows_read"][1])
                            for k in range(NKC):
                                inst = T.matmul(
                                    pb[4][0:1, :], ones_col_bf[:, :],
                                    xtile[bi][:, k * TT:(k + 1) * TT],
                                    start=(k == 0), stop=(k == NKC - 1))
                            T.wait_ge(s_ve.h, sq_done)
                            for k in range(NKC):
                                inst = T.matmul(
                                    pb[5][0:1, :], ones_col_bf[:, :],
                                    xsq[:, k * TT:(k + 1) * TT],
                                    start=(k == 0), stop=(k == NKC - 1))
                            sums_done = s_pe.inc(inst)
                            prev["sq_read"] = (s_pe, sums_done)
                            prev["xfree"][bi] = (s_pe, sums_done)
                            V.wait_ge(s_pe.h, sums_done)
                            inst = V.tensor_scalar(
                                negm_row[0:1, t * TT:(t + 1) * TT],
                                pb[4][0:1, :], -1.0 / D, None, ALU.mult)
                            fence(V, inst)
                            inst = V.tensor_scalar(rowt[0:1, 0:TT],
                                                   pb[5][0:1, :], 1.0 / D,
                                                   None, ALU.mult)
                            fence(V, inst)
                            prev["rows_read"] = (s_ve, s_ve.n)
                            inst = V.tensor_tensor(
                                rowt[0:1, TT:2 * TT],
                                negm_row[0:1, t * TT:(t + 1) * TT],
                                negm_row[0:1, t * TT:(t + 1) * TT], ALU.mult)
                            fence(V, inst)
                            inst = V.tensor_tensor(rowt[0:1, 2 * TT:3 * TT],
                                                   rowt[0:1, 0:TT],
                                                   rowt[0:1, TT:2 * TT],
                                                   ALU.subtract)
                            var_done = s_ve.inc(inst)
                            A.wait_ge(s_ve.h, var_done)
                            inst = A.activation(rowt[0:1, 3 * TT:4 * TT],
                                                rowt[0:1, 2 * TT:3 * TT],
                                                AF.Ln, bias=eps_sb[0:1, 0:1],
                                                scale=1.0)
                            fence(A, inst)
                            V.wait_ge(s_ac.h, s_ac.n)
                            inst = V.nop()
                            fence(V, inst)
                            inst = A.activation(rowt[0:1, 0:TT],
                                                rowt[0:1, 3 * TT:4 * TT],
                                                AF.Exp, scale=-0.5)
                            rstd_done = s_ac.inc(inst)
                            T.wait_ge(s_ac.h, rstd_done)
                            if prev["bc_read"] is not None:
                                T.wait_ge(prev["bc_read"][0].h,
                                          prev["bc_read"][1])
                            inst = T.matmul(pb[6][:, :], ones_row_f[0:1, :],
                                            rowt[0:1, 0:TT], start=True,
                                            stop=True)
                            bc_done = s_pe.inc(inst)
                            A.wait_ge(s_pe.h, bc_done)
                            inst = A.copy(rstd_bcast[:, t * TT:(t + 1) * TT],
                                          pb[6][:, :])
                            bc_ev = s_ac.inc(inst)
                            prev["bc_read"] = (s_ac, bc_ev)
                            if do_cols:
                                A.wait_ge(s_ac.h, bc_ev)
                                for c in range(4):
                                    cc = 4 * t + c
                                    dmai(A, rstd_col[:, cc:cc + 1],
                                         rowt[0:1, c * 128:(c + 1) * 128])
                            V.wait_ge(s_a.h, s_a.n)
                            V.wait_ge(s_ac.h, bc_ev)

                    stats_pass(xbf, NT_FULL, negm_bf, rstd_bc, True)
                    stats_pass(xbfo, NT_OWN, negmo_bf, rstdo_bc, False)
                    barrier()

            with nc.body("p2", valid_engines=ALLE):
                with contextlib.ExitStack() as c2:
                    E2 = c2.enter_context
                    wbuf = [E2(nc.sbuf_tensor(f"qk_w{i}", [128, NKC * SLAB],
                                              BF16)) for i in range(2)]
                    xtile = [E2(nc.sbuf_tensor(f"qk_x{i}", [128, NKC * TT],
                                               BF16)) for i in range(2)]
                    t1b = [E2(nc.sbuf_tensor(f"qk_t1{i}", [128, TT], F32))
                           for i in range(2)]
                    t1sb = [E2(nc.sbuf_tensor(f"qk_t1s{i}", [128, TT], F32))
                            for i in range(2)]
                    r1b = E2(nc.sbuf_tensor("qk_r1", [128, TT], F32))
                    outb = [E2(nc.sbuf_tensor(f"qk_o{i}", [128, TT], BF16))
                            for i in range(2)]
                    nsq = [E2(nc.sbuf_tensor(f"qk_nsq{i}", [128, TT], BF16))
                           for i in range(2)]

                    st2 = {"wfree": {}, "xfree": {}, "t1free": {},
                           "t1sfree": {}, "outfree": {}, "bankfree": {},
                           "cs_read": None, "rope": None, "rope_ld": 0,
                           "slab": 0, "tile": 0, "mm": 0, "ep": 0,
                           "nrm_pend": None, "nsqfree": {}, "nrmfree": None}

                    def flush_norm():
                        if st2["nrm_pend"] is None:
                            return
                        pbuf, psq, pcell = st2["nrm_pend"]
                        st2["nrm_pend"] = None
                        T.wait_ge(s_ac.h, psq)
                        nr = st2["nrmfree"]
                        if nr is not None:
                            T.wait_ge(nr[0].h, nr[1])
                        inst = T.matmul(pb[5][0:1, :], ones_col_bf[:, :],
                                        nsq[pbuf][:, :], start=True,
                                        stop=True)
                        nrm_done = s_pe.inc(inst)
                        st2["nsqfree"][pbuf] = (s_pe, nrm_done)
                        V.wait_ge(s_pe.h, nrm_done)
                        inst = V.reduce_max(nmax[0:1, 2:3], pb[5][0:1, :],
                                            axis=AX)
                        fence(V, inst)
                        st2["nrmfree"] = (s_ve, s_ve.n)
                        inst = V.tensor_tensor(
                            nmax[0:1, pcell:pcell + 1],
                            nmax[0:1, pcell:pcell + 1],
                            nmax[0:1, 2:3], ALU.max)
                        fence(V, inst)

                    def rope_ensure(key):
                        if st2["rope"] == key:
                            return st2["rope_ld"]
                        inst = V.nop()
                        cons = s_ve.inc(inst)
                        A.wait_ge(s_ve.h, cons)
                        if key[0] == "own":
                            dmai(A, crs[:, :], cro_d[:, :])
                            ld = dmai(A, sgs[:, :], sgo_d[:, :])
                        else:
                            b0 = key[1]
                            dmai(A, crs[:, :], crf_d[:, b0:b0 + S // 2])
                            ld = dmai(A, sgs[:, :], sgf_d[:, b0:b0 + S // 2])
                        st2["rope"] = key
                        st2["rope_ld"] = ld
                        return ld

                    def load_slab(c0):
                        si = st2["slab"]
                        st2["slab"] += 1
                        bi = si % 2
                        wf = st2["wfree"].get(bi)
                        if wf is not None:
                            G.wait_ge(wf[0].h, wf[1])
                        ld = dmai(G, wbuf[bi][:, :]
                                  .rearrange("p (c n) -> p c n", c=NKC),
                                  wqkvT[:, c0:c0 + SLAB]
                                  .rearrange("(c p) n -> p c n", p=128))
                        return ld, bi

                    def colsums(wi, w_ld, cs_bf):
                        T.wait_ge(s_g.h, w_ld)
                        if st2["cs_read"] is not None:
                            T.wait_ge(st2["cs_read"][0].h, st2["cs_read"][1])
                        for k in range(NKC):
                            inst = T.matmul(
                                pb[4][0:1, :], ones_col_bf[:, :],
                                wbuf[wi][:, k * SLAB:(k + 1) * SLAB],
                                start=(k == 0), stop=(k == NKC - 1))
                        done = s_pe.inc(inst)
                        A.wait_ge(s_pe.h, done)
                        inst = A.copy(cs_bf[0:1, :], pb[4][0:1, :])
                        v = s_ac.inc(inst)
                        st2["cs_read"] = (s_ac, v)
                        return v

                    def load_xtile(src, t):
                        n = st2["tile"]
                        st2["tile"] += 1
                        bi = n % 2
                        xf = st2["xfree"].get(bi)
                        if xf is not None:
                            SY.wait_ge(xf[0].h, xf[1])
                        ld = dmai(SY, xtile[bi][:, :],
                                  src[:, t * TT:(t + 1) * TT]
                                  .rearrange("(c p) t -> p c t", p=128))
                        return ld, bi

                    def qk_slabs(col_base, n_slabs, src, n_tiles, negm_row,
                                 rstd_bcast, out_scr, bias0, rope_of_t,
                                 ncell):
                        for si in range(n_slabs):
                            c0 = col_base + si * SLAB
                            w_ld, wi = load_slab(c0)
                            cs_done = colsums(wi, w_ld, colsum_bf)
                            for t in range(n_tiles):
                                rope_ld = rope_ensure(rope_of_t(t))
                                x_ld, xi = load_xtile(src, t)
                                for m in range(4):
                                    bank = st2["mm"] % 4
                                    st2["mm"] += 1
                                    T.wait_ge(s_sy.h, x_ld)
                                    T.wait_ge(s_g.h, w_ld)
                                    T.wait_ge(s_ac.h, cs_done)
                                    bf_ = st2["bankfree"].get(bank)
                                    if bf_ is not None:
                                        T.wait_ge(bf_[0].h, bf_[1])
                                    for k in range(NKC):
                                        T.matmul(
                                            pb[bank][:, :],
                                            wbuf[wi][:, k * SLAB + m * 128:
                                                     k * SLAB + (m + 1) * 128],
                                            xtile[xi][:, k * TT:(k + 1) * TT],
                                            start=(k == 0), stop=False)
                                    inst = T.matmul(
                                        pb[bank][:, :],
                                        colsum_bf[0:1, m * 128:(m + 1) * 128],
                                        negm_row[0:1, t * TT:(t + 1) * TT],
                                        start=False, stop=True)
                                    mm_done = s_pe.inc(inst)
                                    ti = st2["ep"] % 2
                                    st2["ep"] += 1
                                    V.wait_ge(s_pe.h, mm_done)
                                    for key in ("t1free", "outfree",
                                                "t1sfree"):
                                        f = st2[key].get(ti)
                                        if f is not None:
                                            V.wait_ge(f[0].h, f[1])
                                    inst = V.tensor_tensor(
                                        t1b[ti][:, :], pb[bank][:, :],
                                        rstd_bcast[:, t * TT:(t + 1) * TT],
                                        ALU.mult)
                                    t1_done = s_ve.inc(inst)
                                    st2["bankfree"][bank] = (s_ve, t1_done)
                                    A.wait_ge(s_ve.h, t1_done)
                                    dmai(A, t1sb[ti][0:64, :],
                                         t1b[ti][64:128, :])
                                    sw = dmai(A, t1sb[ti][64:128, :],
                                              t1b[ti][0:64, :])
                                    cchunk = bias0 + 4 * si + m
                                    gc = (t * TT) % (S // 2)
                                    V.wait_ge(s_a.h, sw)
                                    V.wait_ge(s_a.h, rope_ld)
                                    inst = V.scalar_tensor_tensor(
                                        r1b[:, :], t1b[ti][:, :],
                                        bqk_sb[:, cchunk:cchunk + 1],
                                        crs[:, gc:gc + TT],
                                        ALU.add, ALU.mult)
                                    fence(V, inst)
                                    st2["t1free"][ti] = (s_ve, s_ve.n)
                                    inst = V.scalar_tensor_tensor(
                                        t1sb[ti][:, :], t1sb[ti][:, :],
                                        bqks_sb[:, cchunk:cchunk + 1],
                                        sgs[:, gc:gc + TT],
                                        ALU.add, ALU.mult)
                                    fence(V, inst)
                                    inst = V.tensor_tensor(outb[ti][:, :],
                                                           r1b[:, :],
                                                           t1sb[ti][:, :],
                                                           ALU.add)
                                    rp_done = s_ve.inc(inst)
                                    st2["t1sfree"][ti] = (s_ve, rp_done)
                                    G.wait_ge(s_ve.h, rp_done)
                                    r0 = si * SLAB + m * 128
                                    stv = dmai(G, out_scr[r0:r0 + 128,
                                                          t * TT:(t + 1) * TT],
                                               outb[ti][:, :])
                                    st2["outfree"][ti] = (s_g, stv)
                                    nf = st2["nsqfree"].get(ti)
                                    if nf is not None:
                                        A.wait_ge(nf[0].h, nf[1])
                                    A.wait_ge(s_ve.h, rp_done)
                                    inst = A.activation(nsq[ti][:, :],
                                                        outb[ti][:, :],
                                                        AF.Square)
                                    sq_done = s_ac.inc(inst)
                                    flush_norm()
                                    st2["nrm_pend"] = (ti, sq_done, ncell)
                                st2["xfree"][xi] = (s_pe, s_pe.n)
                            st2["wfree"][wi] = (s_pe, s_pe.n)

                    qk_slabs(0, QCOLS // SLAB, xbfo, NT_OWN, negmo_bf,
                             rstdo_bc, qs, 0, lambda t: ("own", 0), 0)
                    qk_slabs(QCOLS, KCOLS // SLAB, xbf, NT_FULL, negm_bf,
                             rstd_bc, ks, 32,
                             lambda t: ("full", (t // 4) * (S // 2)), 1)

                    for si in range(VCOLS // SLAB):
                        c0 = QCOLS + KCOLS + si * SLAB
                        w_ld, wi = load_slab(c0)
                        cs_done = colsums(wi, w_ld, colsum_v_bf)
                        for t in range(NT_FULL):
                            x_ld, xi = load_xtile(xbf, t)
                            for m in range(4):
                                bank = st2["mm"] % 4
                                st2["mm"] += 1
                                T.wait_ge(s_sy.h, x_ld)
                                T.wait_ge(s_g.h, w_ld)
                                T.wait_ge(s_ac.h, cs_done)
                                bf_ = st2["bankfree"].get(bank)
                                if bf_ is not None:
                                    T.wait_ge(bf_[0].h, bf_[1])
                                for k in range(NKC):
                                    T.matmul(
                                        pb[bank][:, :],
                                        xtile[xi][:, k * TT + m * 128:
                                                  k * TT + (m + 1) * 128],
                                        wbuf[wi][:, k * SLAB:(k + 1) * SLAB],
                                        start=(k == 0), stop=False)
                                tok0 = t * TT + m * 128
                                T.matmul(pb[bank][:, :],
                                         negm_bf[0:1, tok0:tok0 + 128],
                                         colsum_v_bf[0:1, :],
                                         start=False, stop=False)
                                inst = T.matmul(
                                    pb[bank][:, :], ones_row_bf[0:1, :],
                                    bv_sb[0:1, si * SLAB:(si + 1) * SLAB],
                                    start=False, stop=True)
                                mm_done = s_pe.inc(inst)
                                ti = st2["ep"] % 2
                                st2["ep"] += 1
                                V.wait_ge(s_pe.h, mm_done)
                                of = st2["outfree"].get(ti)
                                if of is not None:
                                    V.wait_ge(of[0].h, of[1])
                                cc = 4 * t + m
                                inst = V.tensor_scalar(outb[ti][:, :],
                                                       pb[bank][:, :],
                                                       rstd_col[:, cc:cc + 1],
                                                       None, ALU.mult)
                                ep_done = s_ve.inc(inst)
                                st2["bankfree"][bank] = (s_ve, ep_done)
                                G.wait_ge(s_ve.h, ep_done)
                                stv = dmai(G, vs[tok0:tok0 + 128,
                                                 si * SLAB:(si + 1) * SLAB],
                                           outb[ti][:, :])
                                st2["outfree"][ti] = (s_g, stv)
                            st2["xfree"][xi] = (s_pe, s_pe.n)
                        st2["wfree"][wi] = (s_pe, s_pe.n)
                    flush_norm()
                    barrier()

        if _STOP_AFTER == "qkv":
            _finalize(nc, ALLE, SY, s_st)
            return nc

        with contextlib.ExitStack() as c3:
            E3 = c3.enter_context
            k_sb = [E3(nc.sbuf_tensor(f"at_k{i}", [128, S], BF16))
                    for i in range(2)]
            v_sb = [E3(nc.sbuf_tensor(f"at_v{i}", [128, 32 * 128], BF16))
                    for i in range(2)]
            q_sb = [E3(nc.sbuf_tensor(f"at_q{i}", [128, S // 2], BF16))
                    for i in range(2)]
            probs = [E3(nc.sbuf_tensor(f"at_p{i}", [128, TT], BF16))
                     for i in range(4)]
            probm = [E3(nc.sbuf_tensor(f"at_pm{i}", [128, TT], BF16))
                     for i in range(4)]
            mask_bf = E3(nc.sbuf_tensor("at_mb", [128, 32 * TT], BF16))
            gbias = E3(nc.sbuf_tensor("at_gb", [128, 1], F32))
            grow = E3(nc.sbuf_tensor("at_gr", [1, 8], F32))
            dent = [E3(nc.sbuf_tensor(f"at_dn{i}", [1, 2 * TT], F32))
                    for i in range(2)]
            denb = [E3(nc.sbuf_tensor(f"at_db{i}", [1, TT], BF16))
                    for i in range(2)]
            rdbc = [E3(nc.sbuf_tensor(f"at_rb{i}", [128, TT], F32))
                    for i in range(2)]
            ao_sb = [E3(nc.sbuf_tensor(f"at_ao{i}", [128, TT], BF16))
                     for i in range(2)]

            with nc.body("p3pre", valid_engines=ALLE):
                for j in range(4):
                    dmai(G, mask_bf[:, j * 8 * TT:(j + 1) * 8 * TT],
                         msk_d[:, j * 8 * TT:(j + 1) * 8 * TT])
                inst = V.tensor_tensor(grow[0:1, 0:1], nmax[0:1, 0:1],
                                       nmax[0:1, 1:2], ALU.mult)
                fence(V, inst)
                A.wait_ge(s_ve.h, s_ve.n)
                inst = A.activation(grow[0:1, 1:2], grow[0:1, 0:1], AF.Sqrt)
                fence(A, inst)
                V.wait_ge(s_ac.h, s_ac.n)
                inst = V.tensor_scalar(grow[0:1, 2:3], grow[0:1, 1:2], 2.0,
                                       -INV_SQRT_HD, ALU.add, ALU.mult)
                gb_done = s_ve.inc(inst)
                T.wait_ge(s_ve.h, gb_done)
                inst = T.matmul(pb[0][:, 0:1], ones_row_f[0:1, :],
                                grow[0:1, 2:3], start=True, stop=True)
                gb_bc = s_pe.inc(inst)
                A.wait_ge(s_pe.h, gb_bc)
                inst = A.copy(gbias[:, :], pb[0][:, 0:1])
                s_ac.inc(inst)
                barrier()

            def attn_body():
                lim = _ATTN_LIMIT or (NKV, 4, NT_OWN)
                st3 = {"bankfree": {}, "probfree": {}, "pvfree": {},
                       "denfree": {}, "aofree": {}, "rdfree": {},
                       "dentfree": {}, "denbfree": {}, "pending": None}
                pairs = [(kvh, qi) for kvh in range(lim[0])
                         for qi in range(lim[1])]
                k_ld = {}
                v_ld = {}
                q_ld = {}
                pe_end = {}
                q_end = {}

                def load_kv(j):
                    if j >= lim[0]:
                        return
                    bi = j % 2
                    if j >= 2:
                        SY.wait_ge(s_pe.h, pe_end[j - 2])
                    k_ld[j] = dmai(SY, k_sb[bi][:, :],
                                   ks[j * 128:(j + 1) * 128, :])
                    v_ld[j] = dmai(SY, v_sb[bi][:, :],
                                   vs[:, j * 128:(j + 1) * 128]
                                   .rearrange("(c p) h -> p c h", p=128))

                def load_q(j):
                    if j >= len(pairs):
                        return
                    kvh_, qi_ = pairs[j]
                    qh_ = kvh_ * 4 + qi_
                    bi = j % 2
                    if j >= 2:
                        SY.wait_ge(s_pe.h, q_end[j - 2])
                    q_ld[j] = dmai(SY, q_sb[bi][:, :],
                                   qs[qh_ * 128:(qh_ + 1) * 128, :])

                def mk_epi(ti, P, D, pv_last, qh, t):
                    def epi():
                        ei = ti % 2
                        V.wait_ge(s_pe.h, pv_last)
                        df = st3["dentfree"].get(ei)
                        if df is not None:
                            V.wait_ge(df[0].h, df[1])
                        inst = V.tensor_copy(dent[ei][0:1, 0:TT],
                                             pb[D][0:1, :])
                        fence(V, inst)
                        inst = V.reciprocal(dent[ei][0:1, TT:2 * TT],
                                            dent[ei][0:1, 0:TT])
                        fence(V, inst)
                        dbf = st3["denbfree"].get(ei)
                        if dbf is not None:
                            V.wait_ge(dbf[0].h, dbf[1])
                        inst = V.tensor_copy(denb[ei][0:1, :],
                                             dent[ei][0:1, TT:2 * TT])
                        rdb = s_ve.inc(inst)
                        st3["dentfree"][ei] = (s_ve, rdb)
                        T.wait_ge(s_ve.h, rdb)
                        inst = T.matmul(pb[D][:, :], ones_row_bf[0:1, :],
                                        denb[ei][0:1, :], start=True,
                                        stop=True)
                        bc_done = s_pe.inc(inst)
                        st3["denbfree"][ei] = (s_pe, bc_done)
                        A.wait_ge(s_pe.h, bc_done)
                        rf = st3["rdfree"].get(ei)
                        if rf is not None:
                            A.wait_ge(rf[0].h, rf[1])
                        inst = A.copy(rdbc[ei][:, :], pb[D][:, :])
                        rbc_done = s_ac.inc(inst)
                        st3["denfree"][D] = (s_ac, rbc_done)
                        V.wait_ge(s_ac.h, rbc_done)
                        af = st3["aofree"].get(ei)
                        if af is not None:
                            V.wait_ge(af[0].h, af[1])
                        inst = V.tensor_tensor(ao_sb[ei][:, :], pb[P][:, :],
                                               rdbc[ei][:, :], ALU.mult)
                        ao_done = s_ve.inc(inst)
                        st3["pvfree"][P] = (s_ve, ao_done)
                        st3["rdfree"][ei] = (s_ve, ao_done)
                        G.wait_ge(s_ve.h, ao_done)
                        stv = dmai(G, aos[qh * 128:(qh + 1) * 128,
                                          t * TT:(t + 1) * TT],
                                   ao_sb[ei][:, :])
                        st3["aofree"][ei] = (s_g, stv)
                    return epi

                load_kv(0)
                load_kv(1)
                load_q(0)
                load_q(1)

                for idx, (kvh, qi) in enumerate(pairs):
                    qh = kvh * 4 + qi
                    if qi == 0 and kvh >= 1:
                        load_kv(kvh + 1)
                    if idx >= 1:
                        load_q(idx + 1)
                    for t in range(lim[2]):
                        ti = idx * NT_OWN + t
                        P = 4 + ti % 2
                        D = 6 + ti % 2
                        nk = 4 * (2 * t + 2)
                        sc_done = {}
                        use = {}
                        pm = {}

                        def emit_score(c):
                            sb = c % 4
                            bf_ = st3["bankfree"].get(sb)
                            if bf_ is not None:
                                T.wait_ge(bf_[0].h, bf_[1])
                            T.wait_ge(s_sy.h, max(k_ld[kvh], q_ld[idx]))
                            inst = T.matmul(
                                pb[sb][:, :],
                                k_sb[kvh % 2][:, c * 128:(c + 1) * 128],
                                q_sb[idx % 2][:, t * TT:(t + 1) * TT],
                                start=True, stop=True)
                            sc_done[c] = s_pe.inc(inst)

                        def emit_exp(c):
                            sb = c % 4
                            A.wait_ge(s_pe.h, sc_done[c])
                            pf = st3["probfree"].get(sb)
                            if pf is not None:
                                A.wait_ge(pf[0].h, pf[1])
                            inst = A.activation(probs[sb][:, :], pb[sb][:, :],
                                                AF.Exp, bias=gbias[:, 0:1],
                                                scale=INV_SQRT_HD)
                            e_done = s_ac.inc(inst)
                            st3["bankfree"][sb] = (s_ac, e_done)
                            d_ = c - (nk - 8)
                            if d_ >= 0:
                                mcol = (t * 8 + d_) * TT
                                V.wait_ge(s_ac.h, e_done)
                                if pf is not None:
                                    V.wait_ge(pf[0].h, pf[1])
                                inst = V.tensor_tensor(
                                    probm[sb][:, :], probs[sb][:, :],
                                    mask_bf[:, mcol:mcol + TT], ALU.mult)
                                use[c] = (s_ve, s_ve.inc(inst))
                                pm[c] = probm[sb]
                            else:
                                use[c] = (s_ac, e_done)
                                pm[c] = probs[sb]

                        def emit_pv(c):
                            sb = c % 4
                            T.wait_ge(use[c][0].h, use[c][1])
                            if c == 0:
                                pvf = st3["pvfree"].get(P)
                                if pvf is not None:
                                    T.wait_ge(pvf[0].h, pvf[1])
                                dnf = st3["denfree"].get(D)
                                if dnf is not None:
                                    T.wait_ge(dnf[0].h, dnf[1])
                                T.wait_ge(s_sy.h, v_ld[kvh])
                            T.matmul(pb[P][:, :],
                                     v_sb[kvh % 2][:, c * 128:(c + 1) * 128],
                                     pm[c][:, :], start=(c == 0),
                                     stop=(c == nk - 1),
                                     skip_group_check=True)
                            inst = T.matmul(pb[D][0:1, :], ones_col_bf[:, :],
                                            pm[c][:, :], start=(c == 0),
                                            stop=(c == nk - 1),
                                            skip_group_check=True)
                            pvd = s_pe.inc(inst)
                            st3["probfree"][sb] = (s_pe, pvd)
                            return pvd

                        pv_last = None
                        for c in range(nk):
                            emit_score(c)
                            if c == 2 and st3["pending"] is not None:
                                st3["pending"]()
                                st3["pending"] = None
                            if c >= 1:
                                emit_exp(c - 1)
                            if c >= 2:
                                pv_last = emit_pv(c - 2)
                        emit_exp(nk - 1)
                        pv_last = emit_pv(nk - 2)
                        pv_last = emit_pv(nk - 1)
                        st3["pending"] = mk_epi(ti, P, D, pv_last, qh, t)
                    q_end[idx] = s_pe.n
                    if qi == lim[1] - 1:
                        pe_end[kvh] = s_pe.n
                if st3["pending"] is not None:
                    st3["pending"]()
                    st3["pending"] = None

            with nc.body("p3main", valid_engines=ALLE):
                attn_body()
                barrier()

        if _STOP_AFTER == "attn":
            _finalize(nc, ALLE, SY, s_st)
            return nc

        with nc.body("p4", valid_engines=ALLE):
            with contextlib.ExitStack() as c4:
                E4 = c4.enter_context
                wbuf = [E4(nc.sbuf_tensor(f"op_w{i}", [128, NKC * SLAB],
                                          BF16)) for i in range(2)]
                atile = [E4(nc.sbuf_tensor(f"op_a{i}", [128, NKC * TT],
                                           BF16)) for i in range(2)]
                xres = [E4(nc.sbuf_tensor(f"op_x{i}", [128, 4 * TT], F32))
                        for i in range(2)]
                t1 = E4(nc.sbuf_tensor("op_t1", [128, TT], F32))
                x2o = [E4(nc.sbuf_tensor(f"op_o{i}", [128, TT], F32))
                       for i in range(2)]
                st4 = {"wfree": {}, "afree": {}, "xfree": {}, "bankfree": {},
                       "ofree": {}}
                for si in range(D // SLAB):
                    wi = si % 2
                    wf = st4["wfree"].get(wi)
                    if wf is not None:
                        G.wait_ge(wf[0].h, wf[1])
                    w_ld = dmai(G, wbuf[wi][:, :]
                                .rearrange("p (c n) -> p c n", c=NKC),
                                woT[:, si * SLAB:(si + 1) * SLAB]
                                .rearrange("(c p) n -> p c n", p=128))
                    for t in range(NT_OWN):
                        it = si * NT_OWN + t
                        ai = it % 2
                        af = st4["afree"].get(ai)
                        if af is not None:
                            SY.wait_ge(af[0].h, af[1])
                        a_ld = dmai(SY, atile[ai][:, :],
                                    aos[:, t * TT:(t + 1) * TT]
                                    .rearrange("(c p) t -> p c t", p=128))
                        xf = st4["xfree"].get(ai)
                        if xf is not None:
                            A.wait_ge(xf[0].h, xf[1])
                        x_ld = dmai(A, xres[ai][:, :]
                                    .rearrange("p (c t) -> p c t", c=4),
                                    xTo[si * SLAB:(si + 1) * SLAB,
                                        t * TT:(t + 1) * TT]
                                    .rearrange("(c p) t -> p c t", p=128))
                        for m in range(4):
                            bank = (it * 4 + m) % 4
                            bf_ = st4["bankfree"].get(bank)
                            T.wait_ge(s_g.h, w_ld)
                            T.wait_ge(s_sy.h, a_ld)
                            if bf_ is not None:
                                T.wait_ge(bf_[0].h, bf_[1])
                            for k in range(NKC):
                                inst = T.matmul(
                                    pb[bank][:, :],
                                    wbuf[wi][:, k * SLAB + m * 128:
                                             k * SLAB + (m + 1) * 128],
                                    atile[ai][:, k * TT:(k + 1) * TT],
                                    start=(k == 0), stop=(k == NKC - 1))
                            mm_done = s_pe.inc(inst)
                            V.wait_ge(s_pe.h, mm_done)
                            V.wait_ge(s_a.h, x_ld)
                            oc = si * 4 + m
                            inst = V.tensor_scalar(t1[:, :], pb[bank][:, :],
                                                   bo_sb[:, oc:oc + 1], None,
                                                   ALU.add)
                            fence(V, inst)
                            st4["bankfree"][bank] = (s_ve, s_ve.n)
                            oi = (it * 4 + m) % 2
                            of = st4["ofree"].get(oi)
                            if of is not None:
                                V.wait_ge(of[0].h, of[1])
                            inst = V.tensor_tensor(
                                x2o[oi][:, :], t1[:, :],
                                xres[ai][:, m * TT:(m + 1) * TT], ALU.add)
                            x2_done = s_ve.inc(inst)
                            G.wait_ge(s_ve.h, x2_done)
                            r0 = si * SLAB + m * 128
                            dmai(G, x2s[r0:r0 + 128, t * TT:(t + 1) * TT],
                                 x2o[oi][:, :])
                            stv = dmai(G, x2bf[r0:r0 + 128,
                                               t * TT:(t + 1) * TT],
                                       x2o[oi][:, :])
                            st4["ofree"][oi] = (s_g, stv)
                        st4["afree"][ai] = (s_pe, s_pe.n)
                        st4["xfree"][ai] = (s_ve, s_ve.n)
                    st4["wfree"][wi] = (s_pe, s_pe.n)
                barrier()

        if _STOP_AFTER == "oproj":
            _finalize(nc, ALLE, SY, s_st)
            return nc

        with contextlib.ExitStack() as c56:
            E56 = c56.enter_context
            rhe = [E56(nc.sbuf_tensor(f"f_rhe{i}", [128, 8 * TT], BF16))
                   for i in range(NT_OWN)]
            negm2_bf = E56(nc.sbuf_tensor("f_negm2", [1, S // 2], BF16))
            rstd2_bc = E56(nc.sbuf_tensor("f_rstd2bc", [128, S // 2], F32))

            with nc.body("p5", valid_engines=ALLE):
                with contextlib.ExitStack() as c5:
                    E5 = c5.enter_context
                    xtile = [E5(nc.sbuf_tensor(f"f5_x{i}", [128, NKC * TT],
                                               BF16)) for i in range(2)]
                    xsq5 = E5(nc.sbuf_tensor("f5_xsq", [128, NKC * TT], BF16))
                    rowt5 = E5(nc.sbuf_tensor("f5_rowt", [1, 4 * TT], F32))
                    rstd2_col = E5(nc.sbuf_tensor("f5_rcol", [128, 16], F32))
                    art_sb = E5(nc.sbuf_tensor("f5_art", [128, NKC * 80],
                                               BF16))
                    csar_bf = E5(nc.sbuf_tensor("f5_csar", [1, 80], BF16))
                    rh2 = [E5(nc.sbuf_tensor(f"f5_rh2{i}", [128, TT], F32))
                           for i in range(2)]
                    lgs = E5(nc.sbuf_tensor("f5_lgs", [16, TT], F32))
                    lt_tm = E5(nc.sbuf_tensor("f5_lt", [128, 64], F32))
                    ptm = E5(nc.sbuf_tensor("f5_ptm", [128, 64], F32))
                    rcol = E5(nc.sbuf_tensor("f5_rc", [128, 12], F32))
                    rtmp = E5(nc.sbuf_tensor("f5_rt", [128, 64], F32))
                    comb_tm = E5(nc.sbuf_tensor("f5_comb", [128, 64], F32))
                    comb_sb = E5(nc.sbuf_tensor("f5_csb", [16, TT], BF16))
                    comb_row = E5(nc.sbuf_tensor("f5_crow", [1, 16 * TT],
                                                 BF16))

                    art_ld = dmai(G, art_sb[:, :]
                                  .rearrange("p (c n) -> p c n", c=NKC),
                                  art_d[:, :]
                                  .rearrange("(c p) n -> p c n", p=128))
                    T.wait_ge(s_g.h, art_ld)
                    for k in range(NKC):
                        inst = T.matmul(pb[4][0:1, 0:80], ones_col_bf[:, :],
                                        art_sb[:, k * 80:(k + 1) * 80],
                                        start=(k == 0), stop=(k == NKC - 1))
                    csd = s_pe.inc(inst)
                    A.wait_ge(s_pe.h, csd)
                    inst = A.copy(csar_bf[0:1, :], pb[4][0:1, 0:80])
                    csar_done = s_ac.inc(inst)

                    prev5 = {"rows_read": (s_ac, csar_done), "xfree": {},
                             "sq_read": None, "bc_read": None}
                    for t in range(NT_OWN):
                        bi = t % 2
                        xf = prev5["xfree"].get(bi)
                        if xf is not None:
                            SY.wait_ge(xf[0].h, xf[1])
                        ld = dmai(SY, xtile[bi][:, :],
                                  x2bf[:, t * TT:(t + 1) * TT]
                                  .rearrange("(c p) t -> p c t", p=128))
                        V.wait_ge(s_sy.h, ld)
                        if prev5["sq_read"] is not None:
                            V.wait_ge(prev5["sq_read"][0].h,
                                      prev5["sq_read"][1])
                        inst = V.tensor_tensor(xsq5[:, :], xtile[bi][:, :],
                                               xtile[bi][:, :], ALU.mult)
                        sq_done = s_ve.inc(inst)
                        T.wait_ge(s_sy.h, ld)
                        T.wait_ge(prev5["rows_read"][0].h,
                                  prev5["rows_read"][1])
                        for k in range(NKC):
                            inst = T.matmul(pb[4][0:1, :], ones_col_bf[:, :],
                                            xtile[bi][:, k * TT:(k + 1) * TT],
                                            start=(k == 0),
                                            stop=(k == NKC - 1))
                        T.wait_ge(s_ve.h, sq_done)
                        for k in range(NKC):
                            inst = T.matmul(pb[5][0:1, :], ones_col_bf[:, :],
                                            xsq5[:, k * TT:(k + 1) * TT],
                                            start=(k == 0),
                                            stop=(k == NKC - 1))
                        sums_done = s_pe.inc(inst)
                        prev5["sq_read"] = (s_pe, sums_done)
                        prev5["xfree"][bi] = (s_pe, sums_done)
                        V.wait_ge(s_pe.h, sums_done)
                        inst = V.tensor_scalar(
                            negm2_bf[0:1, t * TT:(t + 1) * TT],
                            pb[4][0:1, :], -1.0 / D, None, ALU.mult)
                        fence(V, inst)
                        inst = V.tensor_scalar(rowt5[0:1, 0:TT],
                                               pb[5][0:1, :], 1.0 / D, None,
                                               ALU.mult)
                        fence(V, inst)
                        prev5["rows_read"] = (s_ve, s_ve.n)
                        inst = V.tensor_tensor(
                            rowt5[0:1, TT:2 * TT],
                            negm2_bf[0:1, t * TT:(t + 1) * TT],
                            negm2_bf[0:1, t * TT:(t + 1) * TT], ALU.mult)
                        fence(V, inst)
                        inst = V.tensor_tensor(rowt5[0:1, 2 * TT:3 * TT],
                                               rowt5[0:1, 0:TT],
                                               rowt5[0:1, TT:2 * TT],
                                               ALU.subtract)
                        var_done = s_ve.inc(inst)
                        A.wait_ge(s_ve.h, var_done)
                        inst = A.activation(rowt5[0:1, 3 * TT:4 * TT],
                                            rowt5[0:1, 2 * TT:3 * TT], AF.Ln,
                                            bias=eps_sb[0:1, 0:1], scale=1.0)
                        fence(A, inst)
                        inst = A.activation(rowt5[0:1, 0:TT],
                                            rowt5[0:1, 3 * TT:4 * TT], AF.Exp,
                                            scale=-0.5)
                        rstd_done = s_ac.inc(inst)
                        T.wait_ge(s_ac.h, rstd_done)
                        if prev5["bc_read"] is not None:
                            T.wait_ge(prev5["bc_read"][0].h,
                                      prev5["bc_read"][1])
                        inst = T.matmul(pb[6][:, :], ones_row_f[0:1, :],
                                        rowt5[0:1, 0:TT], start=True,
                                        stop=True)
                        bc_done = s_pe.inc(inst)
                        A.wait_ge(s_pe.h, bc_done)
                        inst = A.copy(rstd2_bc[:, t * TT:(t + 1) * TT],
                                      pb[6][:, :])
                        bc_ev = s_ac.inc(inst)
                        prev5["bc_read"] = (s_ac, bc_ev)
                        A.wait_ge(s_ac.h, bc_ev)
                        for c in range(4):
                            cc = 4 * t + c
                            dmai(A, rstd2_col[:, cc:cc + 1],
                                 rowt5[0:1, c * 128:(c + 1) * 128])
                        V.wait_ge(s_a.h, s_a.n)
                        V.wait_ge(s_ac.h, bc_ev)

                    for t in range(NT_OWN):
                        bi = t % 2
                        xf = prev5["xfree"].get(bi)
                        if xf is not None:
                            SY.wait_ge(xf[0].h, xf[1])
                        ld = dmai(SY, xtile[bi][:, :],
                                  x2bf[:, t * TT:(t + 1) * TT]
                                  .rearrange("(c p) t -> p c t", p=128))
                        ri = t % 2
                        T.wait_ge(s_sy.h, ld)
                        T.wait_ge(s_ve.h, s_ve.n)
                        for k in range(NKC):
                            T.matmul(pb[4][0:64, :],
                                     art_sb[:, k * 80:k * 80 + 64],
                                     xtile[bi][:, k * TT:(k + 1) * TT],
                                     start=(k == 0), stop=False)
                        inst = T.matmul(pb[4][0:64, :], csar_bf[0:1, 0:64],
                                        negm2_bf[0:1, t * TT:(t + 1) * TT],
                                        start=False, stop=True)
                        rh_done = s_pe.inc(inst)
                        for k in range(NKC):
                            T.matmul(pb[5][0:16, :],
                                     art_sb[:, k * 80 + 64:k * 80 + 80],
                                     xtile[bi][:, k * TT:(k + 1) * TT],
                                     start=(k == 0), stop=False)
                        inst = T.matmul(pb[5][0:16, :], csar_bf[0:1, 64:80],
                                        negm2_bf[0:1, t * TT:(t + 1) * TT],
                                        start=False, stop=True)
                        lg_done = s_pe.inc(inst)
                        prev5["xfree"][bi] = (s_pe, lg_done)
                        V.wait_ge(s_pe.h, rh_done)
                        inst = V.tensor_tensor(
                            rh2[ri][0:64, :], pb[4][0:64, :],
                            rstd2_bc[0:64, t * TT:(t + 1) * TT], ALU.mult)
                        rh_ev = s_ve.inc(inst)
                        A.wait_ge(s_ve.h, rh_ev)
                        rh_c2 = dmai(A, rh2[ri][64:128, :], rh2[ri][0:64, :])
                        V.wait_ge(s_pe.h, lg_done)
                        inst = V.tensor_tensor(
                            lgs[0:16, :], pb[5][0:16, :],
                            rstd2_bc[0:16, t * TT:(t + 1) * TT], ALU.mult)
                        lg_ev = s_ve.inc(inst)
                        T.wait_ge(s_ve.h, lg_ev)
                        for j in range(4):
                            inst = T.transpose(pb[6][:, j * 16:(j + 1) * 16],
                                               lgs[0:16,
                                                   j * 128:(j + 1) * 128],
                                               eye_sb[0:16, 0:16])
                        ltr_done = s_pe.inc(inst)
                        A.wait_ge(s_pe.h, ltr_done)
                        inst = A.copy(lt_tm[:, :], pb[6][:, 0:64])
                        lt_ev = s_ac.inc(inst)
                        T.wait_ge(s_ac.h, lt_ev)
                        V.wait_ge(s_ac.h, lt_ev)
                        for j in range(4):
                            lsl = lt_tm[:, j * 16:(j + 1) * 16]
                            psl = ptm[:, j * 16:(j + 1) * 16]
                            csl = comb_tm[:, j * 16:(j + 1) * 16]
                            inst = V.reduce_max(rcol[:, 0:1], lsl, axis=AX)
                            fence(V, inst)
                            inst = V.tensor_scalar(rcol[:, 1:2], rcol[:, 0:1],
                                                   -1.0, None, ALU.mult)
                            fence(V, inst)
                            A.wait_ge(s_ve.h, s_ve.n)
                            inst = A.activation(psl, lsl, AF.Exp,
                                                bias=rcol[:, 1:2], scale=1.0)
                            fence(A, inst)
                            V.wait_ge(s_ac.h, s_ac.n)
                            inst = V.reduce_sum(rcol[:, 2:3], psl, axis=AX)
                            fence(V, inst)
                            inst = V.reduce_max(rcol[:, 3:4], psl, axis=AX)
                            fence(V, inst)
                            inst = V.tensor_scalar(rtmp[:, 0:16], psl,
                                                   rcol[:, 3:4], None,
                                                   ALU.is_ge)
                            fence(V, inst)
                            inst = V.scalar_tensor_tensor(rtmp[:, 16:32],
                                                          rtmp[:, 0:16], 1e30,
                                                          psl, ALU.mult,
                                                          ALU.subtract)
                            fence(V, inst)
                            inst = V.tensor_scalar(rtmp[:, 32:48],
                                                   rtmp[:, 16:32], -1.0,
                                                   None, ALU.mult)
                            fence(V, inst)
                            inst = V.reduce_max(rcol[:, 4:5], rtmp[:, 32:48],
                                                axis=AX)
                            fence(V, inst)
                            inst = V.tensor_scalar(rcol[:, 5:6], rcol[:, 2:3],
                                                   1e-6, None, ALU.mult)
                            fence(V, inst)
                            inst = V.tensor_tensor(rcol[:, 6:7], rcol[:, 3:4],
                                                   rcol[:, 4:5], ALU.add)
                            fence(V, inst)
                            inst = V.tensor_tensor(rcol[:, 7:8], rcol[:, 5:6],
                                                   rcol[:, 6:7], ALU.add)
                            fence(V, inst)
                            inst = V.reciprocal(rcol[:, 8:9], rcol[:, 7:8])
                            fence(V, inst)
                            inst = V.reciprocal(rcol[:, 9:10],
                                                rstd2_col[:, 4 * t + j:
                                                          4 * t + j + 1])
                            fence(V, inst)
                            inst = V.tensor_scalar(rtmp[:, 0:16], psl,
                                                   rcol[:, 4:5], None,
                                                   ALU.is_ge)
                            fence(V, inst)
                            inst = V.tensor_scalar(rtmp[:, 16:32], psl,
                                                   rcol[:, 8:9], 0.5,
                                                   ALU.mult, ALU.mult)
                            fence(V, inst)
                            inst = V.tensor_scalar(rtmp[:, 48:64],
                                                   rtmp[:, 16:32],
                                                   rcol[:, 9:10], None,
                                                   ALU.mult)
                            fence(V, inst)
                            inst = V.tensor_tensor(csl, rtmp[:, 0:16],
                                                   rtmp[:, 48:64], ALU.mult)
                            fence(V, inst)
                        comb_done = s_ve.n
                        T.wait_ge(s_ve.h, comb_done)
                        T.wait_ge(s_ac.h, s_ac.n)
                        for j in range(4):
                            inst = T.transpose(pb[6][0:16,
                                                     j * 128:(j + 1) * 128],
                                               comb_tm[:, j * 16:(j + 1) * 16],
                                               eye_sb[:, :])
                        ctr_done = s_pe.inc(inst)
                        A.wait_ge(s_pe.h, ctr_done)
                        inst = A.copy(comb_sb[0:16, :], pb[6][0:16, 0:TT])
                        cb_ev = s_ac.inc(inst)
                        G.wait_ge(s_ac.h, cb_ev)
                        G.wait_ge(s_pe.h, s_pe.n)
                        for e_ in range(16):
                            crow_ld = dmai(G, comb_row[0:1, e_ * TT:
                                                       (e_ + 1) * TT],
                                           comb_sb[e_:e_ + 1, :])
                        for c in range(8):
                            T.wait_ge(s_g.h, crow_ld)
                            T.wait_ge(s_ve.h, s_ve.n)
                            T.matmul(pb[7][:, :], sel_lo[0:1, :],
                                     comb_row[0:1, (2 * c) * TT:
                                              (2 * c + 1) * TT],
                                     start=True, stop=False)
                            inst = T.matmul(pb[7][:, :], sel_hi[0:1, :],
                                            comb_row[0:1, (2 * c + 1) * TT:
                                                     (2 * c + 2) * TT],
                                            start=False, stop=True)
                            bcp_done = s_pe.inc(inst)
                            V.wait_ge(s_pe.h, bcp_done)
                            V.wait_ge(s_a.h, rh_c2)
                            inst = V.tensor_tensor(
                                rhe[t][:, c * TT:(c + 1) * TT], pb[7][:, :],
                                rh2[ri][:, :], ALU.mult)
                            s_ve.inc(inst)
                    barrier()

            if _STOP_AFTER == "router":
                _finalize(nc, ALLE, SY, s_st)
                return nc

            with nc.body("p6", valid_engines=ALLE):
                with contextlib.ExitStack() as c6:
                    E6 = c6.enter_context
                    NCH6 = SLAB6 // 128
                    wbuf = [E6(nc.sbuf_tensor(f"f6_w{i}", [128, NKC * SLAB6],
                                              BF16)) for i in range(2)]
                    bfbuf = [E6(nc.sbuf_tensor(f"f6_b{i}", [128, 8 * SLAB6],
                                               BF16)) for i in range(2)]
                    xtile6 = [E6(nc.sbuf_tensor(f"f6_x{i}", [128, NKC * TT],
                                                BF16)) for i in range(2)]
                    x2res = [E6(nc.sbuf_tensor(f"f6_r{i}",
                                               [128, NCH6 * TT], F32))
                             for i in range(2)]
                    t16 = E6(nc.sbuf_tensor("f6_t1", [128, TT], F32))
                    ob6 = [E6(nc.sbuf_tensor(f"f6_o{i}", [128, TT], F32))
                           for i in range(2)]
                    cs6 = E6(nc.sbuf_tensor("f6_cs", [1, SLAB6], BF16))
                    st6 = {"wfree": {}, "xfree": {}, "rfree": {},
                           "bankfree": {}, "ofree": {}, "cs_read": None}
                    for si in range(D // SLAB6):
                        wi = si % 2
                        wf = st6["wfree"].get(wi)
                        if wf is not None:
                            G.wait_ge(wf[0].h, wf[1])
                        w_ld = dmai(G, wbuf[wi][:, :]
                                    .rearrange("p (c n) -> p c n", c=NKC),
                                    bwT[:, si * SLAB6:(si + 1) * SLAB6]
                                    .rearrange("(c p) n -> p c n", p=128))
                        b_ld = dmai(G, bfbuf[wi][:, :]
                                    .rearrange("p (c n) -> p c n", c=8),
                                    bfm_d[:, si * SLAB6:(si + 1) * SLAB6]
                                    .rearrange("(c p) n -> p c n", p=128))
                        T.wait_ge(s_g.h, w_ld)
                        if st6["cs_read"] is not None:
                            T.wait_ge(st6["cs_read"][0].h,
                                      st6["cs_read"][1])
                        for k in range(NKC):
                            inst = T.matmul(
                                pb[4][0:1, 0:SLAB6], ones_col_bf[:, :],
                                wbuf[wi][:, k * SLAB6:(k + 1) * SLAB6],
                                start=(k == 0), stop=(k == NKC - 1))
                        csd = s_pe.inc(inst)
                        A.wait_ge(s_pe.h, csd)
                        inst = A.copy(cs6[0:1, :], pb[4][0:1, 0:SLAB6])
                        cs_done = s_ac.inc(inst)
                        st6["cs_read"] = (s_ac, cs_done)
                        for t in range(NT_OWN):
                            it = si * NT_OWN + t
                            xi = it % 2
                            xf = st6["xfree"].get(xi)
                            if xf is not None:
                                SY.wait_ge(xf[0].h, xf[1])
                            x_ld = dmai(SY, xtile6[xi][:, :],
                                        x2bf[:, t * TT:(t + 1) * TT]
                                        .rearrange("(c p) t -> p c t", p=128))
                            rf = st6["rfree"].get(xi)
                            if rf is not None:
                                A.wait_ge(rf[0].h, rf[1])
                            r_ld = dmai(A, x2res[xi][:, :]
                                        .rearrange("p (c t) -> p c t",
                                                   c=NCH6),
                                        x2s[si * SLAB6:(si + 1) * SLAB6,
                                            t * TT:(t + 1) * TT]
                                        .rearrange("(c p) t -> p c t", p=128))
                            for m in range(NCH6):
                                bank = (it * NCH6 + m) % 4
                                bf_ = st6["bankfree"].get(bank)
                                T.wait_ge(s_g.h, b_ld)
                                T.wait_ge(s_sy.h, x_ld)
                                T.wait_ge(s_ac.h, cs_done)
                                if bf_ is not None:
                                    T.wait_ge(bf_[0].h, bf_[1])
                                for k in range(NKC):
                                    T.matmul(
                                        pb[bank][:, :],
                                        wbuf[wi][:, k * SLAB6 + m * 128:
                                                 k * SLAB6 + (m + 1) * 128],
                                        xtile6[xi][:, k * TT:(k + 1) * TT],
                                        start=(k == 0), stop=False)
                                for c in range(8):
                                    T.matmul(
                                        pb[bank][:, :],
                                        bfbuf[wi][:, c * SLAB6 + m * 128:
                                                  c * SLAB6 + (m + 1) * 128],
                                        rhe[t][:, c * TT:(c + 1) * TT],
                                        start=False, stop=False)
                                inst = T.matmul(
                                    pb[bank][:, :],
                                    cs6[0:1, m * 128:(m + 1) * 128],
                                    negm2_bf[0:1, t * TT:(t + 1) * TT],
                                    start=False, stop=True)
                                mm_done = s_pe.inc(inst)
                                V.wait_ge(s_pe.h, mm_done)
                                V.wait_ge(s_a.h, r_ld)
                                inst = V.tensor_tensor(
                                    t16[:, :], pb[bank][:, :],
                                    rstd2_bc[:, t * TT:(t + 1) * TT],
                                    ALU.mult)
                                fence(V, inst)
                                st6["bankfree"][bank] = (s_ve, s_ve.n)
                                oi = (it * NCH6 + m) % 2
                                of = st6["ofree"].get(oi)
                                if of is not None:
                                    V.wait_ge(of[0].h, of[1])
                                oc = (si * SLAB6) // 128 + m
                                inst = V.scalar_tensor_tensor(
                                    ob6[oi][:, :], t16[:, :],
                                    bb_sb[:, oc:oc + 1],
                                    x2res[xi][:, m * TT:(m + 1) * TT],
                                    ALU.add, ALU.add)
                                o_done = s_ve.inc(inst)
                                SY.wait_ge(s_ve.h, o_done)
                                r0 = si * SLAB6 + m * 128
                                inst = SY.dma_start(
                                    out_d[r0:r0 + 128, t * TT:(t + 1) * TT],
                                    ob6[oi][:, :])
                                s_st.inc(inst, 16)
                                st6["ofree"][oi] = (s_st, s_st.n)
                            st6["xfree"][xi] = (s_pe, s_pe.n)
                            st6["rfree"][xi] = (s_ve, s_ve.n)
                        st6["wfree"][wi] = (s_pe, s_pe.n)

            _finalize(nc, ALLE, SY, s_st)
    return nc


def _finalize(nc, ALLE, SY, s_st):
    with nc.body("fin", valid_engines=ALLE):
        if s_st.n > 0:
            SY.wait_ge(s_st.h, s_st.n)
        else:
            SY.nop()



def _deinterleave_perm():
    per_head = np.concatenate([np.arange(0, HD, 2), np.arange(1, HD, 2)])
    perm = np.arange(QCOLS + KCOLS + VCOLS)
    for h in range(NH):
        perm[h * HD:(h + 1) * HD] = h * HD + per_head
    for h in range(NKV):
        base = QCOLS + h * HD
        perm[base:base + HD] = base + per_head
    return perm


def _rope_tables(cols):
    inv = 10000.0 ** (-(np.arange(64, dtype=np.float64)) / 64.0)
    ang = cols[None, :].astype(np.float64) * inv[:, None]
    c = np.cos(ang)
    s = np.sin(ang)
    cr = np.ascontiguousarray(np.concatenate([c, c], axis=0)
                              .astype(np.float32))
    sg = np.ascontiguousarray(np.concatenate([-s, s], axis=0)
                              .astype(np.float32))
    return cr, sg


def _own_cols(parity):
    return np.concatenate([np.arange(g * TT, (g + 1) * TT)
                           for g in G_LISTS[parity]])


def _fm_chunks(v, n):
    return np.ascontiguousarray(v.reshape(n, 128).T)


def _prepare_in_maps(L0_attn_x, L0_attn_w_qkv, L0_attn_b_qkv, L0_attn_w_o,
                     L0_attn_b_o, L0_ffn_base_w, L0_ffn_base_b, L0_ffn_A,
                     L0_ffn_B, L0_ffn_router_w):
    x = np.ascontiguousarray(np.asarray(L0_attn_x, dtype=np.float32))
    w_qkv = np.asarray(L0_attn_w_qkv, dtype=np.float32)
    b_qkv = np.asarray(L0_attn_b_qkv, dtype=np.float32)
    w_o = np.asarray(L0_attn_w_o, dtype=np.float32)
    b_o = np.asarray(L0_attn_b_o, dtype=np.float32)
    base_w = np.asarray(L0_ffn_base_w, dtype=np.float32)
    base_b = np.asarray(L0_ffn_base_b, dtype=np.float32)
    A_ = np.asarray(L0_ffn_A, dtype=np.float32)
    B_ = np.asarray(L0_ffn_B, dtype=np.float32)
    router_w = np.asarray(L0_ffn_router_w, dtype=np.float32)

    perm = _deinterleave_perm()
    wq_p = w_qkv[perm]
    b_p = b_qkv[perm]
    wqkvT = np.ascontiguousarray(wq_p.T)
    bqk = _fm_chunks(b_p[:QCOLS + KCOLS], 40)
    bqks = np.ascontiguousarray(
        np.concatenate([bqk[64:128], bqk[0:64]], axis=0))
    bvrow = np.ascontiguousarray(b_p[QCOLS + KCOLS:][None, :])
    woT = np.ascontiguousarray(w_o.T)
    bo = _fm_chunks(b_o, 32)
    bwT = np.ascontiguousarray(base_w.T)
    bb = _fm_chunks(base_b, 32)
    art = np.ascontiguousarray(np.concatenate([A_.T, router_w.T], axis=1))
    bfm = np.ascontiguousarray(B_.transpose(0, 2, 1).reshape(16 * 64, D))
    crf, sgf = _rope_tables(np.arange(S))

    pp = np.arange(128)[:, None]
    ff = np.arange(TT)[None, :]
    msks = []
    for p in range(2):
        m = np.zeros((128, 32 * TT), dtype=np.float32)
        for t in range(NT_OWN):
            g = G_LISTS[p][t]
            nk = 4 * (2 * t + 2)
            for j in range(8):
                c = nk - 8 + j
                m[:, (t * 8 + j) * TT:(t * 8 + j + 1) * TT] = (
                    (512 * g + ff) >= (128 * c + pp)).astype(np.float32)
        msks.append(m)

    in_maps = []
    for core in range(8):
        b = core // 2
        p = core % 2
        cols = _own_cols(p)
        xTa = np.ascontiguousarray(x[b].T)
        xToa = np.ascontiguousarray(xTa[:, cols])
        cro, sgo = _rope_tables(cols)
        in_maps.append({
            "xT": xTa, "xTo": xToa, "wqkvT": wqkvT, "bqk": bqk, "bqks": bqks,
            "bvrow": bvrow, "woT": woT, "bo": bo, "bwT": bwT, "bb": bb,
            "art": art, "bfm": bfm, "cro": cro, "sgo": sgo,
            "crf": crf, "sgf": sgf, "msk": msks[p],
        })

    return in_maps


def kernel(**inputs):
    in_maps = _prepare_in_maps(**inputs)
    nc = build_program()
    res = run_bass_kernel_spmd(nc, in_maps, core_ids=list(range(8)))

    out = np.empty((B, S, D), dtype=np.float32)
    for core in range(8):
        b = core // 2
        p = core % 2
        cols = _own_cols(p)
        out[b, cols, :] = res.results[core]["out"].T
    return out


if __name__ == "__main__":
    import reference
    inputs = reference.setup_inputs()
    got = kernel(**{k: np.asarray(v) for k, v in inputs.items()})
    want = np.asarray(reference.reference(**inputs))
    err = np.sqrt(((got - want) ** 2).mean() / (want ** 2).mean())
    print("rel l2:", err)


# version 12
# speedup vs baseline: 1.5767x; 1.4625x over previous
import contextlib
import math

import numpy as np
import concourse.bass as bass
import concourse.mybir as mybir
from concourse.bass_utils import run_bass_kernel_spmd

F32 = mybir.dt.float32
BF16 = mybir.dt.bfloat16
I32 = mybir.dt.int32
AF = mybir.ActivationFunctionType
ALU = mybir.AluOpType
AX = mybir.AxisListType.X

D = 4096
S = 4096
B = 4
NH = 32
NKV = 8
HD = 128
QCOLS = 4096
KCOLS = 1024
VCOLS = 1024
TT = 512
NT_FULL = S // TT
NT_OWN = NT_FULL // 2
NKC = D // 128
SLAB = 512
SLAB6 = 256
G_LISTS = [[0, 3, 4, 7], [1, 2, 5, 6]]
INV_SQRT_HD = 1.0 / math.sqrt(HD)
LN_EPS = 1e-5

_DEBUG_TAPS = []
_STOP_AFTER = None
_FORCE_PARITY = None
_ATTN_LIMIT = None


class Sem:
    def __init__(self, h):
        self.h = h
        self.n = 0

    def inc(self, inst, k=1):
        inst.then_inc(self.h, k)
        self.n += k
        return self.n


def build_program():
    nc = bass.Bass("TRN2", target_bir_lowering=False)
    ALLE = list(mybir.ALL_ENGINES)

    def din(name, shape):
        return nc.dram_tensor(name, shape, F32, kind="ExternalInput")

    xTo = din("xTo", [D, S // 2])
    wqkvT = din("wqkvT", [12 * 128, NKC * SLAB])
    bqk_d = din("bqk", [128, 40])
    bqks_d = din("bqks", [128, 40])
    bvrow_d = din("bvrow", [1, VCOLS])
    woT = din("woT", [8 * 128, NKC * SLAB])
    bo_d = din("bo", [128, 32])
    bwT = din("bwT", [16 * 128, NKC * SLAB6])
    bb_d = din("bb", [128, 32])
    art_d = din("art", [128, NKC * 80])
    bfm_d = din("bfm", [16 * 128, 8 * SLAB6])
    cro_d = din("cro", [128, S // 2])
    sgo_d = din("sgo", [128, S // 2])
    msk_d = din("msk", [128, 32 * TT])
    sel16_d = din("sel16", [16, 8 * 128])

    out_d = nc.dram_tensor("out", [D, S // 2], F32, kind="ExternalOutput")

    def scratch(name, shape, dtype):
        kind = "ExternalOutput" if name in _DEBUG_TAPS else "Internal"
        return nc.dram_tensor(name, shape, dtype, kind=kind)

    xbfo = scratch("xbfo", [NT_OWN * 128, NKC * TT], BF16)
    qs = scratch("qs", [QCOLS, S // 2], BF16)
    ks = scratch("ks", [KCOLS, S], BF16)
    vs = scratch("vs", [NKV * 128, 32 * 128], BF16)
    ks_own = scratch("ks_own", [KCOLS, S // 2], BF16)
    vs_own = scratch("vs_own", [NKV * 128, 16 * 128], BF16)
    ks_gath = scratch("ks_gath", [2 * KCOLS, S // 2], BF16)
    vs_gath = scratch("vs_gath", [2 * NKV * 128, 16 * 128], BF16)
    nmax_d = scratch("nmax_d", [1, 4], F32)
    nmax_g = scratch("nmax_g", [1, 4], F32)
    aos = scratch("aos", [NT_OWN * 128, 32 * TT], BF16)
    x2s = scratch("x2s", [D, S // 2], F32)
    x2bf = scratch("x2bf", [NT_OWN * 128, NKC * TT], BF16)

    ctx = contextlib.ExitStack()
    with ctx:
        E = ctx.enter_context

        s_sy = Sem(E(nc.semaphore("s_sy")))
        s_a = Sem(E(nc.semaphore("s_a")))
        s_g = Sem(E(nc.semaphore("s_g")))
        s_pe = Sem(E(nc.semaphore("s_pe")))
        s_ve = Sem(E(nc.semaphore("s_ve")))
        s_ac = Sem(E(nc.semaphore("s_ac")))
        s_gp = Sem(E(nc.semaphore("s_gp")))
        s_bar = Sem(E(nc.semaphore("s_bar")))
        s_st = Sem(E(nc.semaphore("s_st")))
        s_cc = Sem(E(nc.semaphore("s_cc")))
        ALL_SEMS = [s_sy, s_a, s_g, s_pe, s_ve, s_ac, s_gp, s_bar, s_st,
                    s_cc]

        T, V, A, SY, G = nc.tensor, nc.vector, nc.scalar, nc.sync, nc.gpsimd
        DMA_SEM = {id(SY): s_sy, id(A): s_a, id(G): s_g}
        ENG_SEM = {id(T): s_pe, id(V): s_ve, id(A): s_ac, id(G): s_gp}

        def dmai(q, out_ap, in_ap, **kw):
            sem = DMA_SEM[id(q)]
            inst = q.dma_start(out_ap, in_ap, **kw)
            sem.inc(inst, 16)
            return sem.n

        def fence(eng, inst):
            sem = ENG_SEM[id(eng)]
            sem.inc(inst)
            eng.wait_ge(sem.h, sem.n)

        def barrier():
            for eng in (T, V, A, SY, G):
                dsem = DMA_SEM.get(id(eng))
                if dsem is not None:
                    eng.wait_ge(dsem.h, dsem.n)
                s_bar.inc(eng.nop())
            tgt = s_bar.n
            for eng in (T, V, A, SY, G):
                eng.wait_ge(s_bar.h, tgt)

        pb = [E(nc.psum_tensor(f"pb{i}", [128, 512], F32)) for i in range(8)]

        ones_col_bf = E(nc.sbuf_tensor("ones_col_bf", [128, 1], BF16))
        ones_row_bf = E(nc.sbuf_tensor("ones_row_bf", [1, 128], BF16))
        ones_row_f = E(nc.sbuf_tensor("ones_row_f", [1, 128], F32))
        sel_lo = E(nc.sbuf_tensor("sel_lo", [1, 128], BF16))
        sel_hi = E(nc.sbuf_tensor("sel_hi", [1, 128], BF16))
        bqk_sb = E(nc.sbuf_tensor("bqk_sb", [128, 40], F32))
        bqks_sb = E(nc.sbuf_tensor("bqks_sb", [128, 40], F32))
        bv_sb = E(nc.sbuf_tensor("bv_sb", [1, VCOLS], BF16))
        bo_sb = E(nc.sbuf_tensor("bo_sb", [128, 32], F32))
        bb_sb = E(nc.sbuf_tensor("bb_sb", [128, 32], F32))
        eye_sb = E(nc.sbuf_tensor("eye_sb", [128, 128], F32))
        eps_sb = E(nc.sbuf_tensor("eps_sb", [1, 1], F32))
        nmax = E(nc.sbuf_tensor("nmax", [1, 4], F32))
        wsl = [E(nc.sbuf_tensor(f"wsl{i}", [128, NKC * SLAB], BF16))
               for i in range(2)]
        sel16_sb = E(nc.sbuf_tensor("sel16_sb", [16, 8 * 128], BF16))

        with nc.Block() as block0:
            @block0.tensor
            def _(e):
                e.nop()

            @block0.vector
            def _(e):
                e.nop()

            @block0.scalar
            def _(e):
                e.nop()

            @block0.sync
            def _(e):
                e.nop()

            @block0.gpsimd
            def _(e):
                e.nop()

        c0s = contextlib.ExitStack()
        eye_tmp = c0s.enter_context(nc.sbuf_tensor("eye_tmp", [128, 256], F32))
        with nc.body("p0", valid_engines=ALLE):
            G.memset(ones_col_bf[:, :], 1.0)
            G.memset(eps_sb[:, :], LN_EPS)
            G.memset(ones_row_bf[:, :], 1.0)
            G.memset(ones_row_f[:, :], 1.0)
            G.memset(sel_lo[0:1, 0:64], 1.0)
            G.memset(sel_lo[0:1, 64:128], 0.0)
            G.memset(sel_hi[0:1, 0:64], 0.0)
            G.memset(sel_hi[0:1, 64:128], 1.0)
            G.memset(nmax[:, :], 0.0)
            idn_i = c0s.enter_context(nc.sbuf_tensor("idn_i", [128, 128], I32))
            inst = G.iota(idn_i[:, :], pattern=[[1, 128]], base=0,
                          channel_multiplier=-1)
            s_gp.inc(inst)
            V.wait_ge(s_gp.h, s_gp.n)
            inst = V.tensor_copy(eye_tmp[:, 0:128], idn_i[:, :])
            fence(V, inst)
            inst = V.tensor_scalar(eye_tmp[:, 128:256], eye_tmp[:, 0:128],
                                   0.5, None, ALU.is_lt)
            fence(V, inst)
            inst = V.tensor_scalar(eye_tmp[:, 0:128], eye_tmp[:, 0:128],
                                   -0.5, None, ALU.is_ge)
            fence(V, inst)
            inst = V.tensor_tensor(eye_sb[:, :], eye_tmp[:, 0:128],
                                   eye_tmp[:, 128:256], ALU.mult)
            s_ve.inc(inst)
            dmai(A, bqk_sb[:, :], bqk_d[:, :])
            dmai(A, bqks_sb[:, :], bqks_d[:, :])
            dmai(G, bv_sb[:, :], bvrow_d[:, :])
            dmai(G, sel16_sb[:, :], sel16_d[:, :])
            dmai(A, bo_sb[:, :], bo_d[:, :])
            dmai(A, bb_sb[:, :], bb_d[:, :])
            xcast_v = []
            for i in range(4):
                xcast_v.append(dmai(G, xbfo[i * 128:(i + 1) * 128, :],
                                    xTo[:, i * TT:(i + 1) * TT]
                                    .rearrange("(c p) n -> p c n", p=128)))
            qkv_w0_ld = dmai(G, wsl[0][:, :],
                             wqkvT[8 * 128:9 * 128, :])
        c0s.close()

        with contextlib.ExitStack() as c12:
            E12 = c12.enter_context
            negmo_bf = E12(nc.sbuf_tensor("negmo_bf", [1, S // 2], BF16))
            rstdo_bc = E12(nc.sbuf_tensor("rstdo_bc", [128, S // 2], F32))
            rstd_col = E12(nc.sbuf_tensor("rstd_col", [128, 16], F32))
            colsum_bf = E12(nc.sbuf_tensor("colsum_bf", [1, SLAB], BF16))
            colsum_v_bf = E12(nc.sbuf_tensor("colsum_v_bf", [1, SLAB], BF16))

            with nc.body("p1", valid_engines=ALLE):
                with contextlib.ExitStack() as c1:
                    E1 = c1.enter_context
                    xtile = [E1(nc.sbuf_tensor(f"st_x{i}", [128, NKC * TT],
                                               BF16)) for i in range(2)]
                    QK1 = NKC // 4
                    xsq = [E1(nc.sbuf_tensor(f"st_xsq{i}", [128, QK1 * TT],
                                             BF16)) for i in range(2)]
                    rowt = E1(nc.sbuf_tensor("rowt", [1, 2 * TT], F32))

                    prev = {"rows_read": None, "xfree": {},
                            "sq_rd": {}, "bc_read": None, "n": 0}

                    def stats_pass(src, n_tiles, negm_row, rstd_bcast,
                                   do_cols, cast_base):
                        for t in range(n_tiles):
                            bi = prev["n"] % 2
                            prev["n"] += 1
                            SY.wait_ge(s_g.h, xcast_v[cast_base + t])
                            xf = prev["xfree"].get(bi)
                            if xf is not None:
                                SY.wait_ge(xf[0].h, xf[1])
                            ld = dmai(SY, xtile[bi][:, :],
                                      src[t * 128:(t + 1) * 128, :])
                            V.wait_ge(s_sy.h, ld)
                            sqv = []
                            for q in range(4):
                                rd = prev["sq_rd"].get(q % 2)
                                if rd is not None:
                                    V.wait_ge(rd[0].h, rd[1])
                                inst = V.tensor_tensor(
                                    xsq[q % 2][:, :],
                                    xtile[bi][:, q * QK1 * TT:
                                              (q + 1) * QK1 * TT],
                                    xtile[bi][:, q * QK1 * TT:
                                              (q + 1) * QK1 * TT],
                                    ALU.mult)
                                sqv.append(s_ve.inc(inst))
                            T.wait_ge(s_sy.h, ld)
                            if prev["rows_read"] is not None:
                                T.wait_ge(prev["rows_read"][0].h,
                                          prev["rows_read"][1])
                            for k in range(NKC):
                                inst = T.matmul(
                                    pb[4][0:1, :], ones_col_bf[:, :],
                                    xtile[bi][:, k * TT:(k + 1) * TT],
                                    start=(k == 0), stop=(k == NKC - 1))
                            for q in range(4):
                                T.wait_ge(s_ve.h, sqv[q])
                                for k in range(QK1):
                                    inst = T.matmul(
                                        pb[5][0:1, :], ones_col_bf[:, :],
                                        xsq[q % 2][:, k * TT:(k + 1) * TT],
                                        start=(q == 0 and k == 0),
                                        stop=(q == 3 and k == QK1 - 1))
                                prev["sq_rd"][q % 2] = (s_pe, s_pe.inc(inst))
                            sums_done = s_pe.n
                            prev["xfree"][bi] = (s_pe, sums_done)
                            V.wait_ge(s_pe.h, sums_done)
                            inst = V.tensor_scalar(
                                negm_row[0:1, t * TT:(t + 1) * TT],
                                pb[4][0:1, :], -1.0 / D, None, ALU.mult)
                            fence(V, inst)
                            inst = V.tensor_scalar(rowt[0:1, 0:TT],
                                                   pb[5][0:1, :], 1.0 / D,
                                                   None, ALU.mult)
                            fence(V, inst)
                            prev["rows_read"] = (s_ve, s_ve.n)
                            inst = V.tensor_tensor(
                                rowt[0:1, TT:2 * TT],
                                negm_row[0:1, t * TT:(t + 1) * TT],
                                negm_row[0:1, t * TT:(t + 1) * TT], ALU.mult)
                            fence(V, inst)
                            inst = V.tensor_tensor(rowt[0:1, 0:TT],
                                                   rowt[0:1, 0:TT],
                                                   rowt[0:1, TT:2 * TT],
                                                   ALU.subtract)
                            var_done = s_ve.inc(inst)
                            A.wait_ge(s_ve.h, var_done)
                            inst = A.activation(rowt[0:1, TT:2 * TT],
                                                rowt[0:1, 0:TT],
                                                AF.Ln, bias=eps_sb[0:1, 0:1],
                                                scale=1.0)
                            fence(A, inst)
                            inst = A.activation(rowt[0:1, 0:TT],
                                                rowt[0:1, TT:2 * TT],
                                                AF.Exp, scale=-0.5)
                            rstd_done = s_ac.inc(inst)
                            T.wait_ge(s_ac.h, rstd_done)
                            if prev["bc_read"] is not None:
                                T.wait_ge(prev["bc_read"][0].h,
                                          prev["bc_read"][1])
                            inst = T.matmul(pb[6][:, :], ones_row_f[0:1, :],
                                            rowt[0:1, 0:TT], start=True,
                                            stop=True)
                            bc_done = s_pe.inc(inst)
                            A.wait_ge(s_pe.h, bc_done)
                            inst = A.copy(rstd_bcast[:, t * TT:(t + 1) * TT],
                                          pb[6][:, :])
                            bc_ev = s_ac.inc(inst)
                            prev["bc_read"] = (s_ac, bc_ev)
                            if do_cols:
                                A.wait_ge(s_ac.h, bc_ev)
                                for c in range(4):
                                    cc = 4 * t + c
                                    dmai(A, rstd_col[:, cc:cc + 1],
                                         rowt[0:1, c * 128:(c + 1) * 128])
                            V.wait_ge(s_a.h, s_a.n)
                            V.wait_ge(s_ac.h, bc_ev)

                    stats_pass(xbfo, NT_OWN, negmo_bf, rstdo_bc, True, 0)
                    barrier()

            with nc.body("p2", valid_engines=ALLE):
                with contextlib.ExitStack() as c2:
                    E2 = c2.enter_context
                    wbuf = wsl
                    crs = E2(nc.sbuf_tensor("crs", [128, S // 2], F32))
                    sgs = E2(nc.sbuf_tensor("sgs", [128, S // 2], F32))
                    xtile = [E2(nc.sbuf_tensor(f"qk_x{i}", [128, NKC * TT],
                                               BF16)) for i in range(2)]
                    t1b = [E2(nc.sbuf_tensor(f"qk_t1{i}", [128, TT], F32))
                           for i in range(2)]
                    t1sb = [E2(nc.sbuf_tensor(f"qk_t1s{i}", [128, TT], F32))
                            for i in range(2)]
                    r1b = E2(nc.sbuf_tensor("qk_r1", [128, TT], F32))
                    outb = [E2(nc.sbuf_tensor(f"qk_o{i}", [128, TT], BF16))
                            for i in range(2)]
                    nsq = [E2(nc.sbuf_tensor(f"qk_nsq{i}", [128, TT], BF16))
                           for i in range(2)]

                    st2 = {"wfree": {}, "xfree": {}, "t1free": {},
                           "t1sfree": {}, "outfree": {}, "bankfree": {},
                           "cs_read": None, "rope": None, "rope_ld": 0,
                           "rope_rb": 0, "rope_use": {},
                           "slab": 0, "tile": 0, "mm": 0, "ep": 0,
                           "nrm_pend": None, "nsqfree": {}, "nrmfree": None}

                    def flush_norm():
                        if st2["nrm_pend"] is None:
                            return
                        pbuf, psq, pcell = st2["nrm_pend"]
                        st2["nrm_pend"] = None
                        T.wait_ge(s_ac.h, psq)
                        nr = st2["nrmfree"]
                        if nr is not None:
                            T.wait_ge(nr[0].h, nr[1])
                        inst = T.matmul(pb[5][0:1, :], ones_col_bf[:, :],
                                        nsq[pbuf][:, :], start=True,
                                        stop=True)
                        nrm_done = s_pe.inc(inst)
                        st2["nsqfree"][pbuf] = (s_pe, nrm_done)
                        V.wait_ge(s_pe.h, nrm_done)
                        inst = V.reduce_max(nmax[0:1, 2:3], pb[5][0:1, :],
                                            axis=AX)
                        fence(V, inst)
                        st2["nrmfree"] = (s_ve, s_ve.n)
                        inst = V.tensor_tensor(
                            nmax[0:1, pcell:pcell + 1],
                            nmax[0:1, pcell:pcell + 1],
                            nmax[0:1, 2:3], ALU.max)
                        fence(V, inst)

                    def rope_ensure(key):
                        if st2["rope"] == key:
                            return st2["rope_ld"], 0
                        ru = st2["rope_use"].get(0)
                        if ru is not None:
                            SY.wait_ge(ru[0].h, ru[1])
                        assert key[0] == "own"
                        dmai(SY, crs[:, :], cro_d[:, :])
                        ld = dmai(SY, sgs[:, :], sgo_d[:, :])
                        st2["rope"] = key
                        st2["rope_ld"] = ld
                        return ld, 0

                    def load_slab(c0):
                        si = st2["slab"]
                        st2["slab"] += 1
                        bi = si % 2
                        if si == 0:
                            return (s_g, qkv_w0_ld), bi
                        wf = st2["wfree"].get(bi)
                        if wf is not None:
                            G.wait_ge(wf[0].h, wf[1])
                        gsi = c0 // SLAB
                        ld = dmai(G, wbuf[bi][:, :],
                                  wqkvT[gsi * 128:(gsi + 1) * 128, :])
                        return (s_g, ld), bi

                    def colsums(wi, w_ld, cs_bf):
                        T.wait_ge(w_ld[0].h, w_ld[1])
                        if st2["cs_read"] is not None:
                            T.wait_ge(st2["cs_read"][0].h, st2["cs_read"][1])
                        for k in range(NKC):
                            inst = T.matmul(
                                pb[4][0:1, :], ones_col_bf[:, :],
                                wbuf[wi][:, k * SLAB:(k + 1) * SLAB],
                                start=(k == 0), stop=(k == NKC - 1))
                        done = s_pe.inc(inst)
                        A.wait_ge(s_pe.h, done)
                        inst = A.copy(cs_bf[0:1, :], pb[4][0:1, :])
                        v = s_ac.inc(inst)
                        st2["cs_read"] = (s_ac, v)
                        return v

                    def load_xtile(src, t):
                        n = st2["tile"]
                        st2["tile"] += 1
                        bi = n % 2
                        xf = st2["xfree"].get(bi)
                        if xf is not None:
                            SY.wait_ge(xf[0].h, xf[1])
                        ld = dmai(SY, xtile[bi][:, :],
                                  src[t * 128:(t + 1) * 128, :])
                        return ld, bi

                    def qk_slabs(col_base, n_slabs, src, n_tiles, negm_row,
                                 rstd_bcast, out_scr, bias0, rope_of_t,
                                 ncell):
                        for si in range(n_slabs):
                            c0 = col_base + si * SLAB
                            w_ld, wi = load_slab(c0)
                            cs_done = colsums(wi, w_ld, colsum_bf)
                            for t in range(n_tiles):
                                rope_ld, rb = rope_ensure(rope_of_t(t))
                                x_ld, xi = load_xtile(src, t)
                                for m in range(4):
                                    bank = st2["mm"] % 4
                                    st2["mm"] += 1
                                    T.wait_ge(s_sy.h, x_ld)
                                    T.wait_ge(w_ld[0].h, w_ld[1])
                                    T.wait_ge(s_ac.h, cs_done)
                                    bf_ = st2["bankfree"].get(bank)
                                    if bf_ is not None:
                                        T.wait_ge(bf_[0].h, bf_[1])
                                    for k in range(NKC):
                                        T.matmul(
                                            pb[bank][:, :],
                                            wbuf[wi][:, k * SLAB + m * 128:
                                                     k * SLAB + (m + 1) * 128],
                                            xtile[xi][:, k * TT:(k + 1) * TT],
                                            start=(k == 0), stop=False)
                                    inst = T.matmul(
                                        pb[bank][:, :],
                                        colsum_bf[0:1, m * 128:(m + 1) * 128],
                                        negm_row[0:1, t * TT:(t + 1) * TT],
                                        start=False, stop=True)
                                    mm_done = s_pe.inc(inst)
                                    ti = st2["ep"] % 2
                                    st2["ep"] += 1
                                    V.wait_ge(s_pe.h, mm_done)
                                    for key in ("t1free", "outfree",
                                                "t1sfree"):
                                        f = st2[key].get(ti)
                                        if f is not None:
                                            V.wait_ge(f[0].h, f[1])
                                    inst = V.tensor_tensor(
                                        t1b[ti][:, :], pb[bank][:, :],
                                        rstd_bcast[:, t * TT:(t + 1) * TT],
                                        ALU.mult)
                                    V.tensor_tensor(
                                        t1sb[ti][0:64, :],
                                        pb[bank][64:128, :],
                                        rstd_bcast[64:128,
                                                   t * TT:(t + 1) * TT],
                                        ALU.mult)
                                    inst = V.tensor_tensor(
                                        t1sb[ti][64:128, :],
                                        pb[bank][0:64, :],
                                        rstd_bcast[0:64, t * TT:(t + 1) * TT],
                                        ALU.mult)
                                    t1_done = s_ve.inc(inst)
                                    V.wait_ge(s_ve.h, t1_done)
                                    st2["bankfree"][bank] = (s_ve, t1_done)
                                    cchunk = bias0 + 4 * si + m
                                    gc = (t * TT) % (S // 2)
                                    V.wait_ge(s_sy.h, rope_ld)
                                    inst = V.scalar_tensor_tensor(
                                        r1b[:, :], t1b[ti][:, :],
                                        bqk_sb[:, cchunk:cchunk + 1],
                                        crs[:, gc:gc + TT],
                                        ALU.add, ALU.mult)
                                    fence(V, inst)
                                    st2["t1free"][ti] = (s_ve, s_ve.n)
                                    inst = V.scalar_tensor_tensor(
                                        t1sb[ti][:, :], t1sb[ti][:, :],
                                        bqks_sb[:, cchunk:cchunk + 1],
                                        sgs[:, gc:gc + TT],
                                        ALU.add, ALU.mult)
                                    fence(V, inst)
                                    st2["rope_use"][0] = (s_ve, s_ve.n)
                                    inst = V.tensor_tensor(outb[ti][:, :],
                                                           r1b[:, :],
                                                           t1sb[ti][:, :],
                                                           ALU.add)
                                    rp_done = s_ve.inc(inst)
                                    st2["t1sfree"][ti] = (s_ve, rp_done)
                                    nf = st2["nsqfree"].get(ti)
                                    if nf is not None:
                                        A.wait_ge(nf[0].h, nf[1])
                                    A.wait_ge(s_ve.h, rp_done)
                                    inst = A.activation(nsq[ti][:, :],
                                                        outb[ti][:, :],
                                                        AF.Square)
                                    sq_done = s_ac.inc(inst)
                                    r0 = si * SLAB + m * 128
                                    stv = dmai(A, out_scr[r0:r0 + 128,
                                                          t * TT:(t + 1) * TT],
                                               outb[ti][:, :])
                                    st2["outfree"][ti] = (s_a, stv)
                                    flush_norm()
                                    st2["nrm_pend"] = (ti, sq_done, ncell)
                                st2["xfree"][xi] = (s_pe, s_pe.n)
                            st2["wfree"][wi] = (s_pe, s_pe.n)

                    qk_slabs(QCOLS, KCOLS // SLAB, xbfo, NT_OWN, negmo_bf,
                             rstdo_bc, ks_own, 32, lambda t: ("own", 0), 1)

                    for si in range(VCOLS // SLAB):
                        c0 = QCOLS + KCOLS + si * SLAB
                        w_ld, wi = load_slab(c0)
                        cs_done = colsums(wi, w_ld, colsum_v_bf)
                        for t in range(NT_OWN):
                            x_ld, xi = load_xtile(xbfo, t)
                            for m in range(4):
                                bank = st2["mm"] % 4
                                st2["mm"] += 1
                                T.wait_ge(s_sy.h, x_ld)
                                T.wait_ge(w_ld[0].h, w_ld[1])
                                T.wait_ge(s_ac.h, cs_done)
                                bf_ = st2["bankfree"].get(bank)
                                if bf_ is not None:
                                    T.wait_ge(bf_[0].h, bf_[1])
                                for k in range(NKC):
                                    T.matmul(
                                        pb[bank][:, :],
                                        xtile[xi][:, k * TT + m * 128:
                                                  k * TT + (m + 1) * 128],
                                        wbuf[wi][:, k * SLAB:(k + 1) * SLAB],
                                        start=(k == 0), stop=False)
                                tok0 = t * TT + m * 128
                                T.matmul(pb[bank][:, :],
                                         negmo_bf[0:1, tok0:tok0 + 128],
                                         colsum_v_bf[0:1, :],
                                         start=False, stop=False)
                                inst = T.matmul(
                                    pb[bank][:, :], ones_row_bf[0:1, :],
                                    bv_sb[0:1, si * SLAB:(si + 1) * SLAB],
                                    start=False, stop=True)
                                mm_done = s_pe.inc(inst)
                                ti = st2["ep"] % 2
                                st2["ep"] += 1
                                V.wait_ge(s_pe.h, mm_done)
                                of = st2["outfree"].get(ti)
                                if of is not None:
                                    V.wait_ge(of[0].h, of[1])
                                cc = 4 * t + m
                                inst = V.tensor_scalar(outb[ti][:, :],
                                                       pb[bank][:, :],
                                                       rstd_col[:, cc:cc + 1],
                                                       None, ALU.mult)
                                ep_done = s_ve.inc(inst)
                                st2["bankfree"][bank] = (s_ve, ep_done)
                                A.wait_ge(s_ve.h, ep_done)
                                stv = dmai(A, vs_own[4 * si * 128:
                                                     (4 * si + 4) * 128,
                                                     cc * 128:(cc + 1) * 128]
                                           .rearrange("(k p) v -> p k v",
                                                      p=128),
                                           outb[ti][:, :]
                                           .rearrange("p (k v) -> p k v",
                                                      k=4))
                                st2["outfree"][ti] = (s_a, stv)
                            st2["xfree"][xi] = (s_pe, s_pe.n)
                        st2["wfree"][wi] = (s_pe, s_pe.n)

                    G.wait_ge(s_a.h, s_a.n)
                    pairs_rg = [[2 * b, 2 * b + 1] for b in range(4)]
                    inst = G.collective_compute(
                        "AllGather", ALU.bypass, replica_groups=pairs_rg,
                        ins=[ks_own.ap().opt()], outs=[ks_gath.ap().opt()])
                    s_cc.inc(inst, 1)
                    inst = G.collective_compute(
                        "AllGather", ALU.bypass, replica_groups=pairs_rg,
                        ins=[vs_own.ap().opt()], outs=[vs_gath.ap().opt()])
                    s_cc.inc(inst, 1)
                    G.wait_ge(s_cc.h, s_cc.n)
                    for g in range(8):
                        r = 0 if g in G_LISTS[0] else 1
                        gi = G_LISTS[r].index(g)
                        dmai(G, ks[:, g * TT:(g + 1) * TT],
                             ks_gath[r * KCOLS:(r + 1) * KCOLS,
                                     gi * TT:(gi + 1) * TT])
                        dmai(G, vs[:, 4 * g * 128:(4 * g + 4) * 128],
                             vs_gath[r * NKV * 128:(r + 1) * NKV * 128,
                                     4 * gi * 128:(4 * gi + 4) * 128])

                    qk_slabs(0, QCOLS // SLAB, xbfo, NT_OWN, negmo_bf,
                             rstdo_bc, qs, 0, lambda t: ("own", 0), 0)
                    flush_norm()

                    inst = V.nop()
                    nm_cons = s_ve.inc(inst)
                    A.wait_ge(s_ve.h, nm_cons)
                    nm_st = dmai(A, nmax_d[:, :], nmax[0:1, :])
                    G.wait_ge(s_a.h, nm_st)
                    inst = G.collective_compute(
                        "AllReduce", ALU.max, replica_groups=pairs_rg,
                        ins=[nmax_d.ap().opt()], outs=[nmax_g.ap().opt()])
                    s_cc.inc(inst, 1)
                    G.wait_ge(s_cc.h, s_cc.n)
                    dmai(G, nmax[0:1, :], nmax_g[:, :])
                    barrier()

        if _STOP_AFTER == "qkv":
            _finalize(nc, ALLE, SY, s_st)
            return nc

        with contextlib.ExitStack() as c3:
            E3 = c3.enter_context
            k_sb = [E3(nc.sbuf_tensor(f"at_k{i}", [128, S], BF16))
                    for i in range(2)]
            v_sb = [E3(nc.sbuf_tensor(f"at_v{i}", [128, 32 * 128], BF16))
                    for i in range(2)]
            q_sb = [E3(nc.sbuf_tensor(f"at_q{i}", [128, S // 2], BF16))
                    for i in range(2)]
            probs = [E3(nc.sbuf_tensor(f"at_p{i}", [128, TT], BF16))
                     for i in range(8)]
            probm = [E3(nc.sbuf_tensor(f"at_pm{i}", [128, TT], BF16))
                     for i in range(8)]
            mask_bf = E3(nc.sbuf_tensor("at_mb", [128, 32 * TT], BF16))
            gbias = E3(nc.sbuf_tensor("at_gb", [128, 1], F32))
            grow = E3(nc.sbuf_tensor("at_gr", [1, 8], F32))
            dent = [E3(nc.sbuf_tensor(f"at_dn{i}", [1, TT], F32))
                    for i in range(2)]
            rdbc = [E3(nc.sbuf_tensor(f"at_rb{i}", [128, TT], F32))
                    for i in range(2)]
            ao_sb = [E3(nc.sbuf_tensor(f"at_ao{i}", [128, TT], BF16))
                     for i in range(2)]


            pref = {}

            def attn_body():
                mask_q = []
                for j in range(4):
                    mask_q.append(
                        dmai(G, mask_bf[:, j * 8 * TT:(j + 1) * 8 * TT],
                             msk_d[:, j * 8 * TT:(j + 1) * 8 * TT]))
                inst = V.tensor_tensor(grow[0:1, 0:1], nmax[0:1, 0:1],
                                       nmax[0:1, 1:2], ALU.mult)
                fence(V, inst)
                A.wait_ge(s_ve.h, s_ve.n)
                inst = A.activation(grow[0:1, 1:2], grow[0:1, 0:1], AF.Sqrt)
                fence(A, inst)
                V.wait_ge(s_ac.h, s_ac.n)
                inst = V.tensor_scalar(grow[0:1, 2:3], grow[0:1, 1:2], 2.0,
                                       -INV_SQRT_HD, ALU.add, ALU.mult)
                gb_done = s_ve.inc(inst)
                T.wait_ge(s_ve.h, gb_done)
                inst = T.matmul(pb[0][:, 0:1], ones_row_f[0:1, :],
                                grow[0:1, 2:3], start=True, stop=True)
                gb_bc = s_pe.inc(inst)
                A.wait_ge(s_pe.h, gb_bc)
                inst = A.copy(gbias[:, :], pb[0][:, 0:1])
                gb_cp = s_ac.inc(inst)
                A.wait_ge(s_ac.h, gb_cp)

                lim = _ATTN_LIMIT or (NKV, 4, NT_OWN)
                st3 = {"bankfree": {0: (s_ac, gb_cp)}, "probfree": {},
                       "pvfree": {},
                       "denfree": {}, "aofree": {}, "rdfree": {},
                       "denbfree": {}, "pending": None,
                       "pending_v": None, "rdb": None, "accfree": None}
                pairs = [(kvh, qi) for kvh in range(lim[0])
                         for qi in range(lim[1])]
                k_ld = {}
                v_ld = {}
                q_ld = {}
                pe_end = {}
                q_end = {}

                def load_kv(j):
                    if j >= lim[0]:
                        return
                    bi = j % 2
                    if j >= 2:
                        SY.wait_ge(s_pe.h, pe_end[j - 2])
                    k_ld[j] = dmai(SY, k_sb[bi][:, :],
                                   ks[j * 128:(j + 1) * 128, :])
                    v_ld[j] = dmai(SY, v_sb[bi][:, :],
                                   vs[j * 128:(j + 1) * 128, :])

                def load_q(j):
                    if j >= len(pairs):
                        return
                    kvh_, qi_ = pairs[j]
                    qh_ = kvh_ * 4 + qi_
                    bi = j % 2
                    if j >= 2:
                        SY.wait_ge(s_pe.h, q_end[j - 2])
                    q_ld[j] = dmai(SY, q_sb[bi][:, :],
                                   qs[qh_ * 128:(qh_ + 1) * 128, :])

                def mk_epi1(ti, D, den_done):
                    def epi1():
                        ei = ti % 2
                        V.wait_ge(s_pe.h, den_done)
                        dbf = st3["denbfree"].get(ei)
                        if dbf is not None:
                            V.wait_ge(dbf[0].h, dbf[1])
                        inst = V.tensor_copy(dent[ei][0:1, :],
                                             pb[D][0:1, :])
                        rdb = s_ve.inc(inst)
                        return rdb
                    return epi1

                def mk_epi2(ti, P, D, qh, t):
                    def epi2(rdb):
                        ei = ti % 2
                        T.wait_ge(s_ve.h, rdb)
                        inst = T.matmul(pb[D][:, :], ones_row_f[0:1, :],
                                        dent[ei][0:1, :], start=True,
                                        stop=True)
                        bc_done = s_pe.inc(inst)
                        st3["denbfree"][ei] = (s_pe, bc_done)
                        V.wait_ge(s_pe.h, bc_done)
                        rf = st3["rdfree"].get(ei)
                        if rf is not None:
                            V.wait_ge(rf[0].h, rf[1])
                        inst = V.reciprocal(rdbc[ei][:, :], pb[D][:, :])
                        rbc_done = s_ve.inc(inst)
                        st3["denfree"][D] = (s_ve, rbc_done)
                        V.wait_ge(s_ve.h, rbc_done)
                        af = st3["aofree"].get(ei)
                        if af is not None:
                            V.wait_ge(af[0].h, af[1])
                        inst = V.tensor_tensor(ao_sb[ei][:, :], pb[P][:, :],
                                               rdbc[ei][:, :], ALU.mult)
                        ao_done = s_ve.inc(inst)
                        st3["pvfree"][P] = (s_ve, ao_done)
                        st3["rdfree"][ei] = (s_ve, ao_done)
                        G.wait_ge(s_ve.h, ao_done)
                        stv = dmai(G, aos[t * 128:(t + 1) * 128,
                                          qh * TT:(qh + 1) * TT],
                                   ao_sb[ei][:, :])
                        st3["aofree"][ei] = (s_g, stv)
                    return epi2

                load_kv(0)
                load_kv(1)
                load_q(0)
                load_q(1)

                for idx, (kvh, qi) in enumerate(pairs):
                    qh = kvh * 4 + qi
                    if idx == 2:
                        pref["p4w0"] = dmai(G, wsl[0][:, :], woT[0:128, :])
                    if qi == 0 and kvh >= 1:
                        load_kv(kvh + 1)
                    if idx >= 1:
                        load_q(idx + 1)
                    for t in range(lim[2]):
                        V.wait_ge(s_g.h, mask_q[t])
                        ti = idx * NT_OWN + t
                        P = 4 + ti % 2
                        D = 6 + ti % 2
                        nk = 4 * (2 * t + 2)
                        sc_done = {}
                        use = {}
                        pm = {}

                        def emit_score(c):
                            sb = c % 4
                            bf_ = st3["bankfree"].get(sb)
                            if bf_ is not None:
                                T.wait_ge(bf_[0].h, bf_[1])
                            T.wait_ge(s_sy.h, max(k_ld[kvh], q_ld[idx]))
                            inst = T.matmul(
                                pb[sb][:, :],
                                k_sb[kvh % 2][:, c * 128:(c + 1) * 128],
                                q_sb[idx % 2][:, t * TT:(t + 1) * TT],
                                start=True, stop=True)
                            sc_done[c] = s_pe.inc(inst)

                        def emit_exp(c):
                            sb = c % 4
                            pbuf = c % 8
                            A.wait_ge(s_pe.h, sc_done[c])
                            pf = st3["probfree"].get(pbuf)
                            if pf is not None:
                                A.wait_ge(pf[0].h, pf[1])
                            inst = A.activation(probs[pbuf][:, :],
                                                pb[sb][:, :],
                                                AF.Exp, bias=gbias[:, 0:1],
                                                scale=INV_SQRT_HD)
                            e_done = s_ac.inc(inst)
                            st3["bankfree"][sb] = (s_ac, e_done)
                            d_ = c - (nk - 8)
                            if d_ >= 0:
                                mcol = (t * 8 + d_) * TT
                                V.wait_ge(s_ac.h, e_done)
                                if pf is not None:
                                    V.wait_ge(pf[0].h, pf[1])
                                inst = V.tensor_tensor(
                                    probm[pbuf][:, :], probs[pbuf][:, :],
                                    mask_bf[:, mcol:mcol + TT], ALU.mult)
                                use[c] = (s_ve, s_ve.inc(inst))
                                pm[c] = probm[pbuf]
                            else:
                                use[c] = (s_ac, e_done)
                                pm[c] = probs[pbuf]

                        def emit_pv(c):
                            pbuf = c % 8
                            T.wait_ge(use[c][0].h, use[c][1])
                            if c == 0:
                                pvf = st3["pvfree"].get(P)
                                if pvf is not None:
                                    T.wait_ge(pvf[0].h, pvf[1])
                                dnf = st3["denfree"].get(D)
                                if dnf is not None:
                                    T.wait_ge(dnf[0].h, dnf[1])
                                T.wait_ge(s_sy.h, v_ld[kvh])
                            T.matmul(pb[P][:, :],
                                     v_sb[kvh % 2][:, c * 128:(c + 1) * 128],
                                     pm[c][:, :], start=(c == 0),
                                     stop=(c == nk - 1),
                                     skip_group_check=True)
                            inst = T.matmul(pb[D][0:1, :], ones_col_bf[:, :],
                                            pm[c][:, :], start=(c == 0),
                                            stop=(c == nk - 1),
                                            skip_group_check=True)
                            pvd = s_pe.inc(inst)
                            st3["probfree"][pbuf] = (s_pe, pvd)
                            return pvd

                        pv_last = None
                        LAG = 6
                        for c in range(nk):
                            emit_score(c)
                            if c == 3 and st3["pending_v"] is not None:
                                st3["rdb"] = st3["pending_v"]()
                                st3["pending_v"] = None
                            if c == 7 and st3["pending"] is not None:
                                st3["pending"](st3["rdb"])
                                st3["pending"] = None
                            if c >= 1:
                                emit_exp(c - 1)
                            if c >= LAG:
                                pv_last = emit_pv(c - LAG)
                        emit_exp(nk - 1)
                        for cc in range(nk - LAG, nk):
                            pv_last = emit_pv(cc)
                        if t == lim[2] - 1:
                            st3["pending_v"] = mk_epi1(ti, D, pv_last)
                        else:
                            st3["rdb"] = mk_epi1(ti, D, pv_last)()
                            st3["pending_v"] = None
                        st3["pending"] = mk_epi2(ti, P, D, qh, t)
                    q_end[idx] = s_pe.n
                    if qi == lim[1] - 1:
                        pe_end[kvh] = s_pe.n
                if st3["pending_v"] is not None:
                    st3["rdb"] = st3["pending_v"]()
                    st3["pending_v"] = None
                if st3["pending"] is not None:
                    st3["pending"](st3["rdb"])
                    st3["pending"] = None

            with nc.body("p3main", valid_engines=ALLE):
                attn_body()
                barrier()

        if _STOP_AFTER == "attn":
            _finalize(nc, ALLE, SY, s_st)
            return nc

        with nc.body("p4", valid_engines=ALLE):
            with contextlib.ExitStack() as c4:
                E4 = c4.enter_context
                wbuf = wsl
                atile = [E4(nc.sbuf_tensor(f"op_a{i}", [128, NKC * TT],
                                           BF16)) for i in range(2)]
                xres = [E4(nc.sbuf_tensor(f"op_x{i}", [128, 4 * TT], F32))
                        for i in range(2)]
                t1 = E4(nc.sbuf_tensor("op_t1", [128, TT], F32))
                x2o = [E4(nc.sbuf_tensor(f"op_o{i}", [128, TT], F32))
                       for i in range(2)]
                x2obf = [E4(nc.sbuf_tensor(f"op_ob{i}", [128, TT], BF16))
                         for i in range(2)]
                st4 = {"wfree": {}, "afree": {}, "xfree": {}, "bankfree": {},
                       "ofree": {}}
                for si in range(D // SLAB):
                    wi = si % 2
                    if si == 0 and "p4w0" in pref:
                        w_ld = pref["p4w0"]
                    else:
                        wf = st4["wfree"].get(wi)
                        if wf is not None:
                            G.wait_ge(wf[0].h, wf[1])
                        w_ld = dmai(G, wbuf[wi][:, :],
                                    woT[si * 128:(si + 1) * 128, :])
                    for t in range(NT_OWN):
                        it = si * NT_OWN + t
                        ai = it % 2
                        af = st4["afree"].get(ai)
                        if af is not None:
                            SY.wait_ge(af[0].h, af[1])
                        a_ld = dmai(SY, atile[ai][:, :],
                                    aos[t * 128:(t + 1) * 128, :])
                        xf = st4["xfree"].get(ai)
                        if xf is not None:
                            SY.wait_ge(xf[0].h, xf[1])
                        x_ld = dmai(SY, xres[ai][:, :]
                                    .rearrange("p (c t) -> p c t", c=4),
                                    xTo[si * SLAB:(si + 1) * SLAB,
                                        t * TT:(t + 1) * TT]
                                    .rearrange("(c p) t -> p c t", p=128))
                        for m in range(4):
                            bank = (it * 4 + m) % 4
                            bf_ = st4["bankfree"].get(bank)
                            T.wait_ge(s_g.h, w_ld)
                            T.wait_ge(s_sy.h, a_ld)
                            if bf_ is not None:
                                T.wait_ge(bf_[0].h, bf_[1])
                            for k in range(NKC):
                                inst = T.matmul(
                                    pb[bank][:, :],
                                    wbuf[wi][:, k * SLAB + m * 128:
                                             k * SLAB + (m + 1) * 128],
                                    atile[ai][:, k * TT:(k + 1) * TT],
                                    start=(k == 0), stop=(k == NKC - 1))
                            mm_done = s_pe.inc(inst)
                            V.wait_ge(s_pe.h, mm_done)
                            V.wait_ge(s_sy.h, x_ld)
                            oc = si * 4 + m
                            inst = V.tensor_scalar(t1[:, :], pb[bank][:, :],
                                                   bo_sb[:, oc:oc + 1], None,
                                                   ALU.add)
                            fence(V, inst)
                            st4["bankfree"][bank] = (s_ve, s_ve.n)
                            oi = (it * 4 + m) % 2
                            of = st4["ofree"].get(oi)
                            if of is not None:
                                V.wait_ge(of[0].h, of[1])
                            inst = V.tensor_tensor(
                                x2o[oi][:, :], t1[:, :],
                                xres[ai][:, m * TT:(m + 1) * TT], ALU.add)
                            x2_done = s_ve.inc(inst)
                            V.wait_ge(s_ve.h, x2_done)
                            inst = V.tensor_copy(x2obf[oi][:, :],
                                                 x2o[oi][:, :])
                            xbf_done = s_ve.inc(inst)
                            A.wait_ge(s_ve.h, xbf_done)
                            r0 = si * SLAB + m * 128
                            dmai(A, x2s[r0:r0 + 128, t * TT:(t + 1) * TT],
                                 x2o[oi][:, :])
                            stv = dmai(A, x2bf[t * 128:(t + 1) * 128,
                                               (r0 // 128) * TT:
                                               (r0 // 128 + 1) * TT],
                                       x2obf[oi][:, :])
                            st4["ofree"][oi] = (s_a, stv)
                        st4["afree"][ai] = (s_pe, s_pe.n)
                        st4["xfree"][ai] = (s_ve, s_ve.n)
                    st4["wfree"][wi] = (s_pe, s_pe.n)
                barrier()

        if _STOP_AFTER == "oproj":
            _finalize(nc, ALLE, SY, s_st)
            return nc

        with contextlib.ExitStack() as c56:
            E56 = c56.enter_context
            rhe = [E56(nc.sbuf_tensor(f"f_rhe{i}", [128, 8 * TT], BF16))
                   for i in range(NT_OWN)]
            bfb = [E56(nc.sbuf_tensor(f"bfb{i}", [128, 8 * SLAB6], BF16))
                   for i in range(2)]
            negm2_bf = E56(nc.sbuf_tensor("f_negm2", [1, S // 2], BF16))
            rstd2_bc = E56(nc.sbuf_tensor("f_rstd2bc", [128, S // 2], F32))

            with nc.body("p5", valid_engines=ALLE):
                with contextlib.ExitStack() as c5:
                    E5 = c5.enter_context
                    xtile = [E5(nc.sbuf_tensor(f"f5_x{i}", [128, NKC * TT],
                                               BF16)) for i in range(2)]
                    QK5 = NKC // 8
                    NG5 = NKC // QK5
                    xsq5 = [E5(nc.sbuf_tensor("f5_xsq0", [128, QK5 * TT],
                                              BF16))]
                    rowt5 = E5(nc.sbuf_tensor("f5_rowt", [1, 2 * TT], F32))
                    rstd2_col = E5(nc.sbuf_tensor("f5_rcol", [128, 16], F32))
                    art_sb = E5(nc.sbuf_tensor("f5_art", [128, NKC * 80],
                                               BF16))
                    csar_bf = E5(nc.sbuf_tensor("f5_csar", [1, 80], BF16))
                    rh2 = [E5(nc.sbuf_tensor(f"f5_rh2{i}", [128, TT], F32))
                           for i in range(2)]
                    lgs = E5(nc.sbuf_tensor("f5_lgs", [16, TT], F32))
                    lt_tm = E5(nc.sbuf_tensor("f5_lt", [128, 64], F32))
                    ptm = E5(nc.sbuf_tensor("f5_ptm", [128, 64], F32))
                    rcol = E5(nc.sbuf_tensor("f5_rc", [128, 12], F32))
                    rtmp = E5(nc.sbuf_tensor("f5_rt", [128, 64], F32))
                    comb_tm = E5(nc.sbuf_tensor("f5_comb", [128, 64], F32))
                    comb_sb = E5(nc.sbuf_tensor("f5_csb", [16, TT], BF16))

                    art_ld = dmai(G, art_sb[:, :], art_d[:, :])
                    pref["p6w0"] = dmai(G, wsl[0][:, 0:NKC * SLAB6],
                                        bwT[0:128, :])
                    pref["p6b0"] = dmai(G, bfb[0][:, :], bfm_d[0:128, :])
                    T.wait_ge(s_g.h, art_ld)
                    for k in range(NKC):
                        inst = T.matmul(pb[4][0:1, 0:80], ones_col_bf[:, :],
                                        art_sb[:, k * 80:(k + 1) * 80],
                                        start=(k == 0), stop=(k == NKC - 1))
                    csd = s_pe.inc(inst)
                    A.wait_ge(s_pe.h, csd)
                    inst = A.copy(csar_bf[0:1, :], pb[4][0:1, 0:80])
                    csar_done = s_ac.inc(inst)

                    prev5 = {"rows_read": (s_ac, csar_done), "xfree": {},
                             "sq_rd": {}, "bc_read": None}
                    for t in range(NT_OWN):
                        bi = t % 2
                        xf = prev5["xfree"].get(bi)
                        if xf is not None:
                            SY.wait_ge(xf[0].h, xf[1])
                        ld = dmai(SY, xtile[bi][:, :],
                                  x2bf[t * 128:(t + 1) * 128, :])
                        V.wait_ge(s_sy.h, ld)
                        sqv5 = []
                        for q in range(NG5):
                            rd = prev5["sq_rd"].get(0)
                            if rd is not None:
                                V.wait_ge(rd[0].h, rd[1])
                            inst = V.tensor_tensor(
                                xsq5[0][:, :],
                                xtile[bi][:, q * QK5 * TT:(q + 1) * QK5 * TT],
                                xtile[bi][:, q * QK5 * TT:(q + 1) * QK5 * TT],
                                ALU.mult)
                            sqv5.append(s_ve.inc(inst))
                        T.wait_ge(s_sy.h, ld)
                        T.wait_ge(prev5["rows_read"][0].h,
                                  prev5["rows_read"][1])
                        for k in range(NKC):
                            inst = T.matmul(pb[4][0:1, :], ones_col_bf[:, :],
                                            xtile[bi][:, k * TT:(k + 1) * TT],
                                            start=(k == 0),
                                            stop=(k == NKC - 1))
                        for q in range(NG5):
                            T.wait_ge(s_ve.h, sqv5[q])
                            for k in range(QK5):
                                inst = T.matmul(
                                    pb[5][0:1, :], ones_col_bf[:, :],
                                    xsq5[0][:, k * TT:(k + 1) * TT],
                                    start=(q == 0 and k == 0),
                                    stop=(q == NG5 - 1 and k == QK5 - 1))
                            prev5["sq_rd"][0] = (s_pe, s_pe.inc(inst))
                        sums_done = s_pe.n
                        prev5["xfree"][bi] = (s_pe, sums_done)
                        V.wait_ge(s_pe.h, sums_done)
                        inst = V.tensor_scalar(
                            negm2_bf[0:1, t * TT:(t + 1) * TT],
                            pb[4][0:1, :], -1.0 / D, None, ALU.mult)
                        fence(V, inst)
                        inst = V.tensor_scalar(rowt5[0:1, 0:TT],
                                               pb[5][0:1, :], 1.0 / D, None,
                                               ALU.mult)
                        fence(V, inst)
                        prev5["rows_read"] = (s_ve, s_ve.n)
                        inst = V.tensor_tensor(
                            rowt5[0:1, TT:2 * TT],
                            negm2_bf[0:1, t * TT:(t + 1) * TT],
                            negm2_bf[0:1, t * TT:(t + 1) * TT], ALU.mult)
                        fence(V, inst)
                        inst = V.tensor_tensor(rowt5[0:1, 0:TT],
                                               rowt5[0:1, 0:TT],
                                               rowt5[0:1, TT:2 * TT],
                                               ALU.subtract)
                        var_done = s_ve.inc(inst)
                        A.wait_ge(s_ve.h, var_done)
                        inst = A.activation(rowt5[0:1, TT:2 * TT],
                                            rowt5[0:1, 0:TT], AF.Ln,
                                            bias=eps_sb[0:1, 0:1], scale=1.0)
                        fence(A, inst)
                        inst = A.activation(rowt5[0:1, 0:TT],
                                            rowt5[0:1, TT:2 * TT], AF.Exp,
                                            scale=-0.5)
                        rstd_done = s_ac.inc(inst)
                        T.wait_ge(s_ac.h, rstd_done)
                        if prev5["bc_read"] is not None:
                            T.wait_ge(prev5["bc_read"][0].h,
                                      prev5["bc_read"][1])
                        inst = T.matmul(pb[6][:, :], ones_row_f[0:1, :],
                                        rowt5[0:1, 0:TT], start=True,
                                        stop=True)
                        bc_done = s_pe.inc(inst)
                        A.wait_ge(s_pe.h, bc_done)
                        inst = A.copy(rstd2_bc[:, t * TT:(t + 1) * TT],
                                      pb[6][:, :])
                        bc_ev = s_ac.inc(inst)
                        prev5["bc_read"] = (s_ac, bc_ev)
                        A.wait_ge(s_ac.h, bc_ev)
                        for c in range(4):
                            cc = 4 * t + c
                            dmai(A, rstd2_col[:, cc:cc + 1],
                                 rowt5[0:1, c * 128:(c + 1) * 128])
                        V.wait_ge(s_a.h, s_a.n)
                        V.wait_ge(s_ac.h, bc_ev)

                    for t in range(NT_OWN):
                        bi = t % 2
                        xf = prev5["xfree"].get(bi)
                        if xf is not None:
                            SY.wait_ge(xf[0].h, xf[1])
                        ld = dmai(SY, xtile[bi][:, :],
                                  x2bf[t * 128:(t + 1) * 128, :])
                        ri = t % 2
                        T.wait_ge(s_sy.h, ld)
                        T.wait_ge(s_ve.h, s_ve.n)
                        for k in range(NKC):
                            T.matmul(pb[4][0:64, :],
                                     art_sb[:, k * 80:k * 80 + 64],
                                     xtile[bi][:, k * TT:(k + 1) * TT],
                                     start=(k == 0), stop=False)
                        inst = T.matmul(pb[4][0:64, :], csar_bf[0:1, 0:64],
                                        negm2_bf[0:1, t * TT:(t + 1) * TT],
                                        start=False, stop=True)
                        rh_done = s_pe.inc(inst)
                        for k in range(NKC):
                            T.matmul(pb[5][0:16, :],
                                     art_sb[:, k * 80 + 64:k * 80 + 80],
                                     xtile[bi][:, k * TT:(k + 1) * TT],
                                     start=(k == 0), stop=False)
                        inst = T.matmul(pb[5][0:16, :], csar_bf[0:1, 64:80],
                                        negm2_bf[0:1, t * TT:(t + 1) * TT],
                                        start=False, stop=True)
                        lg_done = s_pe.inc(inst)
                        prev5["xfree"][bi] = (s_pe, lg_done)
                        V.wait_ge(s_pe.h, rh_done)
                        inst = V.tensor_tensor(
                            rh2[ri][0:64, :], pb[4][0:64, :],
                            rstd2_bc[0:64, t * TT:(t + 1) * TT], ALU.mult)
                        rh_ev = s_ve.inc(inst)
                        A.wait_ge(s_ve.h, rh_ev)
                        rh_c2 = dmai(A, rh2[ri][64:128, :], rh2[ri][0:64, :])
                        V.wait_ge(s_pe.h, lg_done)
                        inst = V.tensor_tensor(
                            lgs[0:16, :], pb[5][0:16, :],
                            rstd2_bc[0:16, t * TT:(t + 1) * TT], ALU.mult)
                        lg_ev = s_ve.inc(inst)
                        T.wait_ge(s_ve.h, lg_ev)
                        for j in range(4):
                            inst = T.transpose(pb[6][:, j * 16:(j + 1) * 16],
                                               lgs[0:16,
                                                   j * 128:(j + 1) * 128],
                                               eye_sb[0:16, 0:16])
                        ltr_done = s_pe.inc(inst)
                        A.wait_ge(s_pe.h, ltr_done)
                        inst = A.copy(lt_tm[:, :], pb[6][:, 0:64])
                        lt_ev = s_ac.inc(inst)
                        T.wait_ge(s_ac.h, lt_ev)
                        V.wait_ge(s_ac.h, lt_ev)
                        for j in range(4):
                            lsl = lt_tm[:, j * 16:(j + 1) * 16]
                            psl = ptm[:, j * 16:(j + 1) * 16]
                            csl = comb_tm[:, j * 16:(j + 1) * 16]
                            inst = V.reduce_max(rcol[:, 0:1], lsl, axis=AX)
                            fence(V, inst)
                            inst = V.tensor_scalar(rcol[:, 1:2], rcol[:, 0:1],
                                                   -1.0, None, ALU.mult)
                            fence(V, inst)
                            A.wait_ge(s_ve.h, s_ve.n)
                            inst = A.activation(psl, lsl, AF.Exp,
                                                bias=rcol[:, 1:2], scale=1.0)
                            fence(A, inst)
                            V.wait_ge(s_ac.h, s_ac.n)
                            inst = V.reduce_sum(rcol[:, 2:3], psl, axis=AX)
                            fence(V, inst)
                            inst = V.reduce_max(rcol[:, 3:4], psl, axis=AX)
                            fence(V, inst)
                            inst = V.tensor_scalar(rtmp[:, 0:16], psl,
                                                   rcol[:, 3:4], None,
                                                   ALU.is_ge)
                            fence(V, inst)
                            inst = V.scalar_tensor_tensor(rtmp[:, 16:32],
                                                          rtmp[:, 0:16], 1e30,
                                                          psl, ALU.mult,
                                                          ALU.subtract)
                            fence(V, inst)
                            inst = V.tensor_scalar(rtmp[:, 32:48],
                                                   rtmp[:, 16:32], -1.0,
                                                   None, ALU.mult)
                            fence(V, inst)
                            inst = V.reduce_max(rcol[:, 4:5], rtmp[:, 32:48],
                                                axis=AX)
                            fence(V, inst)
                            inst = V.tensor_scalar(rcol[:, 5:6], rcol[:, 2:3],
                                                   1e-6, None, ALU.mult)
                            fence(V, inst)
                            inst = V.tensor_tensor(rcol[:, 6:7], rcol[:, 3:4],
                                                   rcol[:, 4:5], ALU.add)
                            fence(V, inst)
                            inst = V.tensor_tensor(rcol[:, 7:8], rcol[:, 5:6],
                                                   rcol[:, 6:7], ALU.add)
                            fence(V, inst)
                            inst = V.reciprocal(rcol[:, 8:9], rcol[:, 7:8])
                            fence(V, inst)
                            inst = V.reciprocal(rcol[:, 9:10],
                                                rstd2_col[:, 4 * t + j:
                                                          4 * t + j + 1])
                            fence(V, inst)
                            inst = V.tensor_scalar(rtmp[:, 0:16], psl,
                                                   rcol[:, 4:5], None,
                                                   ALU.is_ge)
                            fence(V, inst)
                            inst = V.tensor_scalar(rtmp[:, 16:32], psl,
                                                   rcol[:, 8:9], 0.5,
                                                   ALU.mult, ALU.mult)
                            fence(V, inst)
                            inst = V.tensor_scalar(rtmp[:, 48:64],
                                                   rtmp[:, 16:32],
                                                   rcol[:, 9:10], None,
                                                   ALU.mult)
                            fence(V, inst)
                            inst = V.tensor_tensor(csl, rtmp[:, 0:16],
                                                   rtmp[:, 48:64], ALU.mult)
                            fence(V, inst)
                        comb_done = s_ve.n
                        T.wait_ge(s_ve.h, comb_done)
                        T.wait_ge(s_ac.h, s_ac.n)
                        for j in range(4):
                            inst = T.transpose(pb[6][0:16,
                                                     j * 128:(j + 1) * 128],
                                               comb_tm[:, j * 16:(j + 1) * 16],
                                               eye_sb[:, :])
                        ctr_done = s_pe.inc(inst)
                        A.wait_ge(s_pe.h, ctr_done)
                        inst = A.copy(comb_sb[0:16, :], pb[6][0:16, 0:TT])
                        cb_ev = s_ac.inc(inst)
                        for c in range(8):
                            T.wait_ge(s_ac.h, cb_ev)
                            T.wait_ge(s_ve.h, s_ve.n)
                            inst = T.matmul(pb[7][:, :],
                                            sel16_sb[0:16,
                                                     c * 128:(c + 1) * 128],
                                            comb_sb[0:16, :],
                                            start=True, stop=True)
                            bcp_done = s_pe.inc(inst)
                            V.wait_ge(s_pe.h, bcp_done)
                            V.wait_ge(s_a.h, rh_c2)
                            inst = V.tensor_tensor(
                                rhe[t][:, c * TT:(c + 1) * TT], pb[7][:, :],
                                rh2[ri][:, :], ALU.mult)
                            s_ve.inc(inst)
                    barrier()

            if _STOP_AFTER == "router":
                _finalize(nc, ALLE, SY, s_st)
                return nc

            with nc.body("p6", valid_engines=ALLE):
                with contextlib.ExitStack() as c6:
                    E6 = c6.enter_context
                    NCH6 = SLAB6 // 128
                    wbuf = wsl
                    bfbuf = bfb
                    xtile6 = [E6(nc.sbuf_tensor(f"f6_x{i}", [128, NKC * TT],
                                                BF16)) for i in range(2)]
                    x2res = [E6(nc.sbuf_tensor(f"f6_r{i}",
                                               [128, NCH6 * TT], F32))
                             for i in range(2)]
                    t16 = E6(nc.sbuf_tensor("f6_t1", [128, TT], F32))
                    ob6 = [E6(nc.sbuf_tensor(f"f6_o{i}", [128, TT], F32))
                           for i in range(2)]
                    cs6 = E6(nc.sbuf_tensor("f6_cs", [1, SLAB6], BF16))
                    st6 = {"wfree": {}, "xfree": {}, "rfree": {},
                           "bankfree": {}, "ofree": {}, "cs_read": None}
                    for si in range(D // SLAB6):
                        wi = si % 2
                        if si == 0:
                            w_ld = (s_g, pref["p6w0"])
                            b_ld = (s_g, pref["p6b0"])
                        else:
                            wf = st6["wfree"].get(wi)
                            if wf is not None:
                                G.wait_ge(wf[0].h, wf[1])
                            w_ld = (s_g,
                                    dmai(G, wbuf[wi][:, 0:NKC * SLAB6],
                                         bwT[si * 128:(si + 1) * 128, :]))
                            b_ld = (s_g,
                                    dmai(G, bfbuf[wi][:, :],
                                         bfm_d[si * 128:(si + 1) * 128, :]))
                        T.wait_ge(w_ld[0].h, w_ld[1])
                        if st6["cs_read"] is not None:
                            T.wait_ge(st6["cs_read"][0].h,
                                      st6["cs_read"][1])
                        for k in range(NKC):
                            inst = T.matmul(
                                pb[4][0:1, 0:SLAB6], ones_col_bf[:, :],
                                wbuf[wi][:, k * SLAB6:(k + 1) * SLAB6],
                                start=(k == 0), stop=(k == NKC - 1))
                        csd = s_pe.inc(inst)
                        A.wait_ge(s_pe.h, csd)
                        inst = A.copy(cs6[0:1, :], pb[4][0:1, 0:SLAB6])
                        cs_done = s_ac.inc(inst)
                        st6["cs_read"] = (s_ac, cs_done)
                        for t in range(NT_OWN):
                            it = si * NT_OWN + t
                            xi = it % 2
                            xf = st6["xfree"].get(xi)
                            if xf is not None:
                                SY.wait_ge(xf[0].h, xf[1])
                            x_ld = dmai(SY, xtile6[xi][:, :],
                                        x2bf[t * 128:(t + 1) * 128, :])
                            rf = st6["rfree"].get(xi)
                            if rf is not None:
                                SY.wait_ge(rf[0].h, rf[1])
                            r_ld = dmai(SY, x2res[xi][:, :]
                                        .rearrange("p (c t) -> p c t",
                                                   c=NCH6),
                                        x2s[si * SLAB6:(si + 1) * SLAB6,
                                            t * TT:(t + 1) * TT]
                                        .rearrange("(c p) t -> p c t", p=128))
                            for m in range(NCH6):
                                bank = (it * NCH6 + m) % 4
                                bf_ = st6["bankfree"].get(bank)
                                T.wait_ge(b_ld[0].h, b_ld[1])
                                T.wait_ge(s_sy.h, x_ld)
                                T.wait_ge(s_ac.h, cs_done)
                                if bf_ is not None:
                                    T.wait_ge(bf_[0].h, bf_[1])
                                for k in range(NKC):
                                    T.matmul(
                                        pb[bank][:, :],
                                        wbuf[wi][:, k * SLAB6 + m * 128:
                                                 k * SLAB6 + (m + 1) * 128],
                                        xtile6[xi][:, k * TT:(k + 1) * TT],
                                        start=(k == 0), stop=False)
                                for c in range(8):
                                    T.matmul(
                                        pb[bank][:, :],
                                        bfbuf[wi][:, c * SLAB6 + m * 128:
                                                  c * SLAB6 + (m + 1) * 128],
                                        rhe[t][:, c * TT:(c + 1) * TT],
                                        start=False, stop=False)
                                inst = T.matmul(
                                    pb[bank][:, :],
                                    cs6[0:1, m * 128:(m + 1) * 128],
                                    negm2_bf[0:1, t * TT:(t + 1) * TT],
                                    start=False, stop=True)
                                mm_done = s_pe.inc(inst)
                                V.wait_ge(s_pe.h, mm_done)
                                V.wait_ge(s_sy.h, r_ld)
                                inst = V.tensor_tensor(
                                    t16[:, :], pb[bank][:, :],
                                    rstd2_bc[:, t * TT:(t + 1) * TT],
                                    ALU.mult)
                                fence(V, inst)
                                st6["bankfree"][bank] = (s_ve, s_ve.n)
                                oi = (it * NCH6 + m) % 2
                                of = st6["ofree"].get(oi)
                                if of is not None:
                                    V.wait_ge(of[0].h, of[1])
                                oc = (si * SLAB6) // 128 + m
                                inst = V.scalar_tensor_tensor(
                                    ob6[oi][:, :], t16[:, :],
                                    bb_sb[:, oc:oc + 1],
                                    x2res[xi][:, m * TT:(m + 1) * TT],
                                    ALU.add, ALU.add)
                                o_done = s_ve.inc(inst)
                                A.wait_ge(s_ve.h, o_done)
                                r0 = si * SLAB6 + m * 128
                                inst = A.dma_start(
                                    out_d[r0:r0 + 128, t * TT:(t + 1) * TT],
                                    ob6[oi][:, :])
                                s_st.inc(inst, 16)
                                st6["ofree"][oi] = (s_st, s_st.n)
                            st6["xfree"][xi] = (s_pe, s_pe.n)
                            st6["rfree"][xi] = (s_ve, s_ve.n)
                        st6["wfree"][wi] = (s_pe, s_pe.n)

            _finalize(nc, ALLE, SY, s_st)
    return nc


def _finalize(nc, ALLE, SY, s_st):
    with nc.body("fin", valid_engines=ALLE):
        if s_st.n > 0:
            SY.wait_ge(s_st.h, s_st.n)
        else:
            SY.nop()



def _deinterleave_perm():
    per_head = np.concatenate([np.arange(0, HD, 2), np.arange(1, HD, 2)])
    perm = np.arange(QCOLS + KCOLS + VCOLS)
    for h in range(NH):
        perm[h * HD:(h + 1) * HD] = h * HD + per_head
    for h in range(NKV):
        base = QCOLS + h * HD
        perm[base:base + HD] = base + per_head
    return perm


def _rope_tables(cols):
    inv = 10000.0 ** (-(np.arange(64, dtype=np.float64)) / 64.0)
    ang = cols[None, :].astype(np.float64) * inv[:, None]
    c = np.cos(ang)
    s = np.sin(ang)
    cr = np.ascontiguousarray(np.concatenate([c, c], axis=0)
                              .astype(np.float32))
    sg = np.ascontiguousarray(np.concatenate([-s, s], axis=0)
                              .astype(np.float32))
    return cr, sg


def _own_cols(parity):
    return np.concatenate([np.arange(g * TT, (g + 1) * TT)
                           for g in G_LISTS[parity]])


def _fm_chunks(v, n):
    return np.ascontiguousarray(v.reshape(n, 128).T)


def _prepare_in_maps(L0_attn_x, L0_attn_w_qkv, L0_attn_b_qkv, L0_attn_w_o,
                     L0_attn_b_o, L0_ffn_base_w, L0_ffn_base_b, L0_ffn_A,
                     L0_ffn_B, L0_ffn_router_w):
    x = np.ascontiguousarray(np.asarray(L0_attn_x, dtype=np.float32))
    w_qkv = np.asarray(L0_attn_w_qkv, dtype=np.float32)
    b_qkv = np.asarray(L0_attn_b_qkv, dtype=np.float32)
    w_o = np.asarray(L0_attn_w_o, dtype=np.float32)
    b_o = np.asarray(L0_attn_b_o, dtype=np.float32)
    base_w = np.asarray(L0_ffn_base_w, dtype=np.float32)
    base_b = np.asarray(L0_ffn_base_b, dtype=np.float32)
    A_ = np.asarray(L0_ffn_A, dtype=np.float32)
    B_ = np.asarray(L0_ffn_B, dtype=np.float32)
    router_w = np.asarray(L0_ffn_router_w, dtype=np.float32)

    perm = _deinterleave_perm()
    wq_p = w_qkv[perm]
    b_p = b_qkv[perm]
    wqkvT = np.ascontiguousarray(
        wq_p.T.reshape(NKC, 128, 12, SLAB).transpose(2, 1, 0, 3)
        .reshape(12 * 128, NKC * SLAB))
    bqk = _fm_chunks(b_p[:QCOLS + KCOLS], 40)
    bqks = np.ascontiguousarray(
        np.concatenate([bqk[64:128], bqk[0:64]], axis=0))
    bvrow = np.ascontiguousarray(b_p[QCOLS + KCOLS:][None, :])
    woT = np.ascontiguousarray(
        w_o.T.reshape(NKC, 128, 8, SLAB).transpose(2, 1, 0, 3)
        .reshape(8 * 128, NKC * SLAB))
    bo = _fm_chunks(b_o, 32)
    bwT = np.ascontiguousarray(
        base_w.T.reshape(NKC, 128, 16, SLAB6).transpose(2, 1, 0, 3)
        .reshape(16 * 128, NKC * SLAB6))
    bb = _fm_chunks(base_b, 32)
    art = np.ascontiguousarray(
        np.concatenate([A_.T, router_w.T], axis=1)
        .reshape(NKC, 128, 80).transpose(1, 0, 2).reshape(128, NKC * 80))
    bfm = np.ascontiguousarray(
        B_.transpose(0, 2, 1).reshape(16 * 64, D)
        .reshape(8, 128, 16, SLAB6).transpose(2, 1, 0, 3)
        .reshape(16 * 128, 8 * SLAB6))
    sel16 = np.zeros((16, 8 * 128), dtype=np.float32)
    for c in range(8):
        sel16[2 * c, c * 128:c * 128 + 64] = 1.0
        sel16[2 * c + 1, c * 128 + 64:(c + 1) * 128] = 1.0

    pp = np.arange(128)[:, None]
    ff = np.arange(TT)[None, :]
    msks = []
    for p in range(2):
        m = np.zeros((128, 32 * TT), dtype=np.float32)
        for t in range(NT_OWN):
            g = G_LISTS[p][t]
            nk = 4 * (2 * t + 2)
            for j in range(8):
                c = nk - 8 + j
                m[:, (t * 8 + j) * TT:(t * 8 + j + 1) * TT] = (
                    (512 * g + ff) >= (128 * c + pp)).astype(np.float32)
        msks.append(m)

    in_maps = []
    for core in range(8):
        b = core // 2
        p = core % 2
        cols = _own_cols(p)
        xTa = np.ascontiguousarray(x[b].T)
        xToa = np.ascontiguousarray(xTa[:, cols])
        cro, sgo = _rope_tables(cols)
        in_maps.append({
            "xTo": xToa, "wqkvT": wqkvT, "bqk": bqk, "bqks": bqks,
            "bvrow": bvrow, "woT": woT, "bo": bo, "bwT": bwT, "bb": bb,
            "art": art, "bfm": bfm, "cro": cro, "sgo": sgo,
            "msk": msks[p], "sel16": sel16,
        })

    return in_maps


def kernel(**inputs):
    in_maps = _prepare_in_maps(**inputs)
    nc = build_program()
    res = run_bass_kernel_spmd(nc, in_maps, core_ids=list(range(8)))

    out = np.empty((B, S, D), dtype=np.float32)
    for core in range(8):
        b = core // 2
        p = core % 2
        cols = _own_cols(p)
        out[b, cols, :] = res.results[core]["out"].T
    return out


if __name__ == "__main__":
    import reference
    inputs = reference.setup_inputs()
    got = kernel(**{k: np.asarray(v) for k, v in inputs.items()})
    want = np.asarray(reference.reference(**inputs))
    err = np.sqrt(((got - want) ** 2).mean() / (want ** 2).mean())
    print("rel l2:", err)


# version 13
# speedup vs baseline: 1.6744x; 1.0619x over previous
import contextlib
import math

import numpy as np
import concourse.bass as bass
import concourse.mybir as mybir
from concourse.bass_utils import run_bass_kernel_spmd

F32 = mybir.dt.float32
BF16 = mybir.dt.bfloat16
I32 = mybir.dt.int32
AF = mybir.ActivationFunctionType
ALU = mybir.AluOpType
AX = mybir.AxisListType.X

D = 4096
S = 4096
B = 4
NH = 32
NKV = 8
HD = 128
QCOLS = 4096
KCOLS = 1024
VCOLS = 1024
TT = 512
NT_FULL = S // TT
NT_OWN = NT_FULL // 2
NKC = D // 128
SLAB = 512
SLAB6 = 256
G_LISTS = [[0, 3, 4, 7], [1, 2, 5, 6]]
INV_SQRT_HD = 1.0 / math.sqrt(HD)
LN_EPS = 1e-5

_DEBUG_TAPS = []
_STOP_AFTER = None
_FORCE_PARITY = None
_ATTN_LIMIT = None


class Sem:
    def __init__(self, h):
        self.h = h
        self.n = 0

    def inc(self, inst, k=1):
        inst.then_inc(self.h, k)
        self.n += k
        return self.n


def build_program():
    nc = bass.Bass("TRN2", target_bir_lowering=False)
    ALLE = list(mybir.ALL_ENGINES)

    def din(name, shape):
        return nc.dram_tensor(name, shape, F32, kind="ExternalInput")

    xTo = din("xTo", [D, S // 2])
    wqkvT = din("wqkvT", [12 * 128, NKC * SLAB])
    bqk_d = din("bqk", [128, 40])
    bqks_d = din("bqks", [128, 40])
    bvrow_d = din("bvrow", [1, VCOLS])
    woT = din("woT", [8 * 128, NKC * SLAB])
    bo_d = din("bo", [128, 32])
    bwT = din("bwT", [16 * 128, NKC * SLAB6])
    bb_d = din("bb", [128, 32])
    art_d = din("art", [128, NKC * 80])
    bfm_d = din("bfm", [16 * 128, 8 * SLAB6])
    cro_d = din("cro", [128, S // 2])
    sgo_d = din("sgo", [128, S // 2])
    msk_d = din("msk", [128, 32 * TT])
    sel16_d = din("sel16", [16, 8 * 128])

    out_d = nc.dram_tensor("out", [D, S // 2], F32, kind="ExternalOutput")

    def scratch(name, shape, dtype):
        kind = "ExternalOutput" if name in _DEBUG_TAPS else "Internal"
        return nc.dram_tensor(name, shape, dtype, kind=kind)

    xbfo = scratch("xbfo", [NT_OWN * 128, NKC * TT], BF16)
    qs = scratch("qs", [QCOLS, S // 2], BF16)
    ks = scratch("ks", [KCOLS, S], BF16)
    vs = scratch("vs", [NKV * 128, 32 * 128], BF16)
    ks_own = scratch("ks_own", [KCOLS, S // 2], BF16)
    vs_own = scratch("vs_own", [NKV * 128, 16 * 128], BF16)
    ks_gath = scratch("ks_gath", [2 * KCOLS, S // 2], BF16)
    vs_gath = scratch("vs_gath", [2 * NKV * 128, 16 * 128], BF16)
    nmax_d = scratch("nmax_d", [1, 4], F32)
    nmax_g = scratch("nmax_g", [1, 4], F32)
    aos = scratch("aos", [NT_OWN * 128, 32 * TT], BF16)
    x2s = scratch("x2s", [D, S // 2], F32)
    x2bf = scratch("x2bf", [NT_OWN * 128, NKC * TT], BF16)

    ctx = contextlib.ExitStack()
    with ctx:
        E = ctx.enter_context

        s_sy = Sem(E(nc.semaphore("s_sy")))
        s_a = Sem(E(nc.semaphore("s_a")))
        s_g = Sem(E(nc.semaphore("s_g")))
        s_pe = Sem(E(nc.semaphore("s_pe")))
        s_ve = Sem(E(nc.semaphore("s_ve")))
        s_ac = Sem(E(nc.semaphore("s_ac")))
        s_gp = Sem(E(nc.semaphore("s_gp")))
        s_bar = Sem(E(nc.semaphore("s_bar")))
        s_st = Sem(E(nc.semaphore("s_st")))
        s_cc = Sem(E(nc.semaphore("s_cc")))
        ALL_SEMS = [s_sy, s_a, s_g, s_pe, s_ve, s_ac, s_gp, s_bar, s_st,
                    s_cc]

        T, V, A, SY, G = nc.tensor, nc.vector, nc.scalar, nc.sync, nc.gpsimd
        DMA_SEM = {id(SY): s_sy, id(A): s_a, id(G): s_g}
        ENG_SEM = {id(T): s_pe, id(V): s_ve, id(A): s_ac, id(G): s_gp}

        def dmai(q, out_ap, in_ap, **kw):
            sem = DMA_SEM[id(q)]
            inst = q.dma_start(out_ap, in_ap, **kw)
            sem.inc(inst, 16)
            return sem.n

        def fence(eng, inst):
            sem = ENG_SEM[id(eng)]
            sem.inc(inst)
            eng.wait_ge(sem.h, sem.n)

        def barrier():
            for eng in (T, V, A, SY, G):
                dsem = DMA_SEM.get(id(eng))
                if dsem is not None:
                    eng.wait_ge(dsem.h, dsem.n)
                s_bar.inc(eng.nop())
            tgt = s_bar.n
            for eng in (T, V, A, SY, G):
                eng.wait_ge(s_bar.h, tgt)

        pb = [E(nc.psum_tensor(f"pb{i}", [128, 512], F32)) for i in range(8)]

        ones_col_bf = E(nc.sbuf_tensor("ones_col_bf", [128, 1], BF16))
        ones_row_bf = E(nc.sbuf_tensor("ones_row_bf", [1, 128], BF16))
        ones_row_f = E(nc.sbuf_tensor("ones_row_f", [1, 128], F32))
        sel_lo = E(nc.sbuf_tensor("sel_lo", [1, 128], BF16))
        sel_hi = E(nc.sbuf_tensor("sel_hi", [1, 128], BF16))
        bqk_sb = E(nc.sbuf_tensor("bqk_sb", [128, 40], F32))
        bqks_sb = E(nc.sbuf_tensor("bqks_sb", [128, 40], F32))
        bv_sb = E(nc.sbuf_tensor("bv_sb", [1, VCOLS], BF16))
        bo_sb = E(nc.sbuf_tensor("bo_sb", [128, 32], F32))
        bb_sb = E(nc.sbuf_tensor("bb_sb", [128, 32], F32))
        eye_sb = E(nc.sbuf_tensor("eye_sb", [128, 128], F32))
        eps_sb = E(nc.sbuf_tensor("eps_sb", [1, 1], F32))
        nmax = E(nc.sbuf_tensor("nmax", [1, 4], F32))
        wsl = [E(nc.sbuf_tensor(f"wsl{i}", [128, NKC * SLAB], BF16))
               for i in range(2)]
        sel16_sb = E(nc.sbuf_tensor("sel16_sb", [16, 8 * 128], BF16))

        with nc.Block() as block0:
            @block0.tensor
            def _(e):
                e.nop()

            @block0.vector
            def _(e):
                e.nop()

            @block0.scalar
            def _(e):
                e.nop()

            @block0.sync
            def _(e):
                e.nop()

            @block0.gpsimd
            def _(e):
                e.nop()

        c0s = contextlib.ExitStack()
        eye_tmp = c0s.enter_context(nc.sbuf_tensor("eye_tmp", [128, 256], F32))
        with nc.body("p0", valid_engines=ALLE):
            G.memset(ones_col_bf[:, :], 1.0)
            G.memset(eps_sb[:, :], LN_EPS)
            G.memset(ones_row_bf[:, :], 1.0)
            G.memset(ones_row_f[:, :], 1.0)
            G.memset(sel_lo[0:1, 0:64], 1.0)
            G.memset(sel_lo[0:1, 64:128], 0.0)
            G.memset(sel_hi[0:1, 0:64], 0.0)
            G.memset(sel_hi[0:1, 64:128], 1.0)
            G.memset(nmax[:, :], 0.0)
            idn_i = c0s.enter_context(nc.sbuf_tensor("idn_i", [128, 128], I32))
            inst = G.iota(idn_i[:, :], pattern=[[1, 128]], base=0,
                          channel_multiplier=-1)
            s_gp.inc(inst)
            V.wait_ge(s_gp.h, s_gp.n)
            inst = V.tensor_copy(eye_tmp[:, 0:128], idn_i[:, :])
            fence(V, inst)
            inst = V.tensor_scalar(eye_tmp[:, 128:256], eye_tmp[:, 0:128],
                                   0.5, None, ALU.is_lt)
            fence(V, inst)
            inst = V.tensor_scalar(eye_tmp[:, 0:128], eye_tmp[:, 0:128],
                                   -0.5, None, ALU.is_ge)
            fence(V, inst)
            inst = V.tensor_tensor(eye_sb[:, :], eye_tmp[:, 0:128],
                                   eye_tmp[:, 128:256], ALU.mult)
            s_ve.inc(inst)
            dmai(A, bqk_sb[:, :], bqk_d[:, :])
            dmai(A, bqks_sb[:, :], bqks_d[:, :])
            dmai(G, bv_sb[:, :], bvrow_d[:, :])
            dmai(G, sel16_sb[:, :], sel16_d[:, :])
            dmai(A, bo_sb[:, :], bo_d[:, :])
            dmai(A, bb_sb[:, :], bb_d[:, :])
            xcast_v = []
            for i in range(4):
                xcast_v.append(dmai(G, xbfo[i * 128:(i + 1) * 128, :],
                                    xTo[:, i * TT:(i + 1) * TT]
                                    .rearrange("(c p) n -> p c n", p=128)))
            qkv_w0_ld = dmai(G, wsl[0][:, :],
                             wqkvT[8 * 128:9 * 128, :])
        c0s.close()

        with contextlib.ExitStack() as c12:
            E12 = c12.enter_context
            negmo_bf = E12(nc.sbuf_tensor("negmo_bf", [1, S // 2], BF16))
            rstdo_bc = E12(nc.sbuf_tensor("rstdo_bc", [128, S // 2], F32))
            rstd_col = E12(nc.sbuf_tensor("rstd_col", [128, 16], F32))
            colsum_bf = E12(nc.sbuf_tensor("colsum_bf", [1, SLAB], BF16))
            colsum_v_bf = E12(nc.sbuf_tensor("colsum_v_bf", [1, SLAB], BF16))

            with nc.body("p1", valid_engines=ALLE):
                with contextlib.ExitStack() as c1:
                    E1 = c1.enter_context
                    xtile = [E1(nc.sbuf_tensor(f"st_x{i}", [128, NKC * TT],
                                               BF16)) for i in range(2)]
                    QK1 = NKC // 4
                    xsq = [E1(nc.sbuf_tensor(f"st_xsq{i}", [128, QK1 * TT],
                                             BF16)) for i in range(2)]
                    rowt = E1(nc.sbuf_tensor("rowt", [1, 2 * TT], F32))

                    prev = {"rows_read": None, "xfree": {},
                            "sq_rd": {}, "bc_read": None, "n": 0}

                    def stats_pass(src, n_tiles, negm_row, rstd_bcast,
                                   do_cols, cast_base):
                        for t in range(n_tiles):
                            bi = prev["n"] % 2
                            prev["n"] += 1
                            SY.wait_ge(s_g.h, xcast_v[cast_base + t])
                            xf = prev["xfree"].get(bi)
                            if xf is not None:
                                SY.wait_ge(xf[0].h, xf[1])
                            ld = dmai(SY, xtile[bi][:, :],
                                      src[t * 128:(t + 1) * 128, :])
                            V.wait_ge(s_sy.h, ld)
                            sqv = []
                            for q in range(4):
                                rd = prev["sq_rd"].get(q % 2)
                                if rd is not None:
                                    V.wait_ge(rd[0].h, rd[1])
                                inst = V.tensor_tensor(
                                    xsq[q % 2][:, :],
                                    xtile[bi][:, q * QK1 * TT:
                                              (q + 1) * QK1 * TT],
                                    xtile[bi][:, q * QK1 * TT:
                                              (q + 1) * QK1 * TT],
                                    ALU.mult)
                                sqv.append(s_ve.inc(inst))
                            T.wait_ge(s_sy.h, ld)
                            if prev["rows_read"] is not None:
                                T.wait_ge(prev["rows_read"][0].h,
                                          prev["rows_read"][1])
                            for k in range(NKC):
                                inst = T.matmul(
                                    pb[4][0:1, :], ones_col_bf[:, :],
                                    xtile[bi][:, k * TT:(k + 1) * TT],
                                    start=(k == 0), stop=(k == NKC - 1))
                            for q in range(4):
                                T.wait_ge(s_ve.h, sqv[q])
                                for k in range(QK1):
                                    inst = T.matmul(
                                        pb[5][0:1, :], ones_col_bf[:, :],
                                        xsq[q % 2][:, k * TT:(k + 1) * TT],
                                        start=(q == 0 and k == 0),
                                        stop=(q == 3 and k == QK1 - 1))
                                prev["sq_rd"][q % 2] = (s_pe, s_pe.inc(inst))
                            sums_done = s_pe.n
                            prev["xfree"][bi] = (s_pe, sums_done)
                            V.wait_ge(s_pe.h, sums_done)
                            inst = V.tensor_scalar(
                                negm_row[0:1, t * TT:(t + 1) * TT],
                                pb[4][0:1, :], -1.0 / D, None, ALU.mult)
                            fence(V, inst)
                            inst = V.tensor_scalar(rowt[0:1, 0:TT],
                                                   pb[5][0:1, :], 1.0 / D,
                                                   None, ALU.mult)
                            fence(V, inst)
                            prev["rows_read"] = (s_ve, s_ve.n)
                            inst = V.tensor_tensor(
                                rowt[0:1, TT:2 * TT],
                                negm_row[0:1, t * TT:(t + 1) * TT],
                                negm_row[0:1, t * TT:(t + 1) * TT], ALU.mult)
                            fence(V, inst)
                            inst = V.tensor_tensor(rowt[0:1, 0:TT],
                                                   rowt[0:1, 0:TT],
                                                   rowt[0:1, TT:2 * TT],
                                                   ALU.subtract)
                            var_done = s_ve.inc(inst)
                            A.wait_ge(s_ve.h, var_done)
                            inst = A.activation(rowt[0:1, TT:2 * TT],
                                                rowt[0:1, 0:TT],
                                                AF.Ln, bias=eps_sb[0:1, 0:1],
                                                scale=1.0)
                            fence(A, inst)
                            inst = A.activation(rowt[0:1, 0:TT],
                                                rowt[0:1, TT:2 * TT],
                                                AF.Exp, scale=-0.5)
                            rstd_done = s_ac.inc(inst)
                            T.wait_ge(s_ac.h, rstd_done)
                            if prev["bc_read"] is not None:
                                T.wait_ge(prev["bc_read"][0].h,
                                          prev["bc_read"][1])
                            inst = T.matmul(pb[6][:, :], ones_row_f[0:1, :],
                                            rowt[0:1, 0:TT], start=True,
                                            stop=True)
                            bc_done = s_pe.inc(inst)
                            A.wait_ge(s_pe.h, bc_done)
                            inst = A.copy(rstd_bcast[:, t * TT:(t + 1) * TT],
                                          pb[6][:, :])
                            bc_ev = s_ac.inc(inst)
                            prev["bc_read"] = (s_ac, bc_ev)
                            if do_cols:
                                A.wait_ge(s_ac.h, bc_ev)
                                for c in range(4):
                                    cc = 4 * t + c
                                    dmai(A, rstd_col[:, cc:cc + 1],
                                         rowt[0:1, c * 128:(c + 1) * 128])
                            V.wait_ge(s_a.h, s_a.n)
                            V.wait_ge(s_ac.h, bc_ev)

                    stats_pass(xbfo, NT_OWN, negmo_bf, rstdo_bc, True, 0)
                    barrier()

            with nc.body("p2", valid_engines=ALLE):
                with contextlib.ExitStack() as c2:
                    E2 = c2.enter_context
                    wbuf = wsl
                    crs = E2(nc.sbuf_tensor("crs", [128, S // 2], F32))
                    sgs = E2(nc.sbuf_tensor("sgs", [128, S // 2], F32))
                    xtile = [E2(nc.sbuf_tensor(f"qk_x{i}", [128, NKC * TT],
                                               BF16)) for i in range(2)]
                    t1b = [E2(nc.sbuf_tensor(f"qk_t1{i}", [128, TT], F32))
                           for i in range(2)]
                    t1sb = [E2(nc.sbuf_tensor(f"qk_t1s{i}", [128, TT], F32))
                            for i in range(2)]
                    r1b = E2(nc.sbuf_tensor("qk_r1", [128, TT], F32))
                    outb = [E2(nc.sbuf_tensor(f"qk_o{i}", [128, TT], BF16))
                            for i in range(2)]
                    nsq = [E2(nc.sbuf_tensor(f"qk_nsq{i}", [128, TT], BF16))
                           for i in range(2)]

                    st2 = {"wfree": {}, "xfree": {}, "t1free": {},
                           "t1sfree": {}, "outfree": {}, "bankfree": {},
                           "cs_read": None, "rope": None, "rope_ld": 0,
                           "rope_rb": 0, "rope_use": {}, "pre": {},
                           "slab": 0, "tile": 0, "mm": 0, "ep": 0,
                           "nrm_pend": None, "nsqfree": {}, "nrmfree": None}

                    def flush_norm():
                        if st2["nrm_pend"] is None:
                            return
                        pbuf, psq, pcell = st2["nrm_pend"]
                        st2["nrm_pend"] = None
                        T.wait_ge(s_ac.h, psq)
                        nr = st2["nrmfree"]
                        if nr is not None:
                            T.wait_ge(nr[0].h, nr[1])
                        inst = T.matmul(pb[5][0:1, :], ones_col_bf[:, :],
                                        nsq[pbuf][:, :], start=True,
                                        stop=True)
                        nrm_done = s_pe.inc(inst)
                        st2["nsqfree"][pbuf] = (s_pe, nrm_done)
                        V.wait_ge(s_pe.h, nrm_done)
                        inst = V.reduce_max(nmax[0:1, 2:3], pb[5][0:1, :],
                                            axis=AX)
                        fence(V, inst)
                        st2["nrmfree"] = (s_ve, s_ve.n)
                        inst = V.tensor_tensor(
                            nmax[0:1, pcell:pcell + 1],
                            nmax[0:1, pcell:pcell + 1],
                            nmax[0:1, 2:3], ALU.max)
                        fence(V, inst)

                    def rope_ensure(key):
                        if st2["rope"] == key:
                            return st2["rope_ld"], 0
                        ru = st2["rope_use"].get(0)
                        if ru is not None:
                            SY.wait_ge(ru[0].h, ru[1])
                        assert key[0] == "own"
                        dmai(SY, crs[:, :], cro_d[:, :])
                        ld = dmai(SY, sgs[:, :], sgo_d[:, :])
                        st2["rope"] = key
                        st2["rope_ld"] = ld
                        return ld, 0

                    def load_slab(c0):
                        si = st2["slab"]
                        st2["slab"] += 1
                        bi = si % 2
                        if si == 0:
                            return (s_g, qkv_w0_ld), bi
                        if si in st2["pre"]:
                            return st2["pre"].pop(si)
                        wf = st2["wfree"].get(bi)
                        if wf is not None:
                            G.wait_ge(wf[0].h, wf[1])
                        gsi = c0 // SLAB
                        ld = dmai(G, wbuf[bi][:, :],
                                  wqkvT[gsi * 128:(gsi + 1) * 128, :])
                        return (s_g, ld), bi

                    def colsums(wi, w_ld, cs_bf):
                        T.wait_ge(w_ld[0].h, w_ld[1])
                        if st2["cs_read"] is not None:
                            T.wait_ge(st2["cs_read"][0].h, st2["cs_read"][1])
                        for k in range(NKC):
                            inst = T.matmul(
                                pb[4][0:1, :], ones_col_bf[:, :],
                                wbuf[wi][:, k * SLAB:(k + 1) * SLAB],
                                start=(k == 0), stop=(k == NKC - 1))
                        done = s_pe.inc(inst)
                        A.wait_ge(s_pe.h, done)
                        inst = A.copy(cs_bf[0:1, :], pb[4][0:1, :])
                        v = s_ac.inc(inst)
                        st2["cs_read"] = (s_ac, v)
                        return v

                    def load_xtile(src, t):
                        n = st2["tile"]
                        st2["tile"] += 1
                        bi = n % 2
                        xf = st2["xfree"].get(bi)
                        if xf is not None:
                            SY.wait_ge(xf[0].h, xf[1])
                        ld = dmai(SY, xtile[bi][:, :],
                                  src[t * 128:(t + 1) * 128, :])
                        return ld, bi

                    def qk_slabs(col_base, n_slabs, src, n_tiles, negm_row,
                                 rstd_bcast, out_scr, bias0, rope_of_t,
                                 ncell):
                        for si in range(n_slabs):
                            c0 = col_base + si * SLAB
                            w_ld, wi = load_slab(c0)
                            cs_done = colsums(wi, w_ld, colsum_bf)
                            for t in range(n_tiles):
                                rope_ld, rb = rope_ensure(rope_of_t(t))
                                x_ld, xi = load_xtile(src, t)
                                for m in range(4):
                                    bank = st2["mm"] % 4
                                    st2["mm"] += 1
                                    T.wait_ge(s_sy.h, x_ld)
                                    T.wait_ge(w_ld[0].h, w_ld[1])
                                    T.wait_ge(s_ac.h, cs_done)
                                    bf_ = st2["bankfree"].get(bank)
                                    if bf_ is not None:
                                        T.wait_ge(bf_[0].h, bf_[1])
                                    for k in range(NKC):
                                        T.matmul(
                                            pb[bank][:, :],
                                            wbuf[wi][:, k * SLAB + m * 128:
                                                     k * SLAB + (m + 1) * 128],
                                            xtile[xi][:, k * TT:(k + 1) * TT],
                                            start=(k == 0), stop=False)
                                    inst = T.matmul(
                                        pb[bank][:, :],
                                        colsum_bf[0:1, m * 128:(m + 1) * 128],
                                        negm_row[0:1, t * TT:(t + 1) * TT],
                                        start=False, stop=True)
                                    mm_done = s_pe.inc(inst)
                                    ti = st2["ep"] % 2
                                    st2["ep"] += 1
                                    V.wait_ge(s_pe.h, mm_done)
                                    for key in ("t1free", "outfree",
                                                "t1sfree"):
                                        f = st2[key].get(ti)
                                        if f is not None:
                                            V.wait_ge(f[0].h, f[1])
                                    inst = V.tensor_tensor(
                                        t1b[ti][:, :], pb[bank][:, :],
                                        rstd_bcast[:, t * TT:(t + 1) * TT],
                                        ALU.mult)
                                    V.tensor_tensor(
                                        t1sb[ti][0:64, :],
                                        pb[bank][64:128, :],
                                        rstd_bcast[64:128,
                                                   t * TT:(t + 1) * TT],
                                        ALU.mult)
                                    inst = V.tensor_tensor(
                                        t1sb[ti][64:128, :],
                                        pb[bank][0:64, :],
                                        rstd_bcast[0:64, t * TT:(t + 1) * TT],
                                        ALU.mult)
                                    t1_done = s_ve.inc(inst)
                                    V.wait_ge(s_ve.h, t1_done)
                                    st2["bankfree"][bank] = (s_ve, t1_done)
                                    cchunk = bias0 + 4 * si + m
                                    gc = (t * TT) % (S // 2)
                                    V.wait_ge(s_sy.h, rope_ld)
                                    inst = V.scalar_tensor_tensor(
                                        r1b[:, :], t1b[ti][:, :],
                                        bqk_sb[:, cchunk:cchunk + 1],
                                        crs[:, gc:gc + TT],
                                        ALU.add, ALU.mult)
                                    fence(V, inst)
                                    st2["t1free"][ti] = (s_ve, s_ve.n)
                                    inst = V.scalar_tensor_tensor(
                                        t1sb[ti][:, :], t1sb[ti][:, :],
                                        bqks_sb[:, cchunk:cchunk + 1],
                                        sgs[:, gc:gc + TT],
                                        ALU.add, ALU.mult)
                                    fence(V, inst)
                                    st2["rope_use"][0] = (s_ve, s_ve.n)
                                    inst = V.tensor_tensor(outb[ti][:, :],
                                                           r1b[:, :],
                                                           t1sb[ti][:, :],
                                                           ALU.add)
                                    rp_done = s_ve.inc(inst)
                                    st2["t1sfree"][ti] = (s_ve, rp_done)
                                    nf = st2["nsqfree"].get(ti)
                                    if nf is not None:
                                        A.wait_ge(nf[0].h, nf[1])
                                    A.wait_ge(s_ve.h, rp_done)
                                    inst = A.activation(nsq[ti][:, :],
                                                        outb[ti][:, :],
                                                        AF.Square)
                                    sq_done = s_ac.inc(inst)
                                    r0 = si * SLAB + m * 128
                                    stv = dmai(A, out_scr[r0:r0 + 128,
                                                          t * TT:(t + 1) * TT],
                                               outb[ti][:, :])
                                    st2["outfree"][ti] = (s_a, stv)
                                    flush_norm()
                                    st2["nrm_pend"] = (ti, sq_done, ncell)
                                st2["xfree"][xi] = (s_pe, s_pe.n)
                            st2["wfree"][wi] = (s_pe, s_pe.n)

                    qk_slabs(QCOLS, KCOLS // SLAB, xbfo, NT_OWN, negmo_bf,
                             rstdo_bc, ks_own, 32, lambda t: ("own", 0), 1)

                    for si in range(VCOLS // SLAB):
                        c0 = QCOLS + KCOLS + si * SLAB
                        w_ld, wi = load_slab(c0)
                        cs_done = colsums(wi, w_ld, colsum_v_bf)
                        for t in range(NT_OWN):
                            x_ld, xi = load_xtile(xbfo, t)
                            for m in range(4):
                                bank = st2["mm"] % 4
                                st2["mm"] += 1
                                T.wait_ge(s_sy.h, x_ld)
                                T.wait_ge(w_ld[0].h, w_ld[1])
                                T.wait_ge(s_ac.h, cs_done)
                                bf_ = st2["bankfree"].get(bank)
                                if bf_ is not None:
                                    T.wait_ge(bf_[0].h, bf_[1])
                                for k in range(NKC):
                                    T.matmul(
                                        pb[bank][:, :],
                                        xtile[xi][:, k * TT + m * 128:
                                                  k * TT + (m + 1) * 128],
                                        wbuf[wi][:, k * SLAB:(k + 1) * SLAB],
                                        start=(k == 0), stop=False)
                                tok0 = t * TT + m * 128
                                T.matmul(pb[bank][:, :],
                                         negmo_bf[0:1, tok0:tok0 + 128],
                                         colsum_v_bf[0:1, :],
                                         start=False, stop=False)
                                inst = T.matmul(
                                    pb[bank][:, :], ones_row_bf[0:1, :],
                                    bv_sb[0:1, si * SLAB:(si + 1) * SLAB],
                                    start=False, stop=True)
                                mm_done = s_pe.inc(inst)
                                ti = st2["ep"] % 2
                                st2["ep"] += 1
                                V.wait_ge(s_pe.h, mm_done)
                                of = st2["outfree"].get(ti)
                                if of is not None:
                                    V.wait_ge(of[0].h, of[1])
                                cc = 4 * t + m
                                inst = V.tensor_scalar(outb[ti][:, :],
                                                       pb[bank][:, :],
                                                       rstd_col[:, cc:cc + 1],
                                                       None, ALU.mult)
                                ep_done = s_ve.inc(inst)
                                st2["bankfree"][bank] = (s_ve, ep_done)
                                A.wait_ge(s_ve.h, ep_done)
                                stv = dmai(A, vs_own[4 * si * 128:
                                                     (4 * si + 4) * 128,
                                                     cc * 128:(cc + 1) * 128]
                                           .rearrange("(k p) v -> p k v",
                                                      p=128),
                                           outb[ti][:, :]
                                           .rearrange("p (k v) -> p k v",
                                                      k=4))
                                st2["outfree"][ti] = (s_a, stv)
                            st2["xfree"][xi] = (s_pe, s_pe.n)
                        st2["wfree"][wi] = (s_pe, s_pe.n)

                    for j in range(2):
                        si_p = st2["slab"] + j
                        bi_p = si_p % 2
                        wf = st2["wfree"].get(bi_p)
                        if wf is not None:
                            G.wait_ge(wf[0].h, wf[1])
                        ld_p = dmai(G, wbuf[bi_p][:, :],
                                    wqkvT[j * 128:(j + 1) * 128, :])
                        st2["pre"][si_p] = ((s_g, ld_p), bi_p)

                    G.wait_ge(s_a.h, s_a.n)
                    pairs_rg = [[2 * b, 2 * b + 1] for b in range(4)]
                    inst = G.collective_compute(
                        "AllGather", ALU.bypass, replica_groups=pairs_rg,
                        ins=[ks_own.ap().opt()], outs=[ks_gath.ap().opt()])
                    s_cc.inc(inst, 1)
                    inst = G.collective_compute(
                        "AllGather", ALU.bypass, replica_groups=pairs_rg,
                        ins=[vs_own.ap().opt()], outs=[vs_gath.ap().opt()])
                    s_cc.inc(inst, 1)
                    G.wait_ge(s_cc.h, s_cc.n)
                    for g in range(8):
                        r = 0 if g in G_LISTS[0] else 1
                        gi = G_LISTS[r].index(g)
                        dmai(G, ks[:, g * TT:(g + 1) * TT],
                             ks_gath[r * KCOLS:(r + 1) * KCOLS,
                                     gi * TT:(gi + 1) * TT])
                        dmai(G, vs[:, 4 * g * 128:(4 * g + 4) * 128],
                             vs_gath[r * NKV * 128:(r + 1) * NKV * 128,
                                     4 * gi * 128:(4 * gi + 4) * 128])

                    qk_slabs(0, QCOLS // SLAB, xbfo, NT_OWN, negmo_bf,
                             rstdo_bc, qs, 0, lambda t: ("own", 0), 0)
                    flush_norm()

                    inst = V.nop()
                    nm_cons = s_ve.inc(inst)
                    A.wait_ge(s_ve.h, nm_cons)
                    nm_st = dmai(A, nmax_d[:, :], nmax[0:1, :])
                    G.wait_ge(s_a.h, nm_st)
                    inst = G.collective_compute(
                        "AllReduce", ALU.max, replica_groups=pairs_rg,
                        ins=[nmax_d.ap().opt()], outs=[nmax_g.ap().opt()])
                    s_cc.inc(inst, 1)
                    G.wait_ge(s_cc.h, s_cc.n)
                    dmai(G, nmax[0:1, :], nmax_g[:, :])
                    barrier()

        if _STOP_AFTER == "qkv":
            _finalize(nc, ALLE, SY, s_st)
            return nc

        with contextlib.ExitStack() as c3:
            E3 = c3.enter_context
            k_sb = [E3(nc.sbuf_tensor(f"at_k{i}", [128, S], BF16))
                    for i in range(2)]
            v_sb = [E3(nc.sbuf_tensor(f"at_v{i}", [128, 32 * 128], BF16))
                    for i in range(2)]
            q_sb = [E3(nc.sbuf_tensor(f"at_q{i}", [128, S // 2], BF16))
                    for i in range(2)]
            probs = [E3(nc.sbuf_tensor(f"at_p{i}", [128, TT], BF16))
                     for i in range(8)]
            probm = [E3(nc.sbuf_tensor(f"at_pm{i}", [128, TT], BF16))
                     for i in range(8)]
            mask_bf = E3(nc.sbuf_tensor("at_mb", [128, 32 * TT], BF16))
            gbias = E3(nc.sbuf_tensor("at_gb", [128, 1], F32))
            grow = E3(nc.sbuf_tensor("at_gr", [1, 8], F32))
            dent = [E3(nc.sbuf_tensor(f"at_dn{i}", [1, TT], F32))
                    for i in range(2)]
            rdbc = [E3(nc.sbuf_tensor(f"at_rb{i}", [128, TT], F32))
                    for i in range(2)]
            ao_sb = [E3(nc.sbuf_tensor(f"at_ao{i}", [128, TT], BF16))
                     for i in range(2)]


            pref = {}

            def attn_body():
                mask_q = []
                for j in range(4):
                    mask_q.append(
                        dmai(G, mask_bf[:, j * 8 * TT:(j + 1) * 8 * TT],
                             msk_d[:, j * 8 * TT:(j + 1) * 8 * TT]))
                inst = V.tensor_tensor(grow[0:1, 0:1], nmax[0:1, 0:1],
                                       nmax[0:1, 1:2], ALU.mult)
                fence(V, inst)
                A.wait_ge(s_ve.h, s_ve.n)
                inst = A.activation(grow[0:1, 1:2], grow[0:1, 0:1], AF.Sqrt)
                fence(A, inst)
                V.wait_ge(s_ac.h, s_ac.n)
                inst = V.tensor_scalar(grow[0:1, 2:3], grow[0:1, 1:2], 2.0,
                                       -INV_SQRT_HD, ALU.add, ALU.mult)
                gb_done = s_ve.inc(inst)
                T.wait_ge(s_ve.h, gb_done)
                inst = T.matmul(pb[0][:, 0:1], ones_row_f[0:1, :],
                                grow[0:1, 2:3], start=True, stop=True)
                gb_bc = s_pe.inc(inst)
                A.wait_ge(s_pe.h, gb_bc)
                inst = A.copy(gbias[:, :], pb[0][:, 0:1])
                gb_cp = s_ac.inc(inst)
                A.wait_ge(s_ac.h, gb_cp)

                lim = _ATTN_LIMIT or (NKV, 4, NT_OWN)
                st3 = {"bankfree": {0: (s_ac, gb_cp)}, "probfree": {},
                       "pvfree": {},
                       "denfree": {}, "aofree": {}, "rdfree": {},
                       "denbfree": {}, "pending": None,
                       "pending_v": None, "rdb": None, "accfree": None}
                pairs = [(kvh, qi) for kvh in range(lim[0])
                         for qi in range(lim[1])]
                k_ld = {}
                v_ld = {}
                q_ld = {}
                pe_end = {}
                q_end = {}

                def load_kv(j):
                    if j >= lim[0]:
                        return
                    bi = j % 2
                    if j >= 2:
                        SY.wait_ge(s_pe.h, pe_end[j - 2])
                    k_ld[j] = dmai(SY, k_sb[bi][:, :],
                                   ks[j * 128:(j + 1) * 128, :])
                    v_ld[j] = dmai(SY, v_sb[bi][:, :],
                                   vs[j * 128:(j + 1) * 128, :])

                def load_q(j):
                    if j >= len(pairs):
                        return
                    kvh_, qi_ = pairs[j]
                    qh_ = kvh_ * 4 + qi_
                    bi = j % 2
                    if j >= 2:
                        SY.wait_ge(s_pe.h, q_end[j - 2])
                    q_ld[j] = dmai(SY, q_sb[bi][:, :],
                                   qs[qh_ * 128:(qh_ + 1) * 128, :])

                def mk_epi1(ti, D, den_done):
                    def epi1():
                        ei = ti % 2
                        V.wait_ge(s_pe.h, den_done)
                        dbf = st3["denbfree"].get(ei)
                        if dbf is not None:
                            V.wait_ge(dbf[0].h, dbf[1])
                        inst = V.tensor_copy(dent[ei][0:1, :],
                                             pb[D][0:1, :])
                        rdb = s_ve.inc(inst)
                        return rdb
                    return epi1

                def mk_epi2(ti, P, D, qh, t):
                    def epi2(rdb):
                        ei = ti % 2
                        T.wait_ge(s_ve.h, rdb)
                        inst = T.matmul(pb[D][:, :], ones_row_f[0:1, :],
                                        dent[ei][0:1, :], start=True,
                                        stop=True)
                        bc_done = s_pe.inc(inst)
                        st3["denbfree"][ei] = (s_pe, bc_done)
                        V.wait_ge(s_pe.h, bc_done)
                        rf = st3["rdfree"].get(ei)
                        if rf is not None:
                            V.wait_ge(rf[0].h, rf[1])
                        inst = V.reciprocal(rdbc[ei][:, :], pb[D][:, :])
                        rbc_done = s_ve.inc(inst)
                        st3["denfree"][D] = (s_ve, rbc_done)
                        V.wait_ge(s_ve.h, rbc_done)
                        af = st3["aofree"].get(ei)
                        if af is not None:
                            V.wait_ge(af[0].h, af[1])
                        inst = V.tensor_tensor(ao_sb[ei][:, :], pb[P][:, :],
                                               rdbc[ei][:, :], ALU.mult)
                        ao_done = s_ve.inc(inst)
                        st3["pvfree"][P] = (s_ve, ao_done)
                        st3["rdfree"][ei] = (s_ve, ao_done)
                        G.wait_ge(s_ve.h, ao_done)
                        stv = dmai(G, aos[t * 128:(t + 1) * 128,
                                          qh * TT:(qh + 1) * TT],
                                   ao_sb[ei][:, :])
                        st3["aofree"][ei] = (s_g, stv)
                    return epi2

                load_kv(0)
                load_kv(1)
                load_q(0)
                load_q(1)

                for idx, (kvh, qi) in enumerate(pairs):
                    qh = kvh * 4 + qi
                    if idx == 2:
                        pref["p4w0"] = dmai(G, wsl[0][:, :], woT[0:128, :])
                    if qi == 0 and kvh >= 1:
                        load_kv(kvh + 1)
                    if idx >= 1:
                        load_q(idx + 1)
                    for t in range(lim[2]):
                        V.wait_ge(s_g.h, mask_q[t])
                        ti = idx * NT_OWN + t
                        P = 4 + ti % 2
                        D = 6 + ti % 2
                        nk = 4 * (2 * t + 2)
                        sc_done = {}
                        use = {}
                        pm = {}

                        def emit_score(c):
                            sb = c % 4
                            bf_ = st3["bankfree"].get(sb)
                            if bf_ is not None:
                                T.wait_ge(bf_[0].h, bf_[1])
                            T.wait_ge(s_sy.h, max(k_ld[kvh], q_ld[idx]))
                            inst = T.matmul(
                                pb[sb][:, :],
                                k_sb[kvh % 2][:, c * 128:(c + 1) * 128],
                                q_sb[idx % 2][:, t * TT:(t + 1) * TT],
                                start=True, stop=True)
                            sc_done[c] = s_pe.inc(inst)

                        def emit_exp(c):
                            sb = c % 4
                            pbuf = c % 8
                            A.wait_ge(s_pe.h, sc_done[c])
                            pf = st3["probfree"].get(pbuf)
                            if pf is not None:
                                A.wait_ge(pf[0].h, pf[1])
                            inst = A.activation(probs[pbuf][:, :],
                                                pb[sb][:, :],
                                                AF.Exp, bias=gbias[:, 0:1],
                                                scale=INV_SQRT_HD)
                            e_done = s_ac.inc(inst)
                            st3["bankfree"][sb] = (s_ac, e_done)
                            d_ = c - (nk - 8)
                            if d_ >= 0:
                                mcol = (t * 8 + d_) * TT
                                V.wait_ge(s_ac.h, e_done)
                                if pf is not None:
                                    V.wait_ge(pf[0].h, pf[1])
                                inst = V.tensor_tensor(
                                    probm[pbuf][:, :], probs[pbuf][:, :],
                                    mask_bf[:, mcol:mcol + TT], ALU.mult)
                                use[c] = (s_ve, s_ve.inc(inst))
                                pm[c] = probm[pbuf]
                            else:
                                use[c] = (s_ac, e_done)
                                pm[c] = probs[pbuf]

                        def emit_pv(c):
                            pbuf = c % 8
                            T.wait_ge(use[c][0].h, use[c][1])
                            if c == 0:
                                pvf = st3["pvfree"].get(P)
                                if pvf is not None:
                                    T.wait_ge(pvf[0].h, pvf[1])
                                dnf = st3["denfree"].get(D)
                                if dnf is not None:
                                    T.wait_ge(dnf[0].h, dnf[1])
                                T.wait_ge(s_sy.h, v_ld[kvh])
                            T.matmul(pb[P][:, :],
                                     v_sb[kvh % 2][:, c * 128:(c + 1) * 128],
                                     pm[c][:, :], start=(c == 0),
                                     stop=(c == nk - 1),
                                     skip_group_check=True)
                            inst = T.matmul(pb[D][0:1, :], ones_col_bf[:, :],
                                            pm[c][:, :], start=(c == 0),
                                            stop=(c == nk - 1),
                                            skip_group_check=True)
                            pvd = s_pe.inc(inst)
                            st3["probfree"][pbuf] = (s_pe, pvd)
                            return pvd

                        pv_last = None
                        LAG = 6
                        for c in range(nk):
                            emit_score(c)
                            if c == 3 and st3["pending_v"] is not None:
                                st3["rdb"] = st3["pending_v"]()
                                st3["pending_v"] = None
                            if c == 7 and st3["pending"] is not None:
                                st3["pending"](st3["rdb"])
                                st3["pending"] = None
                            if c >= 1:
                                emit_exp(c - 1)
                            if c >= LAG:
                                pv_last = emit_pv(c - LAG)
                        emit_exp(nk - 1)
                        for cc in range(nk - LAG, nk):
                            pv_last = emit_pv(cc)
                        if t == lim[2] - 1:
                            st3["pending_v"] = mk_epi1(ti, D, pv_last)
                        else:
                            st3["rdb"] = mk_epi1(ti, D, pv_last)()
                            st3["pending_v"] = None
                        st3["pending"] = mk_epi2(ti, P, D, qh, t)
                    q_end[idx] = s_pe.n
                    if qi == lim[1] - 1:
                        pe_end[kvh] = s_pe.n
                if st3["pending_v"] is not None:
                    st3["rdb"] = st3["pending_v"]()
                    st3["pending_v"] = None
                if st3["pending"] is not None:
                    st3["pending"](st3["rdb"])
                    st3["pending"] = None

            with nc.body("p3main", valid_engines=ALLE):
                attn_body()
                barrier()

        if _STOP_AFTER == "attn":
            _finalize(nc, ALLE, SY, s_st)
            return nc

        with nc.body("p4", valid_engines=ALLE):
            with contextlib.ExitStack() as c4:
                E4 = c4.enter_context
                wbuf = wsl
                atile = [E4(nc.sbuf_tensor(f"op_a{i}", [128, NKC * TT],
                                           BF16)) for i in range(2)]
                xres = [E4(nc.sbuf_tensor(f"op_x{i}", [128, 4 * TT], F32))
                        for i in range(2)]
                t1 = E4(nc.sbuf_tensor("op_t1", [128, TT], F32))
                x2o = [E4(nc.sbuf_tensor(f"op_o{i}", [128, TT], F32))
                       for i in range(2)]
                x2obf = [E4(nc.sbuf_tensor(f"op_ob{i}", [128, TT], BF16))
                         for i in range(2)]
                st4 = {"wfree": {}, "afree": {}, "xfree": {}, "bankfree": {},
                       "ofree": {}}
                for si in range(D // SLAB):
                    wi = si % 2
                    if si == 0 and "p4w0" in pref:
                        w_ld = pref["p4w0"]
                    else:
                        wf = st4["wfree"].get(wi)
                        if wf is not None:
                            G.wait_ge(wf[0].h, wf[1])
                        w_ld = dmai(G, wbuf[wi][:, :],
                                    woT[si * 128:(si + 1) * 128, :])
                    for t in range(NT_OWN):
                        it = si * NT_OWN + t
                        ai = it % 2
                        af = st4["afree"].get(ai)
                        if af is not None:
                            SY.wait_ge(af[0].h, af[1])
                        a_ld = dmai(SY, atile[ai][:, :],
                                    aos[t * 128:(t + 1) * 128, :])
                        xf = st4["xfree"].get(ai)
                        if xf is not None:
                            SY.wait_ge(xf[0].h, xf[1])
                        x_ld = dmai(SY, xres[ai][:, :]
                                    .rearrange("p (c t) -> p c t", c=4),
                                    xTo[si * SLAB:(si + 1) * SLAB,
                                        t * TT:(t + 1) * TT]
                                    .rearrange("(c p) t -> p c t", p=128))
                        for m in range(4):
                            bank = (it * 4 + m) % 4
                            bf_ = st4["bankfree"].get(bank)
                            T.wait_ge(s_g.h, w_ld)
                            T.wait_ge(s_sy.h, a_ld)
                            if bf_ is not None:
                                T.wait_ge(bf_[0].h, bf_[1])
                            for k in range(NKC):
                                inst = T.matmul(
                                    pb[bank][:, :],
                                    wbuf[wi][:, k * SLAB + m * 128:
                                             k * SLAB + (m + 1) * 128],
                                    atile[ai][:, k * TT:(k + 1) * TT],
                                    start=(k == 0), stop=(k == NKC - 1))
                            mm_done = s_pe.inc(inst)
                            V.wait_ge(s_pe.h, mm_done)
                            V.wait_ge(s_sy.h, x_ld)
                            oc = si * 4 + m
                            inst = V.tensor_scalar(t1[:, :], pb[bank][:, :],
                                                   bo_sb[:, oc:oc + 1], None,
                                                   ALU.add)
                            fence(V, inst)
                            st4["bankfree"][bank] = (s_ve, s_ve.n)
                            oi = (it * 4 + m) % 2
                            of = st4["ofree"].get(oi)
                            if of is not None:
                                V.wait_ge(of[0].h, of[1])
                            inst = V.tensor_tensor(
                                x2o[oi][:, :], t1[:, :],
                                xres[ai][:, m * TT:(m + 1) * TT], ALU.add)
                            x2_done = s_ve.inc(inst)
                            V.wait_ge(s_ve.h, x2_done)
                            inst = V.tensor_copy(x2obf[oi][:, :],
                                                 x2o[oi][:, :])
                            xbf_done = s_ve.inc(inst)
                            A.wait_ge(s_ve.h, xbf_done)
                            r0 = si * SLAB + m * 128
                            dmai(A, x2s[r0:r0 + 128, t * TT:(t + 1) * TT],
                                 x2o[oi][:, :])
                            stv = dmai(A, x2bf[t * 128:(t + 1) * 128,
                                               (r0 // 128) * TT:
                                               (r0 // 128 + 1) * TT],
                                       x2obf[oi][:, :])
                            st4["ofree"][oi] = (s_a, stv)
                        st4["afree"][ai] = (s_pe, s_pe.n)
                        st4["xfree"][ai] = (s_ve, s_ve.n)
                    st4["wfree"][wi] = (s_pe, s_pe.n)
                barrier()

        if _STOP_AFTER == "oproj":
            _finalize(nc, ALLE, SY, s_st)
            return nc

        with contextlib.ExitStack() as c56:
            E56 = c56.enter_context
            rhe = [E56(nc.sbuf_tensor(f"f_rhe{i}", [128, 8 * TT], BF16))
                   for i in range(NT_OWN)]
            bfb = [E56(nc.sbuf_tensor(f"bfb{i}", [128, 8 * SLAB6], BF16))
                   for i in range(2)]
            negm2_bf = E56(nc.sbuf_tensor("f_negm2", [1, S // 2], BF16))
            rstd2_bc = E56(nc.sbuf_tensor("f_rstd2bc", [128, S // 2], F32))

            with nc.body("p5", valid_engines=ALLE):
                with contextlib.ExitStack() as c5:
                    E5 = c5.enter_context
                    xtile = [E5(nc.sbuf_tensor(f"f5_x{i}", [128, NKC * TT],
                                               BF16)) for i in range(2)]
                    QK5 = NKC // 8
                    NG5 = NKC // QK5
                    xsq5 = [E5(nc.sbuf_tensor("f5_xsq0", [128, QK5 * TT],
                                              BF16))]
                    rowt5 = E5(nc.sbuf_tensor("f5_rowt", [1, 2 * TT], F32))
                    rstd2_col = E5(nc.sbuf_tensor("f5_rcol", [128, 16], F32))
                    art_sb = E5(nc.sbuf_tensor("f5_art", [128, NKC * 80],
                                               BF16))
                    csar_bf = E5(nc.sbuf_tensor("f5_csar", [1, 80], BF16))
                    rh2 = [E5(nc.sbuf_tensor(f"f5_rh2{i}", [128, TT], F32))
                           for i in range(2)]
                    lgs = E5(nc.sbuf_tensor("f5_lgs", [16, TT], F32))
                    lt_tm = E5(nc.sbuf_tensor("f5_lt", [128, 64], F32))
                    ptm = E5(nc.sbuf_tensor("f5_ptm", [128, 64], F32))
                    rcol = E5(nc.sbuf_tensor("f5_rc", [128, 12], F32))
                    rtmp = E5(nc.sbuf_tensor("f5_rt", [128, 64], F32))
                    comb_tm = E5(nc.sbuf_tensor("f5_comb", [128, 64], F32))
                    comb_sb = E5(nc.sbuf_tensor("f5_csb", [16, TT], BF16))

                    art_ld = dmai(G, art_sb[:, :], art_d[:, :])
                    pref["p6w0"] = dmai(G, wsl[0][:, 0:NKC * SLAB6],
                                        bwT[0:128, :])
                    pref["p6b0"] = dmai(G, bfb[0][:, :], bfm_d[0:128, :])
                    T.wait_ge(s_g.h, art_ld)
                    for k in range(NKC):
                        inst = T.matmul(pb[4][0:1, 0:80], ones_col_bf[:, :],
                                        art_sb[:, k * 80:(k + 1) * 80],
                                        start=(k == 0), stop=(k == NKC - 1))
                    csd = s_pe.inc(inst)
                    A.wait_ge(s_pe.h, csd)
                    inst = A.copy(csar_bf[0:1, :], pb[4][0:1, 0:80])
                    csar_done = s_ac.inc(inst)

                    prev5 = {"rows_read": (s_ac, csar_done), "xfree": {},
                             "sq_rd": {}, "bc_read": None}
                    for t in range(NT_OWN):
                        bi = t % 2
                        xf = prev5["xfree"].get(bi)
                        if xf is not None:
                            SY.wait_ge(xf[0].h, xf[1])
                        ld = dmai(SY, xtile[bi][:, :],
                                  x2bf[t * 128:(t + 1) * 128, :])
                        V.wait_ge(s_sy.h, ld)
                        sqv5 = []
                        for q in range(NG5):
                            rd = prev5["sq_rd"].get(0)
                            if rd is not None:
                                V.wait_ge(rd[0].h, rd[1])
                            inst = V.tensor_tensor(
                                xsq5[0][:, :],
                                xtile[bi][:, q * QK5 * TT:(q + 1) * QK5 * TT],
                                xtile[bi][:, q * QK5 * TT:(q + 1) * QK5 * TT],
                                ALU.mult)
                            sqv5.append(s_ve.inc(inst))
                        T.wait_ge(s_sy.h, ld)
                        T.wait_ge(prev5["rows_read"][0].h,
                                  prev5["rows_read"][1])
                        for k in range(NKC):
                            inst = T.matmul(pb[4][0:1, :], ones_col_bf[:, :],
                                            xtile[bi][:, k * TT:(k + 1) * TT],
                                            start=(k == 0),
                                            stop=(k == NKC - 1))
                        for q in range(NG5):
                            T.wait_ge(s_ve.h, sqv5[q])
                            for k in range(QK5):
                                inst = T.matmul(
                                    pb[5][0:1, :], ones_col_bf[:, :],
                                    xsq5[0][:, k * TT:(k + 1) * TT],
                                    start=(q == 0 and k == 0),
                                    stop=(q == NG5 - 1 and k == QK5 - 1))
                            prev5["sq_rd"][0] = (s_pe, s_pe.inc(inst))
                        sums_done = s_pe.n
                        prev5["xfree"][bi] = (s_pe, sums_done)
                        V.wait_ge(s_pe.h, sums_done)
                        inst = V.tensor_scalar(
                            negm2_bf[0:1, t * TT:(t + 1) * TT],
                            pb[4][0:1, :], -1.0 / D, None, ALU.mult)
                        fence(V, inst)
                        inst = V.tensor_scalar(rowt5[0:1, 0:TT],
                                               pb[5][0:1, :], 1.0 / D, None,
                                               ALU.mult)
                        fence(V, inst)
                        prev5["rows_read"] = (s_ve, s_ve.n)
                        inst = V.tensor_tensor(
                            rowt5[0:1, TT:2 * TT],
                            negm2_bf[0:1, t * TT:(t + 1) * TT],
                            negm2_bf[0:1, t * TT:(t + 1) * TT], ALU.mult)
                        fence(V, inst)
                        inst = V.tensor_tensor(rowt5[0:1, 0:TT],
                                               rowt5[0:1, 0:TT],
                                               rowt5[0:1, TT:2 * TT],
                                               ALU.subtract)
                        var_done = s_ve.inc(inst)
                        A.wait_ge(s_ve.h, var_done)
                        inst = A.activation(rowt5[0:1, TT:2 * TT],
                                            rowt5[0:1, 0:TT], AF.Ln,
                                            bias=eps_sb[0:1, 0:1], scale=1.0)
                        fence(A, inst)
                        inst = A.activation(rowt5[0:1, 0:TT],
                                            rowt5[0:1, TT:2 * TT], AF.Exp,
                                            scale=-0.5)
                        rstd_done = s_ac.inc(inst)
                        T.wait_ge(s_ac.h, rstd_done)
                        if prev5["bc_read"] is not None:
                            T.wait_ge(prev5["bc_read"][0].h,
                                      prev5["bc_read"][1])
                        inst = T.matmul(pb[6][:, :], ones_row_f[0:1, :],
                                        rowt5[0:1, 0:TT], start=True,
                                        stop=True)
                        bc_done = s_pe.inc(inst)
                        A.wait_ge(s_pe.h, bc_done)
                        inst = A.copy(rstd2_bc[:, t * TT:(t + 1) * TT],
                                      pb[6][:, :])
                        bc_ev = s_ac.inc(inst)
                        prev5["bc_read"] = (s_ac, bc_ev)
                        A.wait_ge(s_ac.h, bc_ev)
                        for c in range(4):
                            cc = 4 * t + c
                            dmai(A, rstd2_col[:, cc:cc + 1],
                                 rowt5[0:1, c * 128:(c + 1) * 128])
                        V.wait_ge(s_a.h, s_a.n)
                        V.wait_ge(s_ac.h, bc_ev)

                    for t in range(NT_OWN):
                        bi = t % 2
                        xf = prev5["xfree"].get(bi)
                        if xf is not None:
                            SY.wait_ge(xf[0].h, xf[1])
                        ld = dmai(SY, xtile[bi][:, :],
                                  x2bf[t * 128:(t + 1) * 128, :])
                        ri = t % 2
                        T.wait_ge(s_sy.h, ld)
                        T.wait_ge(s_ve.h, s_ve.n)
                        for k in range(NKC):
                            T.matmul(pb[4][0:64, :],
                                     art_sb[:, k * 80:k * 80 + 64],
                                     xtile[bi][:, k * TT:(k + 1) * TT],
                                     start=(k == 0), stop=False)
                        inst = T.matmul(pb[4][0:64, :], csar_bf[0:1, 0:64],
                                        negm2_bf[0:1, t * TT:(t + 1) * TT],
                                        start=False, stop=True)
                        rh_done = s_pe.inc(inst)
                        for k in range(NKC):
                            T.matmul(pb[5][0:16, :],
                                     art_sb[:, k * 80 + 64:k * 80 + 80],
                                     xtile[bi][:, k * TT:(k + 1) * TT],
                                     start=(k == 0), stop=False)
                        inst = T.matmul(pb[5][0:16, :], csar_bf[0:1, 64:80],
                                        negm2_bf[0:1, t * TT:(t + 1) * TT],
                                        start=False, stop=True)
                        lg_done = s_pe.inc(inst)
                        prev5["xfree"][bi] = (s_pe, lg_done)
                        V.wait_ge(s_pe.h, rh_done)
                        inst = V.tensor_tensor(
                            rh2[ri][0:64, :], pb[4][0:64, :],
                            rstd2_bc[0:64, t * TT:(t + 1) * TT], ALU.mult)
                        rh_ev = s_ve.inc(inst)
                        A.wait_ge(s_ve.h, rh_ev)
                        rh_c2 = dmai(A, rh2[ri][64:128, :], rh2[ri][0:64, :])
                        V.wait_ge(s_pe.h, lg_done)
                        inst = V.tensor_tensor(
                            lgs[0:16, :], pb[5][0:16, :],
                            rstd2_bc[0:16, t * TT:(t + 1) * TT], ALU.mult)
                        lg_ev = s_ve.inc(inst)
                        T.wait_ge(s_ve.h, lg_ev)
                        for j in range(4):
                            inst = T.transpose(pb[6][:, j * 16:(j + 1) * 16],
                                               lgs[0:16,
                                                   j * 128:(j + 1) * 128],
                                               eye_sb[0:16, 0:16])
                        ltr_done = s_pe.inc(inst)
                        A.wait_ge(s_pe.h, ltr_done)
                        inst = A.copy(lt_tm[:, :], pb[6][:, 0:64])
                        lt_ev = s_ac.inc(inst)
                        T.wait_ge(s_ac.h, lt_ev)
                        V.wait_ge(s_ac.h, lt_ev)
                        for j in range(4):
                            lsl = lt_tm[:, j * 16:(j + 1) * 16]
                            psl = ptm[:, j * 16:(j + 1) * 16]
                            csl = comb_tm[:, j * 16:(j + 1) * 16]
                            inst = V.reduce_max(rcol[:, 0:1], lsl, axis=AX)
                            fence(V, inst)
                            inst = V.tensor_scalar(rcol[:, 1:2], rcol[:, 0:1],
                                                   -1.0, None, ALU.mult)
                            fence(V, inst)
                            A.wait_ge(s_ve.h, s_ve.n)
                            inst = A.activation(psl, lsl, AF.Exp,
                                                bias=rcol[:, 1:2], scale=1.0)
                            fence(A, inst)
                            V.wait_ge(s_ac.h, s_ac.n)
                            inst = V.reduce_sum(rcol[:, 2:3], psl, axis=AX)
                            fence(V, inst)
                            inst = V.reduce_max(rcol[:, 3:4], psl, axis=AX)
                            fence(V, inst)
                            inst = V.tensor_scalar(rtmp[:, 0:16], psl,
                                                   rcol[:, 3:4], None,
                                                   ALU.is_ge)
                            fence(V, inst)
                            inst = V.scalar_tensor_tensor(rtmp[:, 16:32],
                                                          rtmp[:, 0:16], 1e30,
                                                          psl, ALU.mult,
                                                          ALU.subtract)
                            fence(V, inst)
                            inst = V.tensor_scalar(rtmp[:, 32:48],
                                                   rtmp[:, 16:32], -1.0,
                                                   None, ALU.mult)
                            fence(V, inst)
                            inst = V.reduce_max(rcol[:, 4:5], rtmp[:, 32:48],
                                                axis=AX)
                            fence(V, inst)
                            inst = V.tensor_scalar(rcol[:, 5:6], rcol[:, 2:3],
                                                   1e-6, None, ALU.mult)
                            fence(V, inst)
                            inst = V.tensor_tensor(rcol[:, 6:7], rcol[:, 3:4],
                                                   rcol[:, 4:5], ALU.add)
                            fence(V, inst)
                            inst = V.tensor_tensor(rcol[:, 7:8], rcol[:, 5:6],
                                                   rcol[:, 6:7], ALU.add)
                            fence(V, inst)
                            inst = V.reciprocal(rcol[:, 8:9], rcol[:, 7:8])
                            fence(V, inst)
                            inst = V.reciprocal(rcol[:, 9:10],
                                                rstd2_col[:, 4 * t + j:
                                                          4 * t + j + 1])
                            fence(V, inst)
                            inst = V.tensor_scalar(rtmp[:, 0:16], psl,
                                                   rcol[:, 4:5], None,
                                                   ALU.is_ge)
                            fence(V, inst)
                            inst = V.tensor_scalar(rtmp[:, 16:32], psl,
                                                   rcol[:, 8:9], 0.5,
                                                   ALU.mult, ALU.mult)
                            fence(V, inst)
                            inst = V.tensor_scalar(rtmp[:, 48:64],
                                                   rtmp[:, 16:32],
                                                   rcol[:, 9:10], None,
                                                   ALU.mult)
                            fence(V, inst)
                            inst = V.tensor_tensor(csl, rtmp[:, 0:16],
                                                   rtmp[:, 48:64], ALU.mult)
                            fence(V, inst)
                        comb_done = s_ve.n
                        T.wait_ge(s_ve.h, comb_done)
                        T.wait_ge(s_ac.h, s_ac.n)
                        for j in range(4):
                            inst = T.transpose(pb[6][0:16,
                                                     j * 128:(j + 1) * 128],
                                               comb_tm[:, j * 16:(j + 1) * 16],
                                               eye_sb[:, :])
                        ctr_done = s_pe.inc(inst)
                        A.wait_ge(s_pe.h, ctr_done)
                        inst = A.copy(comb_sb[0:16, :], pb[6][0:16, 0:TT])
                        cb_ev = s_ac.inc(inst)
                        for c in range(8):
                            T.wait_ge(s_ac.h, cb_ev)
                            T.wait_ge(s_ve.h, s_ve.n)
                            inst = T.matmul(pb[7][:, :],
                                            sel16_sb[0:16,
                                                     c * 128:(c + 1) * 128],
                                            comb_sb[0:16, :],
                                            start=True, stop=True)
                            bcp_done = s_pe.inc(inst)
                            V.wait_ge(s_pe.h, bcp_done)
                            V.wait_ge(s_a.h, rh_c2)
                            inst = V.tensor_tensor(
                                rhe[t][:, c * TT:(c + 1) * TT], pb[7][:, :],
                                rh2[ri][:, :], ALU.mult)
                            s_ve.inc(inst)
                    barrier()

            if _STOP_AFTER == "router":
                _finalize(nc, ALLE, SY, s_st)
                return nc

            with nc.body("p6", valid_engines=ALLE):
                with contextlib.ExitStack() as c6:
                    E6 = c6.enter_context
                    NCH6 = SLAB6 // 128
                    wbuf = wsl
                    bfbuf = bfb
                    xtile6 = [E6(nc.sbuf_tensor(f"f6_x{i}", [128, NKC * TT],
                                                BF16)) for i in range(2)]
                    x2res = [E6(nc.sbuf_tensor(f"f6_r{i}",
                                               [128, NCH6 * TT], F32))
                             for i in range(2)]
                    t16 = E6(nc.sbuf_tensor("f6_t1", [128, TT], F32))
                    ob6 = [E6(nc.sbuf_tensor(f"f6_o{i}", [128, TT], F32))
                           for i in range(2)]
                    cs6 = E6(nc.sbuf_tensor("f6_cs", [1, SLAB6], BF16))
                    st6 = {"wfree": {}, "xfree": {}, "rfree": {},
                           "bankfree": {}, "ofree": {}, "cs_read": None}
                    for si in range(D // SLAB6):
                        wi = si % 2
                        if si == 0:
                            w_ld = (s_g, pref["p6w0"])
                            b_ld = (s_g, pref["p6b0"])
                        else:
                            wf = st6["wfree"].get(wi)
                            if wf is not None:
                                G.wait_ge(wf[0].h, wf[1])
                            w_ld = (s_g,
                                    dmai(G, wbuf[wi][:, 0:NKC * SLAB6],
                                         bwT[si * 128:(si + 1) * 128, :]))
                            b_ld = (s_g,
                                    dmai(G, bfbuf[wi][:, :],
                                         bfm_d[si * 128:(si + 1) * 128, :]))
                        T.wait_ge(w_ld[0].h, w_ld[1])
                        if st6["cs_read"] is not None:
                            T.wait_ge(st6["cs_read"][0].h,
                                      st6["cs_read"][1])
                        for k in range(NKC):
                            inst = T.matmul(
                                pb[4][0:1, 0:SLAB6], ones_col_bf[:, :],
                                wbuf[wi][:, k * SLAB6:(k + 1) * SLAB6],
                                start=(k == 0), stop=(k == NKC - 1))
                        csd = s_pe.inc(inst)
                        A.wait_ge(s_pe.h, csd)
                        inst = A.copy(cs6[0:1, :], pb[4][0:1, 0:SLAB6])
                        cs_done = s_ac.inc(inst)
                        st6["cs_read"] = (s_ac, cs_done)
                        for t in range(NT_OWN):
                            it = si * NT_OWN + t
                            xi = it % 2
                            xf = st6["xfree"].get(xi)
                            if xf is not None:
                                SY.wait_ge(xf[0].h, xf[1])
                            x_ld = dmai(SY, xtile6[xi][:, :],
                                        x2bf[t * 128:(t + 1) * 128, :])
                            rf = st6["rfree"].get(xi)
                            if rf is not None:
                                SY.wait_ge(rf[0].h, rf[1])
                            r_ld = dmai(SY, x2res[xi][:, :]
                                        .rearrange("p (c t) -> p c t",
                                                   c=NCH6),
                                        x2s[si * SLAB6:(si + 1) * SLAB6,
                                            t * TT:(t + 1) * TT]
                                        .rearrange("(c p) t -> p c t", p=128))
                            for m in range(NCH6):
                                bank = (it * NCH6 + m) % 4
                                bf_ = st6["bankfree"].get(bank)
                                T.wait_ge(b_ld[0].h, b_ld[1])
                                T.wait_ge(s_sy.h, x_ld)
                                T.wait_ge(s_ac.h, cs_done)
                                if bf_ is not None:
                                    T.wait_ge(bf_[0].h, bf_[1])
                                for k in range(NKC):
                                    T.matmul(
                                        pb[bank][:, :],
                                        wbuf[wi][:, k * SLAB6 + m * 128:
                                                 k * SLAB6 + (m + 1) * 128],
                                        xtile6[xi][:, k * TT:(k + 1) * TT],
                                        start=(k == 0), stop=False)
                                for c in range(8):
                                    T.matmul(
                                        pb[bank][:, :],
                                        bfbuf[wi][:, c * SLAB6 + m * 128:
                                                  c * SLAB6 + (m + 1) * 128],
                                        rhe[t][:, c * TT:(c + 1) * TT],
                                        start=False, stop=False)
                                inst = T.matmul(
                                    pb[bank][:, :],
                                    cs6[0:1, m * 128:(m + 1) * 128],
                                    negm2_bf[0:1, t * TT:(t + 1) * TT],
                                    start=False, stop=True)
                                mm_done = s_pe.inc(inst)
                                V.wait_ge(s_pe.h, mm_done)
                                V.wait_ge(s_sy.h, r_ld)
                                inst = V.tensor_tensor(
                                    t16[:, :], pb[bank][:, :],
                                    rstd2_bc[:, t * TT:(t + 1) * TT],
                                    ALU.mult)
                                fence(V, inst)
                                st6["bankfree"][bank] = (s_ve, s_ve.n)
                                oi = (it * NCH6 + m) % 2
                                of = st6["ofree"].get(oi)
                                if of is not None:
                                    V.wait_ge(of[0].h, of[1])
                                oc = (si * SLAB6) // 128 + m
                                inst = V.scalar_tensor_tensor(
                                    ob6[oi][:, :], t16[:, :],
                                    bb_sb[:, oc:oc + 1],
                                    x2res[xi][:, m * TT:(m + 1) * TT],
                                    ALU.add, ALU.add)
                                o_done = s_ve.inc(inst)
                                A.wait_ge(s_ve.h, o_done)
                                r0 = si * SLAB6 + m * 128
                                inst = A.dma_start(
                                    out_d[r0:r0 + 128, t * TT:(t + 1) * TT],
                                    ob6[oi][:, :])
                                s_st.inc(inst, 16)
                                st6["ofree"][oi] = (s_st, s_st.n)
                            st6["xfree"][xi] = (s_pe, s_pe.n)
                            st6["rfree"][xi] = (s_ve, s_ve.n)
                        st6["wfree"][wi] = (s_pe, s_pe.n)

            _finalize(nc, ALLE, SY, s_st)
    return nc


def _finalize(nc, ALLE, SY, s_st):
    with nc.body("fin", valid_engines=ALLE):
        if s_st.n > 0:
            SY.wait_ge(s_st.h, s_st.n)
        else:
            SY.nop()



def _deinterleave_perm():
    per_head = np.concatenate([np.arange(0, HD, 2), np.arange(1, HD, 2)])
    perm = np.arange(QCOLS + KCOLS + VCOLS)
    for h in range(NH):
        perm[h * HD:(h + 1) * HD] = h * HD + per_head
    for h in range(NKV):
        base = QCOLS + h * HD
        perm[base:base + HD] = base + per_head
    return perm


def _rope_tables(cols):
    inv = 10000.0 ** (-(np.arange(64, dtype=np.float64)) / 64.0)
    ang = cols[None, :].astype(np.float64) * inv[:, None]
    c = np.cos(ang)
    s = np.sin(ang)
    cr = np.ascontiguousarray(np.concatenate([c, c], axis=0)
                              .astype(np.float32))
    sg = np.ascontiguousarray(np.concatenate([-s, s], axis=0)
                              .astype(np.float32))
    return cr, sg


def _own_cols(parity):
    return np.concatenate([np.arange(g * TT, (g + 1) * TT)
                           for g in G_LISTS[parity]])


def _fm_chunks(v, n):
    return np.ascontiguousarray(v.reshape(n, 128).T)


def _prepare_in_maps(L0_attn_x, L0_attn_w_qkv, L0_attn_b_qkv, L0_attn_w_o,
                     L0_attn_b_o, L0_ffn_base_w, L0_ffn_base_b, L0_ffn_A,
                     L0_ffn_B, L0_ffn_router_w):
    x = np.ascontiguousarray(np.asarray(L0_attn_x, dtype=np.float32))
    w_qkv = np.asarray(L0_attn_w_qkv, dtype=np.float32)
    b_qkv = np.asarray(L0_attn_b_qkv, dtype=np.float32)
    w_o = np.asarray(L0_attn_w_o, dtype=np.float32)
    b_o = np.asarray(L0_attn_b_o, dtype=np.float32)
    base_w = np.asarray(L0_ffn_base_w, dtype=np.float32)
    base_b = np.asarray(L0_ffn_base_b, dtype=np.float32)
    A_ = np.asarray(L0_ffn_A, dtype=np.float32)
    B_ = np.asarray(L0_ffn_B, dtype=np.float32)
    router_w = np.asarray(L0_ffn_router_w, dtype=np.float32)

    perm = _deinterleave_perm()
    wq_p = w_qkv[perm]
    b_p = b_qkv[perm]
    wqkvT = np.ascontiguousarray(
        wq_p.T.reshape(NKC, 128, 12, SLAB).transpose(2, 1, 0, 3)
        .reshape(12 * 128, NKC * SLAB))
    bqk = _fm_chunks(b_p[:QCOLS + KCOLS], 40)
    bqks = np.ascontiguousarray(
        np.concatenate([bqk[64:128], bqk[0:64]], axis=0))
    bvrow = np.ascontiguousarray(b_p[QCOLS + KCOLS:][None, :])
    woT = np.ascontiguousarray(
        w_o.T.reshape(NKC, 128, 8, SLAB).transpose(2, 1, 0, 3)
        .reshape(8 * 128, NKC * SLAB))
    bo = _fm_chunks(b_o, 32)
    bwT = np.ascontiguousarray(
        base_w.T.reshape(NKC, 128, 16, SLAB6).transpose(2, 1, 0, 3)
        .reshape(16 * 128, NKC * SLAB6))
    bb = _fm_chunks(base_b, 32)
    art = np.ascontiguousarray(
        np.concatenate([A_.T, router_w.T], axis=1)
        .reshape(NKC, 128, 80).transpose(1, 0, 2).reshape(128, NKC * 80))
    bfm = np.ascontiguousarray(
        B_.transpose(0, 2, 1).reshape(16 * 64, D)
        .reshape(8, 128, 16, SLAB6).transpose(2, 1, 0, 3)
        .reshape(16 * 128, 8 * SLAB6))
    sel16 = np.zeros((16, 8 * 128), dtype=np.float32)
    for c in range(8):
        sel16[2 * c, c * 128:c * 128 + 64] = 1.0
        sel16[2 * c + 1, c * 128 + 64:(c + 1) * 128] = 1.0

    pp = np.arange(128)[:, None]
    ff = np.arange(TT)[None, :]
    msks = []
    for p in range(2):
        m = np.zeros((128, 32 * TT), dtype=np.float32)
        for t in range(NT_OWN):
            g = G_LISTS[p][t]
            nk = 4 * (2 * t + 2)
            for j in range(8):
                c = nk - 8 + j
                m[:, (t * 8 + j) * TT:(t * 8 + j + 1) * TT] = (
                    (512 * g + ff) >= (128 * c + pp)).astype(np.float32)
        msks.append(m)

    in_maps = []
    for core in range(8):
        b = core // 2
        p = core % 2
        cols = _own_cols(p)
        xTa = np.ascontiguousarray(x[b].T)
        xToa = np.ascontiguousarray(xTa[:, cols])
        cro, sgo = _rope_tables(cols)
        in_maps.append({
            "xTo": xToa, "wqkvT": wqkvT, "bqk": bqk, "bqks": bqks,
            "bvrow": bvrow, "woT": woT, "bo": bo, "bwT": bwT, "bb": bb,
            "art": art, "bfm": bfm, "cro": cro, "sgo": sgo,
            "msk": msks[p], "sel16": sel16,
        })

    return in_maps


def kernel(**inputs):
    in_maps = _prepare_in_maps(**inputs)
    nc = build_program()
    res = run_bass_kernel_spmd(nc, in_maps, core_ids=list(range(8)))

    out = np.empty((B, S, D), dtype=np.float32)
    for core in range(8):
        b = core // 2
        p = core % 2
        cols = _own_cols(p)
        out[b, cols, :] = res.results[core]["out"].T
    return out


if __name__ == "__main__":
    import reference
    inputs = reference.setup_inputs()
    got = kernel(**{k: np.asarray(v) for k, v in inputs.items()})
    want = np.asarray(reference.reference(**inputs))
    err = np.sqrt(((got - want) ** 2).mean() / (want ** 2).mean())
    print("rel l2:", err)
